# version 04
# baseline (speedup 1.0000x reference)
import numpy as np
import ml_dtypes
from contextlib import ExitStack

import concourse.bass as bass
import concourse.mybir as mybir
from concourse.bass_utils import run_bass_kernel_spmd

F32 = mybir.dt.float32
BF16 = mybir.dt.bfloat16
AF = mybir.ActivationFunctionType
ALU = mybir.AluOpType
AX = mybir.AxisListType

NCORE = 8
DM = 1024
TOK = 2048
HALO = 3
TOKH = TOK + HALO
NCH = DM // 128
DI = 2048
NCI = DI // 128
DT = 64
TB = 512
NTB = TOK // TB
TBS = [(i * TB, TB) for i in range(NTB)]
TBS_H = TBS + [(1540, 511)]
BIG = 1.0e30
EPS_LN = 1e-5

L0_NE, L0_DH = 8, 2048
DBG_NEX = None
DBG_NOSCALE = False
DBG_ALPHA_C = None
DBG_STOP = None
L1_NE, L1_DH = 16, 1024

DO_L1 = True


def build_program(do_l0=True, do_l1=True, dbg=()):
    nc = bass.Bass("TRN2", target_bir_lowering=False, debug=False, num_devices=NCORE)
    dbg = set(dbg)
    es = ExitStack()

    sem = {}
    cnt = {}
    for nm in ["ld", "wl", "wla", "wlb", "wl1", "wlu", "wl2",
               "pe", "ve", "ac", "po", "cc", "st"]:
        sem[nm] = es.enter_context(nc.semaphore(nm))
        cnt[nm] = 0

    def inc(name, inst, k=1):
        inst.then_inc(sem[name], k)
        cnt[name] += k
        return cnt[name]

    def wait(eng, name, val=None):
        eng.wait_ge(sem[name], cnt[name] if val is None else val)

    def vsafe(v, ins):
        v.wait_ge(sem["ve"], inc("ve", ins))

    inp = {}

    def dram_in(name, shape, dtype=F32):
        inp[name] = nc.dram_tensor(name, list(shape), dtype, kind="ExternalInput")
        return inp[name]

    if do_l0:
        dram_in("x_in", (DM, TOKH))
    else:
        dram_in("x_mid_in", (DM, TOK))
    out_x = nc.dram_tensor("out_x", [DM, TOK], F32, kind="ExternalOutput")

    dram_in("identity", (128, 128))
    dram_in("ones_col_bf", (128, 1), BF16)
    dram_in("ones_row_bf", (1, 128), BF16)
    dram_in("zeros_col", (128, 1))
    dram_in("one_col", (128, 1))
    dram_in("onehot8", (128, 8))
    dram_in("iota8eps", (128, 8))
    dram_in("iota16eps", (128, 16))
    dram_in("iota4eps", (128, 4))
    dram_in("neg1p", (128, 16))
    dram_in("colpack_in", (128, 160))

    if do_l0:
        dram_in("l0_ip_w", (DM, 2 * DI))
        dram_in("l0_xp_w", (DI, DT))
        dram_in("l0_dt_w", (DT, DI))
        dram_in("l0_op_w", (DI, DM))
        dram_in("l0_gate_w", (DM, L0_NE))
        dram_in("l0_bbias_bc", (128, L0_NE))
        dram_in("l0_w1", (L0_NE, DM, L0_DH))
        dram_in("l0_w2", (L0_NE, L0_DH, DM))
        cc0_in = nc.dram_tensor("cc0_in", [128, 32], F32, kind="Internal")
        cc0_out = nc.dram_tensor("cc0_out", [128 * NCORE, 32], F32, kind="Internal",
                                 addr_space="Local" if globals()["DBG_STOP"]
                                 else "Shared")
    if do_l1:
        dram_in("l1_Wd", (DM, DM))
        dram_in("l1_Wb", (DM, 16))
        dram_in("l1_Wc", (DM, 16))
        dram_in("l1_Wo", (DM, DM))
        dram_in("l1_A", (DM, 16))
        dram_in("l1_router_w", (DM, L1_NE))
        dram_in("l1_rbias_bc", (128, L1_NE))
        dram_in("l1_gate_w", (L1_NE, DM, L1_DH))
        dram_in("l1_up_w", (L1_NE, DM, L1_DH))
        dram_in("l1_down_w", (L1_NE, L1_DH, DM))
        dram_in("l1_shg_w", (DM, 2 * L1_DH))
        dram_in("l1_shu_w", (DM, 2 * L1_DH))
        dram_in("l1_shd_w", (2 * L1_DH, DM))
        cc1_in = nc.dram_tensor("cc1_in", [16, 2], F32, kind="Internal")
        cc1_out = nc.dram_tensor("cc1_out", [16 * NCORE, 2], F32, kind="Internal",
                                 addr_space="Shared")
    combT_d = nc.dram_tensor("combT_d", [16, TOK], F32, kind="Internal")

    dbg_outs = {}

    def dbg_add(name, ap_, shape, dtype=F32):
        if name in dbg:
            od = nc.dram_tensor("dbg_" + name, list(shape), dtype,
                                kind="ExternalOutput")
            dbg_outs[name] = (ap_, od)

    def dbg_snap(name, ap_, shape, dtype=F32):
        if name not in dbg:
            return
        od = nc.dram_tensor("dbg_" + name, list(shape), dtype,
                            kind="ExternalOutput")

        def sp(sync):
            wait(sync, "ve")
            wait(sync, "ac")
            wait(sync, "pe")
            inc("st", sync.dma_start(out=od.ap(), in_=ap_), 16)

        emit("sync", sp)
        stv = cnt["st"]
        emit("vector", lambda v: v.wait_ge(sem["st"], stv))
        emit("scalar", lambda s: s.wait_ge(sem["st"], stv))
        emit("tensor", lambda t: t.wait_ge(sem["st"], stv))
        emit("gpsimd", lambda g: g.wait_ge(sem["st"], stv))

    def sbt(name, shape, dtype):
        return es.enter_context(nc.sbuf_tensor(name, shape, dtype))

    x_res = sbt("x_res", [128, NCH * TOKH], F32)
    xn = sbt("xn", [128, NCH * TOKH], BF16)
    AR = sbt("AR", [128, 40960], BF16)
    SCR = sbt("SCR", [128, 4352], F32)
    ident = sbt("identity_sb", [128, 128], F32)
    ones_col = sbt("ones_col_sb", [128, 1], BF16)
    ones_row = sbt("ones_row_sb", [1, 128], BF16)
    zeros_col = sbt("zeros_col_sb", [128, 1], F32)
    one_col = sbt("one_col_sb", [128, 1], F32)
    onehot8 = sbt("onehot8_sb", [128, 8], F32)
    iota8 = sbt("iota8_sb", [128, 8], F32)
    iota16 = sbt("iota16_sb", [128, 16], F32)
    iota4 = sbt("iota4_sb", [128, 4], F32)
    neg1p = sbt("neg1p_sb", [128, 16], F32)
    colpack = sbt("colpack", [128, 160], F32)
    packF = sbt("packF", [128, 16], F32)
    packP = sbt("packP", [128, 16], F32)
    packR = sbt("packR", [128, 16], F32)
    S0 = sbt("S0", [128, 16], F32)
    agg = sbt("agg", [128, 32 * NCORE], F32)
    bias_bc = sbt("bias_bc", [128, 16], F32)
    comb = sbt("comb", [128, 16 * 16], F32)
    combT = sbt("combT", [16, 128], F32)
    cbc = sbt("cbc", [128, TOK], BF16)
    crow_bf = sbt("crow_bf", [1, TB], BF16)
    aggL1 = sbt("aggL1", [16, 16], F32) if do_l1 else None
    cc1pack = sbt("cc1pack", [16, 2], F32) if do_l1 else None
    gw_sb = sbt("gw_sb", [128, NCH * 16], BF16)
    dtw_sb = sbt("dtw_sb", [64, DI], BF16) if do_l0 else None
    smallw = sbt("smallw", [128, 384], BF16) if do_l1 else None
    l1H = sbt("l1H", [16, 2], F32) if do_l1 else None

    psum = es.enter_context(nc.psum_tensor("psum", [128, 4096], F32))

    def bank(i, p0=0, np_=128, w=512):
        return psum[p0:p0 + np_, i * 512:i * 512 + w]

    U_OFF = 0
    DTL_OFF = 32768
    RING_OFF = 34816
    SZ_OFF = 38912
    CBT_OFF = 39936

    def Uc(j):
        return AR[:, U_OFF + j * TOK:U_OFF + (j + 1) * TOK]

    def ar(off, n):
        return AR[:, off:off + n]

    XP_OFF = 38912

    def xp_sb(j):
        return AR[:, XP_OFF + j * DT:XP_OFF + (j + 1) * DT]

    ROWS_OFF = 34816

    def row_b(i, n=TOKH, o=0):
        return AR[0:1, ROWS_OFF + i * TOKH + o:ROWS_OFF + i * TOKH + o + n]

    def s0(o=0, n=TOK):
        return SCR[:, o:o + n]

    def s1(o=0, n=TOK):
        return SCR[:, TOK + o:TOK + o + n]

    def sxz(o=0, n=TOKH):
        return SCR[:, o:o + n]

    def sconv(o=0, n=TOK):
        return SCR[:, TOKH + o:TOKH + o + n]

    def srt(o, n):
        return SCR[:, 4096 + o:4096 + o + n]

    def xs(c, o=0, n=TOKH):
        return x_res[:, c * TOKH + o:c * TOKH + o + n]

    def xsm(c, o=0, n=TOK):
        return x_res[:, c * TOKH + HALO + o:c * TOKH + HALO + o + n]

    def xnf(c, o=0, n=TOKH):
        return xn[:, c * TOKH + o:c * TOKH + o + n]

    def xnm(c, o=0, n=TOK):
        return xn[:, c * TOKH + HALO + o:c * TOKH + HALO + o + n]

    bh = {}

    _selfsem = {"vector": "ve", "scalar": "ac"}

    def emit(engine, fn):
        nm = _selfsem.get(engine)
        if nm is None:
            getattr(bh["block"], engine)(fn)
            return

        def wrapped(eng):
            eng.wait_ge(sem[nm], cnt[nm])
            fn(eng)

        getattr(bh["block"], engine)(wrapped)


    def ph_loads(sync):
        if do_l0:
            i0 = inp["x_in"].ap().rearrange("(c p) t -> p c t", p=128)
            d0 = x_res[:, :].rearrange("p (c t) -> p c t", c=NCH)
            inc("ld", sync.dma_start(out=d0, in_=i0), 16)
        else:
            for c in range(NCH):
                i0 = inp["x_mid_in"].ap()[c * 128:(c + 1) * 128, :]
                inc("ld", sync.dma_start(out=xsm(c), in_=i0), 16)
        for nm, dst in [("identity", ident), ("ones_col_bf", ones_col),
                        ("ones_row_bf", ones_row), ("zeros_col", zeros_col),
                        ("one_col", one_col), ("onehot8", onehot8),
                        ("iota8eps", iota8), ("iota16eps", iota16),
                        ("iota4eps", iota4), ("neg1p", neg1p),
                        ("colpack_in", colpack)]:
            inc("ld", sync.dma_start(out=dst[:, :], in_=inp[nm].ap()), 16)

    def ph_wloads_early(g):
        if do_l0:
            inc("wl", g.dma_start(
                out=gw_sb[:, 0:NCH * 8].rearrange("p (c e) -> p c e", c=NCH),
                in_=inp["l0_gate_w"].ap().rearrange("(c p) e -> p c e", p=128)), 16)

    def ph_colprep_ac(s):
        wait(s, "ld")
        inc("ac", s.activation(colpack[:, 0:16], colpack[:, 0:16], AF.Exp,
                               bias=0.0, scale=1.0))

    def ph_colprep_ve(v):
        wait(v, "ac")
        vsafe(v, v.tensor_scalar(colpack[:, 0:16], colpack[:, 0:16], -1.0, None,
                                 ALU.mult))
        inc("ve", v.tensor_scalar(colpack[:, 0:16], colpack[:, 0:16], -10.0, -1e-6,
                                  ALU.max, ALU.min))

    def emit_layernorm(full):
        tbs = TBS_H if full else TBS
        tokh = TOKH if full else TOK
        off = 0 if full else HALO
        SQ = 16408

        def xbf(c):
            return AR[:, c * tokh:(c + 1) * tokh]

        def sqbf(c):
            return AR[:, SQ + c * tokh:SQ + (c + 1) * tokh]

        def ve0(v):
            wait(v, "ld")
            wait(v, "ve")
            wait(v, "pe")
            for c in range(NCH):
                ins = v.tensor_copy(xbf(c), xs(c, off, tokh))
            inc("ve", ins)

        def ac0(s):
            wait(s, "ld")
            wait(s, "pe")
            wait(s, "ve")
            for c in range(NCH):
                ins = s.activation(sqbf(c), xs(c, off, tokh), AF.Square,
                                   bias=0.0, scale=1.0)
            inc("ac", ins)

        emit("vector", ve0)
        emit("scalar", ac0)
        ve_in, ac_in = cnt["ve"], cnt["ac"]

        pe_marks, ve_marks = [], []
        for bi, (t0, tw) in enumerate(tbs):
            bs, bq = bi % 2, 2 + bi % 2

            def pe(t, bi=bi, t0=t0, tw=tw, bs=bs, bq=bq):
                if bi == 0:
                    t.wait_ge(sem["ve"], ve_in)
                    t.wait_ge(sem["ac"], ac_in)
                if bi >= 2:
                    t.wait_ge(sem["ve"], ve_marks[bi - 2])
                for c in range(NCH):
                    t.matmul(bank(bs, 0, 1, tw), ones_col[:, :], xbf(c)[:, t0:t0 + tw],
                             start=(c == 0), stop=(c == NCH - 1))
                for c in range(NCH):
                    ins = t.matmul(bank(bq, 0, 1, tw), ones_col[:, :],
                                   sqbf(c)[:, t0:t0 + tw],
                                   start=(c == 0), stop=(c == NCH - 1))
                pe_marks.append(inc("pe", ins))

            def ve(v, bi=bi, t0=t0, tw=tw, bs=bs, bq=bq):
                v.wait_ge(sem["pe"], pe_marks[bi])
                m = row_b(0, tw, t0)
                q = row_b(1, tw, t0)
                t2 = SCR[0:1, 2048:2048 + tw]
                vsafe(v, v.tensor_scalar(m, bank(bs, 0, 1, tw), 1.0 / DM, None,
                                         ALU.mult))
                vsafe(v, v.tensor_scalar(q, bank(bq, 0, 1, tw), 1.0 / DM, None,
                                         ALU.mult))
                vsafe(v, v.tensor_mul(t2, m, m))
                vsafe(v, v.tensor_sub(q, q, t2))
                ins = v.tensor_scalar(q, q, EPS_LN, None, ALU.add)
                ve_marks.append(inc("ve", ins))

            emit("tensor", pe)
            emit("vector", ve)

        def ac1(s):
            wait(s, "ve")
            ins = s.activation(row_b(1, tokh), row_b(1, tokh), AF.Ln,
                               bias=0.0, scale=1.0)
            s.wait_ge(sem["ac"], inc("ac", ins))
            inc("ac", s.activation(row_b(1, tokh), row_b(1, tokh), AF.Exp,
                                   bias=0.0, scale=-0.5))

        emit("scalar", ac1)
        ac_rows = cnt["ac"]

        bc_marks, nrm_marks = [], []
        for bi, (t0, tw) in enumerate(tbs):
            bm, br = 4 + (bi % 2) * 2, 5 + (bi % 2) * 2

            def pe(t, bi=bi, t0=t0, tw=tw, bm=bm, br=br):
                if bi == 0:
                    t.wait_ge(sem["ac"], ac_rows)
                if bi >= 2:
                    t.wait_ge(sem["ve"], nrm_marks[bi - 2])
                t.matmul(bank(bm, 0, 128, tw), ones_row[:, :], row_b(0, tw, t0),
                         start=True, stop=True)
                ins = t.matmul(bank(br, 0, 128, tw), ones_row[:, :], row_b(1, tw, t0),
                               start=True, stop=True)
                bc_marks.append(inc("pe", ins))

            def ve(v, bi=bi, t0=t0, tw=tw, bm=bm, br=br):
                v.wait_ge(sem["pe"], bc_marks[bi])
                for c in range(NCH):
                    vsafe(v, v.tensor_sub(s0(0, tw), xs(c, off + t0, tw),
                                          bank(bm, 0, 128, tw)))
                    ins = v.tensor_mul(
                        xn[:, c * TOKH + off + t0:c * TOKH + off + t0 + tw],
                        s0(0, tw), bank(br, 0, 128, tw))
                    if c < NCH - 1:
                        vsafe(v, ins)
                nrm_marks.append(inc("ve", ins))

            emit("tensor", pe)
            emit("vector", ve)

    def emit_routing(layer):
        ne = L0_NE if layer == 0 else L1_NE
        bname = "l0_bbias_bc" if layer == 0 else "l1_rbias_bc"
        gwc = 8 if layer == 0 else 16
        ln_ve = cnt["ve"]

        def po(g):
            wait(g, "ve", ln_ve)
            if layer == 1:
                inc("wl", g.dma_start(
                    out=gw_sb[:, 0:NCH * 16].rearrange("p (c e) -> p c e", c=NCH),
                    in_=inp["l1_router_w"].ap().rearrange("(c p) e -> p c e", p=128)),
                    16)
            inc("wl", g.dma_start(out=bias_bc[:, 0:ne], in_=inp[bname].ap()), 16)

        emit("gpsimd", po)
        wl_done = cnt["wl"]

        def rb(tt, o, n):
            return SCR[:, tt * 256 + o:tt * 256 + o + n]

        pe_m, ve1_m, ac_m = {}, {}, {}
        for tt in range(16):
            bk = tt % 2

            def pe(t, tt=tt, bk=bk):
                if tt == 0:
                    t.wait_ge(sem["wl"], wl_done)
                    t.wait_ge(sem["ve"], ln_ve)
                if tt >= 2:
                    t.wait_ge(sem["ve"], ve1_m[tt - 2])
                for c in range(NCH):
                    ins = t.matmul(bank(bk, 0, 128, ne), xnm(c, tt * 128, 128),
                                   gw_sb[:, c * gwc:c * gwc + ne],
                                   start=(c == 0), stop=(c == NCH - 1))
                pe_m[tt] = inc("pe", ins)

            def ve1(v, tt=tt, bk=bk):
                v.wait_ge(sem["pe"], pe_m[tt])
                if layer == 0:
                    ins = v.tensor_add(rb(tt, 0, ne), bank(bk, 0, 128, ne),
                                       bias_bc[:, 0:ne])
                else:
                    ins = v.tensor_copy(rb(tt, 0, ne), bank(bk, 0, 128, ne))
                ve1_m[tt] = inc("ve", ins)

            def ac1(s, tt=tt):
                s.wait_ge(sem["ve"], ve1_m[tt])
                ins = s.activation(rb(tt, 0, ne), rb(tt, 0, ne), AF.Sigmoid,
                                   bias=0.0, scale=1.0)
                ac_m[tt] = inc("ac", ins)

            emit("tensor", pe)
            emit("vector", ve1)
            emit("scalar", ac1)

        def ve2(v):
            wait(v, "ac", ac_m[15])
            T16 = range(16)

            def sc(tt):
                return rb(tt, 0, ne)

            def each(f):
                ins = None
                for tt in T16:
                    ins = f(tt)
                return ins

            def step(f):
                ins = each(f)
                v.wait_ge(sem["ve"], inc("ve", ins))
                return ins

            if layer == 0:
                sp = lambda tt: rb(tt, 32, ne)
                u = lambda tt: rb(tt, 64, ne)
                m0 = lambda tt: rb(tt, 96, 1)
                m0b = lambda tt: rb(tt, 97, 1)
                den = lambda tt: rb(tt, 98, 1)
                rec = lambda tt: rb(tt, 99, 1)
                step(lambda tt: v.tensor_sub(sp(tt), sc(tt), iota8[:, 0:ne]))
                step(lambda tt: v.tensor_reduce(
                    m0(tt), sp(tt).rearrange("p (a e) -> p a e", a=1), AX.X, ALU.max))
                step(lambda tt: v.tensor_single_scalar(u(tt), sp(tt), m0(tt),
                                                       ALU.is_ge))
                step(lambda tt: v.scalar_tensor_tensor(sp(tt), u(tt), -BIG, sp(tt),
                                                       ALU.mult, ALU.add))
                step(lambda tt: v.tensor_reduce(
                    m0b(tt), sp(tt).rearrange("p (a e) -> p a e", a=1), AX.X, ALU.max))
                step(lambda tt: v.tensor_single_scalar(sp(tt), sp(tt), m0b(tt),
                                                       ALU.is_ge))
                step(lambda tt: v.tensor_add(u(tt), u(tt), sp(tt)))
                step(lambda tt: v.tensor_mul(sp(tt), sc(tt), u(tt)))
                step(lambda tt: v.tensor_reduce(
                    den(tt), sp(tt).rearrange("p (a e) -> p a e", a=1), AX.X, ALU.add))
                step(lambda tt: v.tensor_scalar(den(tt), den(tt), 1e-6, None,
                                                ALU.add))
                step(lambda tt: v.reciprocal(rec(tt), den(tt)))
                ins = each(lambda tt: v.tensor_single_scalar(
                    comb[:, tt * 16:tt * 16 + ne], sp(tt), rec(tt), ALU.mult))
            else:
                ss = lambda tt: rb(tt, 32, ne)
                u = lambda tt: rb(tt, 64, ne)
                ge = lambda tt: rb(tt, 96, ne)
                wk = lambda tt: rb(tt, 128, ne)
                g1 = lambda tt: rb(tt, 160, 4)
                g2 = lambda tt: rb(tt, 168, 4)
                g3 = lambda tt: rb(tt, 176, 4)
                m0 = lambda tt: rb(tt, 184, 1)
                m1 = lambda tt: rb(tt, 185, 1)
                step(lambda tt: v.tensor_add(ss(tt), sc(tt), bias_bc[:, 0:ne]))
                step(lambda tt: v.tensor_sub(ss(tt), ss(tt), iota16[:, 0:ne]))
                step(lambda tt: v.tensor_reduce(
                    g1(tt), ss(tt).rearrange("p (g r) -> p g r", g=4), AX.X, ALU.max))
                for jj in range(4):
                    step(lambda tt, jj=jj: v.tensor_copy(
                        ge(tt)[:, 4 * jj:4 * jj + 4],
                        g1(tt)[:, jj:jj + 1].broadcast_to([128, 4])))
                step(lambda tt: v.tensor_sub(wk(tt), ss(tt), ge(tt)))
                step(lambda tt: v.tensor_scalar(u(tt), wk(tt), -1e-12, None,
                                                ALU.is_ge))
                step(lambda tt: v.scalar_tensor_tensor(wk(tt), u(tt), -BIG, ss(tt),
                                                       ALU.mult, ALU.add))
                step(lambda tt: v.tensor_reduce(
                    g2(tt), wk(tt).rearrange("p (g r) -> p g r", g=4), AX.X, ALU.max))
                step(lambda tt: v.tensor_add(g1(tt), g1(tt), g2(tt)))
                step(lambda tt: v.tensor_sub(g1(tt), g1(tt), iota4[:, :]))
                step(lambda tt: v.tensor_reduce(
                    m0(tt), g1(tt).rearrange("p (a e) -> p a e", a=1), AX.X, ALU.max))
                step(lambda tt: v.tensor_single_scalar(g2(tt), g1(tt), m0(tt),
                                                       ALU.is_ge))
                step(lambda tt: v.scalar_tensor_tensor(g3(tt), g2(tt), -BIG, g1(tt),
                                                       ALU.mult, ALU.add))
                step(lambda tt: v.tensor_reduce(
                    m0(tt), g3(tt).rearrange("p (a e) -> p a e", a=1), AX.X, ALU.max))
                step(lambda tt: v.tensor_single_scalar(g3(tt), g1(tt), m0(tt),
                                                       ALU.is_ge))
                for jj in range(4):
                    step(lambda tt, jj=jj: v.tensor_copy(
                        ge(tt)[:, 4 * jj:4 * jj + 4],
                        g3(tt)[:, jj:jj + 1].broadcast_to([128, 4])))
                step(lambda tt: v.tensor_sub(wk(tt), ss(tt), neg1p[:, 0:ne]))
                step(lambda tt: v.tensor_mul(wk(tt), wk(tt), ge(tt)))
                step(lambda tt: v.tensor_add(wk(tt), wk(tt), neg1p[:, 0:ne]))
                step(lambda tt: v.tensor_single_scalar(u(tt), wk(tt), 0.0, ALU.add))
                for _ in range(3):
                    step(lambda tt: v.tensor_reduce(
                        m0(tt), u(tt).rearrange("p (a e) -> p a e", a=1), AX.X,
                        ALU.max))
                    step(lambda tt: v.tensor_single_scalar(ge(tt), u(tt), m0(tt),
                                                           ALU.is_ge))
                    step(lambda tt: v.scalar_tensor_tensor(u(tt), ge(tt), -BIG,
                                                           u(tt), ALU.mult, ALU.add))
                step(lambda tt: v.tensor_reduce(
                    m0(tt), u(tt).rearrange("p (a e) -> p a e", a=1), AX.X, ALU.max))
                step(lambda tt: v.tensor_single_scalar(u(tt), wk(tt), m0(tt),
                                                       ALU.is_ge))
                step(lambda tt: v.tensor_mul(wk(tt), sc(tt), u(tt)))
                step(lambda tt: v.tensor_reduce(
                    m0(tt), wk(tt).rearrange("p (a e) -> p a e", a=1), AX.X, ALU.add))
                step(lambda tt: v.tensor_scalar(m0(tt), m0(tt), 1e-20, None,
                                                ALU.add))
                step(lambda tt: v.reciprocal(m1(tt), m0(tt)))
                ins = each(lambda tt: v.tensor_scalar(
                    comb[:, tt * 16:tt * 16 + ne], wk(tt), m1(tt), 2.5,
                    ALU.mult, ALU.mult))
            inc("ve", ins)

        emit("vector", ve2)
        ve2_all = cnt["ve"]

        tr_pe, tr_ve, tr_po = {}, {}, {}
        for tt in range(16):
            bk = 4 + tt % 2

            def pe(t, tt=tt, bk=bk):
                if tt == 0:
                    t.wait_ge(sem["ve"], ve2_all)
                if tt >= 2:
                    t.wait_ge(sem["ve"], tr_ve[tt - 2])
                ins = t.transpose(bank(bk, 0, ne, 128), comb[:, tt * 16:tt * 16 + ne],
                                  ident[:, :])
                tr_pe[tt] = inc("pe", ins)

            def ve(v, tt=tt, bk=bk):
                v.wait_ge(sem["pe"], tr_pe[tt])
                if tt >= 1:
                    v.wait_ge(sem["po"], tr_po[tt - 1])
                ins = v.tensor_copy(combT[0:ne, :], bank(bk, 0, ne, 128))
                tr_ve[tt] = inc("ve", ins)

            def po2(g, tt=tt):
                g.wait_ge(sem["ve"], tr_ve[tt])
                ins = g.dma_start(out=combT_d.ap()[0:ne, tt * 128:(tt + 1) * 128],
                                  in_=combT[0:ne, :])
                tr_po[tt] = inc("po", ins, 16)

            emit("tensor", pe)
            emit("vector", ve)
            emit("gpsimd", po2)

    def emit_moe(layer):
        if layer == 0:
            nhc = L0_DH // 128
            W1_OFF, W2_OFF, H_OFF, SIL_OFF = 0, 16384, 32768, None
            WU_OFF = None
            experts = [("l0_w1", "l0_w2", None, e, True) for e in range(L0_NE)]
            if DBG_NEX:
                experts = experts[:DBG_NEX]
            dh = L0_DH
        else:
            nhc = L1_DH // 128
            W1_OFF, WU_OFF, W2_OFF, H_OFF, SIL_OFF = 0, 8192, 16384, 24576, 32768
            experts = [("l1_gate_w", "l1_down_w", "l1_up_w", e, True)
                       for e in range(L1_NE)]
            experts += [("l1_shg_w", "l1_shd_w", "l1_shu_w", sh, False)
                        for sh in range(2)]
            dh = L1_DH
        nex = len(experts)

        npar = 1 if layer == 0 else 2

        def Htile(par, hc, tw):
            o = H_OFF + (par % npar) * nhc * TB + hc * TB
            return AR[:, o:o + tw]

        ld_m = [None] * nex
        ld2_m = [None] * nex
        last_a = [0, 0]
        alpha_done = [None] * nex
        beta_done = [None] * nex
        ev_m = {}
        hready = {}
        hfree = {}
        cb_last = [0]

        def emit_load(ei):
            g1n, g2n, gun, e, is_real = experts[ei]

            def po(g, ei=ei, g1n=g1n, g2n=g2n, gun=gun, e=e, is_real=is_real):
                if ei == 0:
                    wait(g, "pe")
                    wait(g, "ve")
                if ei >= 1 and alpha_done[ei - 1] is not None:
                    g.wait_ge(sem["pe"], alpha_done[ei - 1])
                if layer == 0 or is_real:
                    src = inp[g1n].ap()[e]
                else:
                    src = inp[g1n].ap()[:, e * L1_DH:(e + 1) * L1_DH]
                ins = g.dma_start(
                    out=ar(W1_OFF, NCH * dh).rearrange("p (c h) -> p c h", c=NCH),
                    in_=src.rearrange("(c p) h -> p c h", p=128))
                inc("wl1", ins, 16)
                if layer == 1:
                    srcu = inp[gun].ap()[e] if is_real else \
                        inp[gun].ap()[:, e * L1_DH:(e + 1) * L1_DH]
                    ins = g.dma_start(
                        out=ar(WU_OFF, NCH * dh).rearrange("p (c h) -> p c h", c=NCH),
                        in_=srcu.rearrange("(c p) h -> p c h", p=128))
                    inc("wlu", ins, 16)
                ld_m[ei] = (cnt["wl1"], cnt["wlu"])
                if ei >= 1 and beta_done[ei - 1] is not None:
                    g.wait_ge(sem["pe"], beta_done[ei - 1])
                if layer == 0 or is_real:
                    srcd = inp[g2n].ap()[e]
                else:
                    srcd = inp[g2n].ap()[e * L1_DH:(e + 1) * L1_DH, :]
                ins = g.dma_start(
                    out=ar(W2_OFF, nhc * DM).rearrange("p (h d) -> p h d", h=nhc),
                    in_=srcd.rearrange("(h p) d -> p h d", p=128))
                ld2_m[ei] = inc("wl2", ins, 16)

            emit("gpsimd", po)

        emit_load(0)
        rt_po = cnt["po"]

        for ei in range(nex):
            g1n, g2n, gun, e, is_real = experts[ei]

            if is_real:
                cb_pe = {}
                cb_po = {}
                cb_vem = {}
                for bi, (t0, tw) in enumerate(TBS):
                    def poc(g, ei=ei, e=e, bi=bi, t0=t0, tw=tw):
                        if ei == 0 and bi == 0:
                            g.wait_ge(sem["po"], rt_po)
                        if bi >= 1:
                            g.wait_ge(sem["pe"], cb_pe[bi - 1])
                        elif cb_last[0]:
                            g.wait_ge(sem["pe"], cb_last[0])
                        cb_po[bi] = inc("po", g.dma_start(
                            out=crow_bf[0:1, :],
                            in_=combT_d.ap()[e:e + 1, t0:t0 + tw]), 16)

                    def pec(t, ei=ei, bi=bi, t0=t0, tw=tw):
                        if bi < 2:
                            wait(t, "ve")
                        else:
                            t.wait_ge(sem["ve"], cb_vem[bi - 2])
                        t.wait_ge(sem["po"], cb_po[bi])
                        ins = t.matmul(bank(6 + bi % 2, 0, 128, tw), ones_row[:, :],
                                       crow_bf[0:1, 0:tw], start=True, stop=True)
                        cb_pe[bi] = inc("pe", ins)
                        if bi == NTB - 1:
                            cb_last[0] = cnt["pe"]

                    def vec(v, bi=bi, t0=t0, tw=tw):
                        v.wait_ge(sem["pe"], cb_pe[bi])
                        ins = v.tensor_copy(cbc[:, t0:t0 + tw],
                                            bank(6 + bi % 2, 0, 128, tw))
                        cb_vem[bi] = inc("ve", ins)

                    emit("gpsimd", poc)
                    emit("tensor", pec)
                    emit("vector", vec)
            cb_ve = cnt["ve"]

            if layer == 0:
                sched = [("a", 0), ("b", 0), ("a", 1), ("b", 1), ("a", 2),
                         ("b", 2), ("a", 3), ("b", 3)]
            else:
                sched = [("a", 0), ("a", 1), ("b", 0), ("a", 2), ("b", 1),
                         ("a", 3), ("b", 2), ("b", 3)]
            for kind, bi in sched:
                t0, tw = TBS[bi]
                par = bi % 2
                if kind == "a":
                    al_pe, al_ac, al_ve = {}, {}, {}

                    def pea(t, ei=ei, bi=bi, t0=t0, tw=tw,
                            al_pe=al_pe, cb_ve=cb_ve, is_real=is_real):
                        ac_base = cnt["ac"]
                        ve_base = cnt["ve"]
                        if layer == 0:
                            nve = 1 if is_real else 0
                        else:
                            nve = 2 if is_real else 1
                        if bi == 0:
                            t.wait_ge(sem["wl1"], ld_m[ei][0])
                            if layer == 1:
                                t.wait_ge(sem["wlu"], ld_m[ei][1])
                            t.wait_ge(sem["ve"], cb_ve)
                        if bi >= npar:
                            t.wait_ge(sem["pe"], hfree[(ei, bi - npar)])
                        if last_a[0]:
                            t.wait_ge(sem["ac"], last_a[0])
                        if last_a[1]:
                            t.wait_ge(sem["ve"], last_a[1])
                        for hc in range(nhc):
                            if layer == 0:
                                bkg, bku = 4 + (hc % 2), None
                            else:
                                bkg, bku = 4 + (hc % 2) * 2, 5 + (hc % 2) * 2
                            if hc >= 2:
                                t.wait_ge(sem["ac"], ac_base + hc - 1)
                                if layer == 1 and nve:
                                    t.wait_ge(sem["ve"], ve_base + (hc - 1) * nve)
                            crange = (list(range(NCH)) if globals()["DBG_ALPHA_C"]
                                      is None else globals()["DBG_ALPHA_C"])
                            for ci, c in enumerate(crange):
                                ins = t.matmul(
                                    bank(bkg, 0, 128, tw),
                                    AR[:, W1_OFF + c * dh + hc * 128:
                                       W1_OFF + c * dh + hc * 128 + 128],
                                    xnm(c, t0, tw),
                                    start=(ci == 0), stop=(ci == len(crange) - 1))
                            al_pe[hc] = inc("pe", ins)
                            if layer == 1:
                                for c in range(NCH):
                                    ins = t.matmul(
                                        bank(bku, 0, 128, tw),
                                        AR[:, WU_OFF + c * dh + hc * 128:
                                           WU_OFF + c * dh + hc * 128 + 128],
                                        xnm(c, t0, tw),
                                        start=(c == 0), stop=(c == NCH - 1))
                                al_pe[hc] = inc("pe", ins)
                        if bi == 3:
                            alpha_done[ei] = cnt["pe"]

                    def aca(s, ei=ei, bi=bi, tw=tw, par=par,
                            al_pe=al_pe, al_ac=al_ac):
                        if layer == 1 and bi >= 2 and (ei, bi - 2) in hready:
                            s.wait_ge(sem["ve"], hready[(ei, bi - 2)][1])
                        if layer == 0 and bi >= 1:
                            s.wait_ge(sem["pe"], hfree[(ei, bi - 1)])
                        for hc in range(nhc):
                            bkg = 4 + (hc % 2) if layer == 0 else 4 + (hc % 2) * 2
                            s.wait_ge(sem["pe"], al_pe[hc])
                            if layer == 0:
                                dst = Htile(par, hc, tw)
                            else:
                                dst = AR[:, SIL_OFF + hc * TB:SIL_OFF + hc * TB + tw]
                            ins = s.activation(dst, bank(bkg, 0, 128, tw), AF.Silu,
                                               bias=0.0, scale=1.0)
                            al_ac[hc] = inc("ac", ins)

                    def vea(v, ei=ei, bi=bi, t0=t0, tw=tw, par=par, is_real=is_real,
                            al_ac=al_ac, al_ve=al_ve):
                        ins = None
                        for hc in range(nhc):
                            v.wait_ge(sem["ac"], al_ac[hc])
                            h = Htile(par, hc, tw)
                            if layer == 0:
                                if is_real and not globals()['DBG_NOSCALE']:
                                    ins = v.tensor_mul(h, h, cbc[:, t0:t0 + tw])
                            else:
                                bku = 5 + (hc % 2) * 2
                                sg = AR[:, SIL_OFF + hc * TB:SIL_OFF + hc * TB + tw]
                                ins = v.tensor_mul(h, bank(bku, 0, 128, tw), sg)
                                if is_real:
                                    vsafe(v, ins)
                                    ins = v.tensor_mul(h, h, cbc[:, t0:t0 + tw])
                            if ins is not None:
                                al_ve[hc] = inc("ve", ins)
                            else:
                                al_ve[hc] = cnt["ve"]

                    emit("tensor", pea)
                    emit("scalar", aca)
                    emit("vector", vea)
                    hready[(ei, bi)] = (cnt["ac"], cnt["ve"])
                    last_a[0], last_a[1] = cnt["ac"], cnt["ve"]
                    if ei == 0 and bi == 0 and layer == 0:
                        dbg_snap("H00", AR[:, H_OFF:H_OFF + nhc * TB],
                                 (128, nhc * TB), BF16)
                        dbg_snap("xnsnap", xn[:, :], (128, NCH * TOKH), BF16)
                        dbg_snap("w1snap", AR[:, 0:8192], (128, 8192), BF16)
                else:
                    for dq in range(2):
                        be_pe = {}

                        def peb(t, ei=ei, bi=bi, t0=t0, tw=tw, par=par, dq=dq,
                                be_pe=be_pe):
                            if dq == 0:
                                hac, hve = hready[(ei, bi)]
                                t.wait_ge(sem["ac"], hac)
                                t.wait_ge(sem["ve"], hve)
                                if bi == 0:
                                    t.wait_ge(sem["wl2"], ld2_m[ei])
                            for dc4 in range(4):
                                if dq == 1:
                                    km = ev_m.get((ei, bi, "q0"))
                                elif bi > 0:
                                    km = ev_m.get((ei, bi - 1, dc4))
                                elif ei > 0:
                                    km = ev_m.get((ei - 1, NTB - 1, dc4))
                                else:
                                    km = None
                                if km:
                                    t.wait_ge(sem["ve"], km)
                                    break
                            for hc in range(nhc):
                                for dc4 in range(4):
                                    dc = dq * 4 + dc4
                                    ins = t.matmul(
                                        bank(dc4, 0, 128, tw),
                                        AR[:, W2_OFF + hc * DM + dc * 128:
                                           W2_OFF + hc * DM + dc * 128 + 128],
                                        Htile(par, hc, tw),
                                        start=(hc == 0), stop=(hc == nhc - 1))
                            be_pe[0] = inc("pe", ins)
                            if dq == 1:
                                hfree[(ei, bi)] = cnt["pe"]
                                if bi == 3:
                                    beta_done[ei] = cnt["pe"]

                        def veb(v, ei=ei, bi=bi, t0=t0, tw=tw, dq=dq, be_pe=be_pe):
                            v.wait_ge(sem["pe"], be_pe[0])
                            for dc4 in range(4):
                                dc = dq * 4 + dc4
                                ins = v.tensor_add(xsm(dc, t0, tw), xsm(dc, t0, tw),
                                                   bank(dc4, 0, 128, tw))
                                inc("ve", ins)
                            if dq == 0:
                                ev_m[(ei, bi, "q0")] = cnt["ve"]
                            else:
                                for dc4 in range(4):
                                    ev_m[(ei, bi, dc4)] = cnt["ve"]

                        emit("tensor", peb)
                        emit("vector", veb)

            if ei + 1 < nex:
                emit_load(ei + 1)

    def emit_l0_passA():
        ln_ve = cnt["ve"]
        ipw_m = {}
        pe_m, ac_m, ve_m = {}, {}, {}
        def xzh(j):
            return SCR[:, 4096 + 3 * (j % 2):4096 + 3 * (j % 2) + 3]

        def po0(g):
            wait(g, "ve", ln_ve)
            wait(g, "pe")
            inc("wl", g.dma_start(
                out=AR[:, XP_OFF:XP_OFF + NCI * DT]
                .rearrange("p (c r) -> p c r", c=NCI),
                in_=inp["l0_xp_w"].ap().rearrange("(c p) r -> p c r", p=128)), 16)
            inc("wl", g.dma_start(out=dtw_sb[:, :], in_=inp["l0_dt_w"].ap()), 16)
            for j in range(2):
                ins = g.dma_start(
                    out=ar(RING_OFF + (j % 2) * 1024, 1024)
                    .rearrange("p (c m) -> p c m", c=NCH),
                    in_=inp["l0_ip_w"].ap()[:, j * 128:(j + 1) * 128]
                    .rearrange("(c p) m -> p c m", p=128))
                ipw_m[j] = inc("wla" if j % 2 == 0 else "wlb", ins, 16)

        emit("gpsimd", po0)

        for j in range(NCI):
            if j >= 2:
                def po(g, j=j):
                    g.wait_ge(sem["pe"], pe_m[(j - 2, len(TBS_H) - 1)])
                    ins = g.dma_start(
                        out=ar(RING_OFF + (j % 2) * 1024, 1024)
                        .rearrange("p (c m) -> p c m", c=NCH),
                        in_=inp["l0_ip_w"].ap()[:, j * 128:(j + 1) * 128]
                        .rearrange("(c p) m -> p c m", p=128))
                    ipw_m[j] = inc("wla" if j % 2 == 0 else "wlb", ins, 16)

                emit("gpsimd", po)

            for bi, (t0, tw) in enumerate(TBS_H):
                g_idx = j * len(TBS_H) + bi
                bk = g_idx % 3

                def pe(t, j=j, bi=bi, t0=t0, tw=tw, bk=bk, g_idx=g_idx):
                    if bi == 0:
                        t.wait_ge(sem["wla" if j % 2 == 0 else "wlb"], ipw_m[j])
                        if j == 0:
                            t.wait_ge(sem["ve"], ln_ve)
                        if j >= 1:
                            km = ve_m.get((j - 2, "conv"))
                            if km:
                                t.wait_ge(sem["ve"], km)
                    if g_idx >= 3:
                        pj, pb = divmod(g_idx - 3, len(TBS_H))
                        t.wait_ge(sem["ac"], ac_m[(pj, pb)])
                    for c in range(NCH):
                        ins = t.matmul(bank(bk, 0, 128, tw),
                                       ar(RING_OFF + (j % 2) * 1024, 1024)
                                       [:, c * 128:(c + 1) * 128],
                                       xnf(c, t0, tw),
                                       start=(c == 0), stop=(c == NCH - 1))
                    pe_m[(j, bi)] = inc("pe", ins)

                def ac(s, j=j, bi=bi, t0=t0, tw=tw, bk=bk):
                    s.wait_ge(sem["pe"], pe_m[(j, bi)])
                    if bi == 0 and j >= 1:
                        km = ve_m.get((j - 2, "conv"))
                        if km:
                            s.wait_ge(sem["ve"], km)
                    if bi == len(TBS_H) - 1:
                        ins = s.copy(Uc(j)[:, 1537:2048], bank(bk, 0, 128, tw))
                    elif bi == 0:
                        s.copy(xzh(j), bank(bk, 0, 128, 3))
                        ins = s.copy(Uc(j)[:, 0:509], bank(bk, 0, 128, tw)[:, 3:512])
                    else:
                        ins = s.copy(Uc(j)[:, t0 - 3:t0 - 3 + tw],
                                     bank(bk, 0, 128, tw))
                    ac_m[(j, bi)] = inc("ac", ins)

                emit("tensor", pe)
                emit("scalar", ac)

            def ve(v, j=j):
                wait(v, "ac", ac_m[(j, len(TBS_H) - 1)])
                if j >= 1:
                    v.wait_ge(sem["ac"], ac_m[(j - 1, "silu")])
                first = True
                for k in range(4):
                    cwk = colpack[:, 64 + 16 * k + j:64 + 16 * k + j + 1]
                    hw = 3 - k
                    if hw > 0:
                        if first:
                            vsafe(v, v.tensor_single_scalar(
                                sconv(0, hw), xzh(j)[:, k:3], cwk, ALU.mult))
                        else:
                            vsafe(v, v.scalar_tensor_tensor(
                                sconv(0, hw), xzh(j)[:, k:3], cwk, sconv(0, hw),
                                ALU.mult, ALU.add))
                    mw = TOK - max(0, hw)
                    if first:
                        vsafe(v, v.tensor_single_scalar(
                            sconv(max(0, hw), mw), Uc(j)[:, 0:mw], cwk, ALU.mult))
                        first = False
                    else:
                        ins = v.scalar_tensor_tensor(
                            sconv(max(0, hw), mw), Uc(j)[:, 0:mw], cwk,
                            sconv(max(0, hw), mw), ALU.mult, ALU.add)
                        if k < 3:
                            vsafe(v, ins)
                ve_m[(j, "conv")] = inc("ve", ins)

            def ac2(s, j=j):
                wait(s, "ve", ve_m[(j, "conv")])
                cb = colpack[:, 32 + j:32 + j + 1]
                ins = s.activation(Uc(j), sconv(), AF.Silu, bias=cb, scale=1.0)
                ac_m[(j, "silu")] = inc("ac", ins)

            emit("vector", ve)
            emit("scalar", ac2)

            if j >= 1:
                def pexp(t, jj=j - 1):
                    if jj == 0:
                        wait(t, "wl")
                    t.wait_ge(sem["ac"], ac_m[(jj, "silu")])
                    for bi, (t0, tw) in enumerate(TBS):
                        ins = t.matmul(bank(4 + bi, 0, 64, tw),
                                       xp_sb(jj),
                                       Uc(jj)[:, t0:t0 + tw],
                                       start=(jj == 0), stop=(jj == NCI - 1))
                    inc("pe", ins)

                emit("tensor", pexp)

        def pexpl(t):
            jj = NCI - 1
            t.wait_ge(sem["ac"], ac_m[(jj, "silu")])
            for bi, (t0, tw) in enumerate(TBS):
                ins = t.matmul(bank(4 + bi, 0, 64, tw),
                               xp_sb(jj),
                               Uc(jj)[:, t0:t0 + tw],
                               start=(jj == 0), stop=(jj == NCI - 1))
            inc("pe", ins)

        emit("tensor", pexpl)
        pe_xp = cnt["pe"]

        def vedt(v):
            v.wait_ge(sem["pe"], pe_xp)
            for bi, (t0, tw) in enumerate(TBS):
                ins = v.tensor_copy(AR[0:64, DTL_OFF + t0:DTL_OFF + t0 + tw],
                                    bank(4 + bi, 0, 64, tw))
            inc("ve", ins)

        emit("vector", vedt)

    dc_war = {}

    def emit_delta_chain(j, accum_R, s0free=None):
        bks = (0, 4)[j % 2]
        marks = {}

        def pe(t, j=j, bks=bks):
            if bks in dc_war:
                t.wait_ge(sem["ac"], dc_war[bks])
            sw = dc_war.get("scan%d" % (j % 2))
            if sw:
                t.wait_ge(sem["ve"], sw)
            if j <= 1:
                wait(t, "ve")
                wait(t, "wl")
            for bi, (t0, tw) in enumerate(TBS):
                ins = t.matmul(bank(bks + bi, 0, 128, tw),
                               dtw_sb[0:64, j * 128:(j + 1) * 128],
                               AR[0:64, DTL_OFF + t0:DTL_OFF + t0 + tw],
                               start=True, stop=True)
            marks["pe"] = inc("pe", ins)

        def ac(s, j=j, bks=bks):
            s.wait_ge(sem["pe"], marks["pe"])
            if s0free is not None:
                s.wait_ge(sem["ve"], s0free)
            dtb = colpack[:, 48 + j:48 + j + 1]
            for bi, (t0, tw) in enumerate(TBS):
                ins = s.activation(s0(t0, tw), bank(bks + bi, 0, 128, tw), AF.Exp,
                                   bias=dtb, scale=1.0)
            s.wait_ge(sem["ac"], inc("ac", ins))
            ins = s.activation(s0(), s0(), AF.Ln, bias=1.0, scale=1.0)
            marks["ac"] = inc("ac", ins)
            dc_war[bks] = marks["ac"]

        def ve(v, j=j):
            wait(v, "ac", marks["ac"])
            vsafe(v, v.tensor_scalar(s0(), s0(), 1e-6, None, ALU.max))
            if accum_R:
                vsafe(v, v.tensor_scalar(s0(), s0(), 10.0, None, ALU.min, ALU.add,
                                         accum_out=packR[:, j:j + 1]))
            else:
                vsafe(v, v.tensor_scalar(s0(), s0(), 10.0, None, ALU.min))
            Acol = colpack[:, j:j + 1]
            ins = v.tensor_single_scalar(s0(), s0(), Acol, ALU.mult)
            marks["ve"] = inc("ve", ins)

        def ac2(s, j=j):
            wait(s, "ve", marks["ve"])
            ins = s.activation(s0(), s0(), AF.Exp, bias=0.0, scale=1.0)
            marks["ac2"] = inc("ac", ins)

        emit("tensor", pe)
        emit("scalar", ac)
        emit("vector", ve)
        emit("scalar", ac2)
        return marks

    def emit_l0_prepass():
        prev_s0free = None
        for j in range(NCI):
            marks = emit_delta_chain(j, accum_R=True, s0free=prev_s0free)

            def ve(v, j=j, marks=marks):
                bks = (0, 4)[j % 2]
                wait(v, "ac", marks["ac2"])
                vsafe(v, v.tensor_scalar(s0(), s0(), 1e-6, 1.0, ALU.max, ALU.min))
                vsafe(v, v.tensor_copy(s1(), Uc(j)))
                pscan = psum[:, bks * 512:bks * 512 + TOK]
                vsafe(v, v.tensor_tensor_scan(pscan, s0(), s1(), zeros_col[:, 0:1],
                                              ALU.mult, ALU.add))
                ins = v.tensor_copy(packF[:, j:j + 1], pscan[:, TOK - 1:TOK])
                marks["scan"] = inc("ve", ins)

            emit("vector", ve)
            prev_s0free = cnt["ve"]
            dc_war["scan%d" % (j % 2)] = cnt["ve"]

        def vp(v):
            wait(v, "ve")
            ins = v.tensor_mul(packP[:, :], packR[:, :], colpack[:, 0:16])
            inc("ve", ins)

        def acp(s):
            wait(s, "ve")
            inc("ac", s.activation(packP[:, :], packP[:, :], AF.Exp,
                                   bias=0.0, scale=1.0))

        def vpack(v):
            wait(v, "ac")
            wait(v, "ve")
            v.tensor_copy(agg[:, 0:16], packF[:, :])
            inc("ve", v.tensor_copy(agg[:, 16:32], packP[:, :]))

        emit("vector", vp)
        emit("scalar", acp)
        emit("vector", vpack)
        ve_pack = cnt["ve"]

        def po(g):
            g.wait_ge(sem["ve"], ve_pack)
            inc("po", g.dma_start(out=cc0_in.ap(), in_=agg[:, 0:32]), 16)
            wait(g, "po")
            inc("cc", g.collective_compute(
                "AllGather", ALU.bypass, replica_groups=[list(range(NCORE))],
                ins=[cc0_in.ap().opt()], outs=[cc0_out.ap().opt()]), 1)
            wait(g, "cc")
            for r in range(NCORE):
                inc("po", g.dma_start(out=agg[:, 32 * r:32 * (r + 1)],
                                      in_=cc0_out.ap()[128 * r:128 * (r + 1), :]), 16)

        emit("gpsimd", po)
        po_done = cnt["po"]

        def vcomb(v):
            v.wait_ge(sem["po"], po_done)
            vsafe(v, v.memset(S0[:, :], 0.0))
            vsafe(v, v.memset(packR[:, :], 0.0))
            for r in range(NCORE):
                vsafe(v, v.scalar_tensor_tensor(S0[:, :], packR[:, :],
                                                onehot8[:, r:r + 1], S0[:, :],
                                                ALU.mult, ALU.add))
                if r < NCORE - 1:
                    vsafe(v, v.tensor_mul(packP[:, :],
                                          agg[:, 32 * r + 16:32 * r + 32],
                                          packR[:, :]))
                    vsafe(v, v.tensor_add(packR[:, :], packP[:, :],
                                          agg[:, 32 * r:32 * r + 16]))

        emit("vector", vcomb)

    def emit_l0_passC1():
        emit_l0_passC1.s0_ready = cnt["ve"]
        prev_s0free = None
        for j in range(NCI):
            marks = emit_delta_chain(j, accum_R=False, s0free=prev_s0free)

            def ve(v, j=j, marks=marks):
                bks = (0, 4)[j % 2]
                wait(v, "ac", marks["ac2"])
                if j == 0:
                    wait(v, "ve", emit_l0_passC1.s0_ready)
                vsafe(v, v.tensor_scalar(s0(), s0(), 1e-6, 1.0, ALU.max, ALU.min))
                vsafe(v, v.tensor_copy(s1(), Uc(j)))
                pscan = psum[:, bks * 512:bks * 512 + TOK]
                vsafe(v, v.tensor_tensor_scan(pscan, s0(), s1(), S0[:, j:j + 1],
                                              ALU.mult, ALU.add))
                Dcol = colpack[:, 16 + j:16 + j + 1]
                vsafe(v, v.scalar_tensor_tensor(s1(), s1(), Dcol, pscan,
                                                ALU.mult, ALU.add))
                vsafe(v, v.tensor_scalar(s1(), s1(), -1e4, 1e4, ALU.max, ALU.min))
                ins = v.tensor_copy(Uc(j), s1())
                inc("ve", ins)

            emit("vector", ve)
            prev_s0free = cnt["ve"]
            dc_war["scan%d" % (j % 2)] = cnt["ve"]

    def emit_l0_passC2():
        zm = {}
        for j in range(NCI):
            def po(g, j=j):
                if j < 2:
                    wait(g, "pe")
                else:
                    g.wait_ge(sem["pe"], zm[(j - 2, NTB - 1, "pe")])
                ins = g.dma_start(
                    out=ar(RING_OFF + (j % 2) * 1024, 1024)
                    .rearrange("p (c m) -> p c m", c=NCH),
                    in_=inp["l0_ip_w"].ap()[:, (NCI + j) * 128:(NCI + j + 1) * 128]
                    .rearrange("(c p) m -> p c m", p=128))
                zm[(j, "wl")] = inc("wla" if j % 2 == 0 else "wlb", ins, 16)

            emit("gpsimd", po)

            for bi, (t0, tw) in enumerate(TBS):
                bk = bi % 2
                sz = SZ_OFF + (bi % 2) * TB

                def pe(t, j=j, bi=bi, t0=t0, tw=tw, bk=bk):
                    if bi == 0:
                        t.wait_ge(sem["wla" if j % 2 == 0 else "wlb"],
                                  zm[(j, "wl")])
                        if j == 0:
                            wait(t, "ve")
                            wait(t, "ac")
                    prev = (j, bi - 2, "ac") if bi >= 2 else (j - 1, bi + 2, "ac")
                    if prev in zm:
                        t.wait_ge(sem["ac"], zm[prev])
                    for c in range(NCH):
                        ins = t.matmul(bank(bk, 0, 128, tw),
                                       ar(RING_OFF + (j % 2) * 1024, 1024)
                                       [:, c * 128:(c + 1) * 128],
                                       xnm(c, t0, tw),
                                       start=(c == 0), stop=(c == NCH - 1))
                    zm[(j, bi, "pe")] = inc("pe", ins)

                def ac(s, j=j, bi=bi, tw=tw, bk=bk, sz=sz):
                    s.wait_ge(sem["pe"], zm[(j, bi, "pe")])
                    prev = (j, bi - 2, "ve") if bi >= 2 else (j - 1, bi + 2, "ve")
                    if prev in zm:
                        s.wait_ge(sem["ve"], zm[prev])
                    ins = s.activation(ar(sz, tw), bank(bk, 0, 128, tw),
                                       AF.Silu, bias=0.0, scale=1.0)
                    zm[(j, bi, "ac")] = inc("ac", ins)

                def ve(v, j=j, bi=bi, t0=t0, tw=tw, sz=sz):
                    wait(v, "ac", zm[(j, bi, "ac")])
                    ins = v.tensor_mul(Uc(j)[:, t0:t0 + tw], Uc(j)[:, t0:t0 + tw],
                                       ar(sz, tw))
                    zm[(j, bi, "ve")] = inc("ve", ins)

                emit("tensor", pe)
                emit("scalar", ac)
                emit("vector", ve)

    def emit_l0_outproj():
        opm = {}
        cve = cnt["ve"]
        for bi, (t0, tw) in enumerate(TBS):
            for j in range(NCI):
                def po(g, bi=bi, j=j):
                    prev = (bi, j - 2) if j >= 2 else ((bi - 1, NCI - 2 + j)
                                                       if bi > 0 else None)
                    if prev and (prev[0], prev[1], "pe") in opm:
                        g.wait_ge(sem["pe"], opm[(prev[0], prev[1], "pe")])
                    elif prev is None:
                        wait(g, "pe")
                    ins = g.dma_start(
                        out=ar(RING_OFF + (j % 2) * 1024, 1024),
                        in_=inp["l0_op_w"].ap()[j * 128:(j + 1) * 128, :])
                    opm[(bi, j, "wl")] = inc("wla" if j % 2 == 0 else "wlb", ins, 16)

                def pe(t, bi=bi, j=j, t0=t0, tw=tw):
                    t.wait_ge(sem["wla" if j % 2 == 0 else "wlb"],
                              opm[(bi, j, "wl")])
                    if bi == 0 and j == 0:
                        wait(t, "ve", cve)
                    if j == 0 and bi > 0:
                        for dc in range(NCH):
                            t.wait_ge(sem["ve"], opm[(bi - 1, dc, "ev")])
                    for dc in range(NCH):
                        ins = t.matmul(bank(dc, 0, 128, tw),
                                       ar(RING_OFF + (j % 2) * 1024, 1024)
                                       [:, dc * 128:(dc + 1) * 128],
                                       Uc(j)[:, t0:t0 + tw],
                                       start=(j == 0), stop=(j == NCI - 1))
                    opm[(bi, j, "pe")] = inc("pe", ins)

                emit("gpsimd", po)
                emit("tensor", pe)

            def ve(v, bi=bi, t0=t0, tw=tw):
                v.wait_ge(sem["pe"], opm[(bi, NCI - 1, "pe")])
                for dc in range(NCH):
                    ins = v.tensor_add(xsm(dc, t0, tw), xsm(dc, t0, tw),
                                       bank(dc, 0, 128, tw))
                    opm[(bi, dc, "ev")] = inc("ve", ins)

            emit("vector", ve)

    def emit_l1_attn_pre():
        WD_OFF = 0
        DBAR_R = 8192
        A_R = 10240
        SZT = 12288
        A16_OFF = 12800
        B16_OFF = 14848
        PROW_OFF = 16896
        ZROW_OFF = 18944
        PC_OFF = 36864

        def Rhs_(o, n=TOK):
            return SCR[0:16, o:o + n]

        R_hs = SCR[0:16, 0:TOK]
        R_ct = SCR[0:16, TOK:2 * TOK]

        ve_at = cnt["ve"]

        def po(g):
            wait(g, "ve", ve_at)
            wait(g, "pe")
            inc("wl1", g.dma_start(
                out=ar(WD_OFF, NCH * 1024).rearrange("p (c o) -> p c o", c=NCH),
                in_=inp["l1_Wd"].ap().rearrange("(c p) o -> p c o", p=128)), 16)
            inc("wl", g.dma_start(
                out=smallw[:, 0:128].rearrange("p (c e) -> p c e", c=NCH),
                in_=inp["l1_Wb"].ap().rearrange("(c p) e -> p c e", p=128)), 16)
            inc("wl", g.dma_start(
                out=smallw[:, 128:256].rearrange("p (c e) -> p c e", c=NCH),
                in_=inp["l1_Wc"].ap().rearrange("(c p) e -> p c e", p=128)), 16)
            inc("wl", g.dma_start(
                out=smallw[:, 256:384].rearrange("p (c e) -> p c e", c=NCH),
                in_=inp["l1_A"].ap().rearrange("(c p) e -> p c e", p=128)), 16)

        emit("gpsimd", po)
        wl1_wd = cnt["wl1"]
        wl_small = cnt["wl"]

        def vz(v):
            inc("ve", v.memset(AR[0:1, ZROW_OFF:ZROW_OFF + TOK], 0.0))

        emit("vector", vz)

        sa_pe = {}

        def pesa(t):
            t.wait_ge(sem["wl"], wl_small)
            wait(t, "ve")
            wait(t, "ac")
            for c in range(NCH):
                ins = t.matmul(bank(3, 0, 16, 1), smallw[:, 256 + 16 * c:272 + 16 * c],
                               ones_col[:, :], start=(c == 0), stop=(c == NCH - 1))
            sa_pe[0] = inc("pe", ins)

        def acsa(s):
            s.wait_ge(sem["pe"], sa_pe[0])
            ins = s.activation(l1H[:, 1:2], bank(3, 0, 16, 1), AF.Sigmoid,
                               bias=0.0, scale=1.0 / DM)
            inc("ac", ins)

        emit("tensor", pesa)
        emit("scalar", acsa)
        sa_ac = cnt["ac"]

        dm = {}
        for bi, (t0, tw) in enumerate(TBS):
            for dc in range(NCH):
                bk = dc % 2

                def pe(t, bi=bi, dc=dc, t0=t0, tw=tw, bk=bk):
                    if bi == 0 and dc == 0:
                        t.wait_ge(sem["wl1"], wl1_wd)
                    prev = dm.get((bi, dc - 2, "ac")) or \
                        (dm.get((bi - 1, NCH - 2 + dc, "ac")) if bi > 0 else None)
                    if prev:
                        t.wait_ge(sem["ac"], prev)
                    for c in range(NCH):
                        ins = t.matmul(bank(bk, 0, 128, tw),
                                       AR[:, WD_OFF + c * 1024 + dc * 128:
                                          WD_OFF + c * 1024 + dc * 128 + 128],
                                       xnm(c, t0, tw),
                                       start=(c == 0), stop=(c == NCH - 1))
                    dm[(bi, dc, "pe")] = inc("pe", ins)

                def ac(s, bi=bi, dc=dc, tw=tw, bk=bk):
                    s.wait_ge(sem["pe"], dm[(bi, dc, "pe")])
                    bd = colpack[:, 128 + dc:129 + dc]
                    ins = s.activation(SCR[:, 2048:2048 + tw], bank(bk, 0, 128, tw),
                                       AF.Exp, bias=bd, scale=1.0)
                    s.wait_ge(sem["ac"], inc("ac", ins))
                    ins = s.activation(AR[:, SZT:SZT + tw], SCR[:, 2048:2048 + tw],
                                       AF.Ln, bias=1.0, scale=1.0)
                    dm[(bi, dc, "ac")] = inc("ac", ins)

                def pe2(t, bi=bi, dc=dc, t0=t0, tw=tw):
                    t.wait_ge(sem["ac"], dm[(bi, dc, "ac")])
                    if dc == 0 and bi > 0:
                        t.wait_ge(sem["ac"], dm[(bi - 1, "dbar")])
                    ins = t.matmul(bank(2, 0, 1, tw), ones_col[:, :],
                                   AR[:, SZT:SZT + tw],
                                   start=(dc == 0), stop=(dc == NCH - 1))
                    dm[(bi, dc, "pe2")] = inc("pe", ins)

                emit("tensor", pe)
                emit("scalar", ac)
                emit("tensor", pe2)

            def acd(s, bi=bi, t0=t0, tw=tw):
                s.wait_ge(sem["pe"], dm[(bi, NCH - 1, "pe2")])
                ins = s.activation(AR[0:1, DBAR_R + t0:DBAR_R + t0 + tw],
                                   bank(2, 0, 1, tw), AF.Copy, bias=0.0,
                                   scale=1.0 / DM)
                dm[(bi, "dbar")] = inc("ac", ins)

            emit("scalar", acd)

        def va(v):
            wait(v, "ac")
            ins = v.tensor_scalar(AR[0:1, A_R:A_R + TOK],
                                  AR[0:1, DBAR_R:DBAR_R + TOK], -1.0, 1.0,
                                  ALU.mult, ALU.add)
            inc("ve", ins)

        emit("vector", va)

        bc_m = {}
        for which, woff, bcol, dst in [("b", 0, 144, R_hs), ("c", 128, 145, R_ct)]:
            for bi, (t0, tw) in enumerate(TBS):
                bk = 4 + bi % 2

                def pe(t, which=which, woff=woff, bi=bi, t0=t0, tw=tw, bk=bk):
                    prev = bc_m.get((which, bi - 2, "ac")) or \
                        (bc_m.get(("b", bi + 2, "ac")) if which == "c" else None)
                    if prev:
                        t.wait_ge(sem["ve"], prev)
                    for c in range(NCH):
                        ins = t.matmul(bank(bk, 0, 16, tw),
                                       smallw[:, woff + 16 * c:woff + 16 * c + 16],
                                       xnm(c, t0, tw),
                                       start=(c == 0), stop=(c == NCH - 1))
                    bc_m[(which, bi, "pe")] = inc("pe", ins)

                def vv(v, which=which, bi=bi, t0=t0, tw=tw, bk=bk, bcol=bcol,
                       dst=dst):
                    v.wait_ge(sem["pe"], bc_m[(which, bi, "pe")])
                    ins = v.tensor_scalar(dst[:, t0:t0 + tw], bank(bk, 0, 16, tw),
                                          colpack[0:16, bcol:bcol + 1], None,
                                          ALU.add)
                    bc_m[(which, bi, "ac")] = inc("ve", ins)

                emit("tensor", pe)
                emit("vector", vv)

        d16_m = {}
        for bi, (t0, tw) in enumerate(TBS):
            bk = 6 + bi % 2

            def pe(t, bi=bi, t0=t0, tw=tw, bk=bk):
                wait(t, "ac") if bi == 0 else None
                prev = d16_m.get((bi - 2, "ve"))
                if prev:
                    t.wait_ge(sem["ve"], prev)
                ins = t.matmul(bank(bk, 0, 16, tw), ones_row[0:1, 0:16],
                               AR[0:1, DBAR_R + t0:DBAR_R + t0 + tw],
                               start=True, stop=True)
                d16_m[(bi, "pe")] = inc("pe", ins)

            def ve(v, bi=bi, t0=t0, tw=tw, bk=bk):
                v.wait_ge(sem["pe"], d16_m[(bi, "pe")])
                if bi == 0:
                    wait(v, "ac")
                vsafe(v, v.tensor_scalar(
                    AR[0:16, A16_OFF + t0:A16_OFF + t0 + tw],
                    bank(bk, 0, 16, tw), -1.0, 1.0, ALU.mult, ALU.add))
                b16 = AR[0:16, B16_OFF + t0:B16_OFF + t0 + tw]
                vsafe(v, v.scalar_tensor_tensor(
                    b16, R_hs[:, t0:t0 + tw], l1H[:, 1:2], bank(bk, 0, 16, tw),
                    ALU.mult, ALU.mult))
                ins = v.engine_nop()
                d16_m[(bi, "ve")] = cnt["ve"]

            emit("tensor", pe)
            emit("vector", ve)

        def vsc(v):
            wait(v, "ac")
            wait(v, "ve")
            vsafe(v, v.tensor_tensor_scan(
                R_hs[:, :], AR[0:16, A16_OFF:A16_OFF + TOK],
                AR[0:16, B16_OFF:B16_OFF + TOK], zeros_col[0:16, 0:1],
                ALU.mult, ALU.add))
            vsafe(v, v.tensor_tensor_scan(
                AR[0:1, PROW_OFF:PROW_OFF + TOK], AR[0:1, A_R:A_R + TOK],
                AR[0:1, ZROW_OFF:ZROW_OFF + TOK], one_col[0:1, 0:1],
                ALU.mult, ALU.add))
            ins = v.tensor_copy(cc1pack[:, 0:1], R_hs[:, TOK - 1:TOK])
            inc("ve", ins)

        emit("vector", vsc)
        sc_ve = cnt["ve"]

        pc_m = {}
        for bi, (t0, tw) in enumerate(TBS):
            bk = 6 + bi % 2

            def pe(t, bi=bi, t0=t0, tw=tw, bk=bk):
                if bi == 0:
                    t.wait_ge(sem["ve"], sc_ve)
                prev = pc_m.get((bi - 2, "ve"))
                if prev:
                    t.wait_ge(sem["ve"], prev)
                ins = t.matmul(bank(bk, 0, 16, tw), ones_row[0:1, 0:16],
                               AR[0:1, PROW_OFF + t0:PROW_OFF + t0 + tw],
                               start=True, stop=True)
                pc_m[(bi, "pe")] = inc("pe", ins)

            def ve(v, bi=bi, t0=t0, tw=tw, bk=bk):
                v.wait_ge(sem["pe"], pc_m[(bi, "pe")])
                ins = v.tensor_copy(AR[0:16, PC_OFF + t0:PC_OFF + t0 + tw],
                                    bank(bk, 0, 16, tw))
                pc_m[(bi, "ve")] = inc("ve", ins)

            emit("tensor", pe)
            emit("vector", ve)

        def pct(t):
            wait(t, "ve")
            ins = t.matmul(bank(3, 0, 16, 1), ones_row[0:1, 0:16],
                           AR[0:1, PROW_OFF + TOK - 1:PROW_OFF + TOK],
                           start=True, stop=True)
            pc_m["ct"] = inc("pe", ins)

        def vct(v):
            v.wait_ge(sem["pe"], pc_m["ct"])
            ins = v.tensor_copy(cc1pack[:, 1:2], bank(3, 0, 16, 1))
            inc("ve", ins)

        emit("tensor", pct)
        emit("vector", vct)
        pack_ve = cnt["ve"]

        def po2(g):
            g.wait_ge(sem["ve"], pack_ve)
            inc("po", g.dma_start(out=cc1_in.ap(), in_=cc1pack[:, :]), 16)
            wait(g, "po")
            inc("cc", g.collective_compute(
                "AllGather", ALU.bypass, replica_groups=[list(range(NCORE))],
                ins=[cc1_in.ap().opt()], outs=[cc1_out.ap().opt()]), 1)

        emit("gpsimd", po2)

        def spagg(sync):
            wait(sync, "cc")
            ins = sync.dma_start(
                out=aggL1[:, :].rearrange("p (r j) -> p r j", r=NCORE),
                in_=cc1_out.ap().rearrange("(r p) j -> p r j", p=16))
            inc("ld", ins, 16)

        emit("sync", spagg)

    def emit_l1_attn_fin():
        PC_OFF = 36864
        YSCBF_OFF = 36864
        PRD_OFF = 38912
        R_hs = SCR[0:16, 0:TOK]
        R_ct = SCR[0:16, TOK:2 * TOK]

        def vcomb(v):
            wait(v, "ld")
            vsafe(v, v.memset(l1H[:, 0:1], 0.0))
            vsafe(v, v.memset(cc1pack[:, 0:1], 0.0))
            for r in range(NCORE):
                vsafe(v, v.scalar_tensor_tensor(l1H[:, 0:1], cc1pack[:, 0:1],
                                                onehot8[0:16, r:r + 1], l1H[:, 0:1],
                                                ALU.mult, ALU.add))
                if r < NCORE - 1:
                    vsafe(v, v.tensor_mul(cc1pack[:, 1:2],
                                          aggL1[:, 2 * r + 1:2 * r + 2],
                                          cc1pack[:, 0:1]))
                    vsafe(v, v.tensor_add(cc1pack[:, 0:1], cc1pack[:, 1:2],
                                          aggL1[:, 2 * r:2 * r + 1]))
            for bi, (t0, tw) in enumerate(TBS):
                vsafe(v, v.scalar_tensor_tensor(
                    R_hs[:, t0:t0 + tw], AR[0:16, PC_OFF + t0:PC_OFF + t0 + tw],
                    l1H[:, 0:1], R_hs[:, t0:t0 + tw], ALU.mult, ALU.add))
            ins = v.tensor_mul(AR[0:16, PRD_OFF:PRD_OFF + TOK], R_hs[:, :],
                               R_ct[:, :])
            inc("ve", ins)

        emit("vector", vcomb)
        prod_ve = cnt["ve"]

        ys_m = {}
        for bi, (t0, tw) in enumerate(TBS):
            def pe(t, bi=bi, t0=t0, tw=tw):
                if bi == 0:
                    t.wait_ge(sem["ve"], prod_ve)
                ins = t.matmul(bank(bi, 0, 1, tw), ones_col[0:16, 0:1],
                               AR[0:16, PRD_OFF + t0:PRD_OFF + t0 + tw],
                               start=True, stop=True)
                ys_m[(bi, "pe")] = inc("pe", ins)

            def ve(v, bi=bi, t0=t0, tw=tw):
                v.wait_ge(sem["pe"], ys_m[(bi, "pe")])
                ins = v.tensor_copy(AR[0:1, YSCBF_OFF + t0:YSCBF_OFF + t0 + tw],
                                    bank(bi, 0, 1, tw))
                ys_m[(bi, "ve")] = inc("ve", ins)

            emit("tensor", pe)
            emit("vector", ve)
        ysc_ve = cnt["ve"]

        yb_m = {}
        for bi, (t0, tw) in enumerate(TBS):
            bk = 4 + bi % 2

            def pe(t, bi=bi, t0=t0, tw=tw, bk=bk):
                if bi == 0:
                    t.wait_ge(sem["ve"], ysc_ve)
                prev = yb_m.get((bi - 2, "ve"))
                if prev:
                    t.wait_ge(sem["ve"], prev)
                ins = t.matmul(bank(bk, 0, 128, tw), ones_row[:, :],
                               AR[0:1, YSCBF_OFF + t0:YSCBF_OFF + t0 + tw],
                               start=True, stop=True)
                yb_m[(bi, "pe")] = inc("pe", ins)

            def ve(v, bi=bi, t0=t0, tw=tw, bk=bk):
                v.wait_ge(sem["pe"], yb_m[(bi, "pe")])
                ins = v.tensor_copy(SCR[:, t0:t0 + tw], bank(bk, 0, 128, tw))
                yb_m[(bi, "ve")] = inc("ve", ins)

            emit("tensor", pe)
            emit("vector", ve)

        def po(g):
            wait(g, "pe")
            wait(g, "ve")
            inc("wl1", g.dma_start(
                out=ar(0, NCH * 1024).rearrange("p (c o) -> p c o", c=NCH),
                in_=inp["l1_Wo"].ap().rearrange("(c p) o -> p c o", p=128)), 16)

        emit("gpsimd", po)
        wo_wl = cnt["wl1"]

        wo_m = {}
        for bi, (t0, tw) in enumerate(TBS):
            def pe(t, bi=bi, t0=t0, tw=tw):
                if bi == 0:
                    t.wait_ge(sem["wl1"], wo_wl)
                    wait(t, "ve")
                if bi > 0:
                    for dc in range(NCH):
                        t.wait_ge(sem["ve"], wo_m[(bi - 1, "ev")])
                        break
                for dj in range(NCH):
                    for dc in range(NCH):
                        ins = t.matmul(bank(dc, 0, 128, tw),
                                       AR[:, dj * 1024 + dc * 128:
                                          dj * 1024 + dc * 128 + 128],
                                       xnm(dj, t0, tw),
                                       start=(dj == 0), stop=(dj == NCH - 1))
                wo_m[(bi, "pe")] = inc("pe", ins)

            def ve(v, bi=bi, t0=t0, tw=tw):
                v.wait_ge(sem["pe"], wo_m[(bi, "pe")])
                for dc in range(NCH):
                    vsafe(v, v.tensor_mul(SCR[:, 2048:2048 + tw],
                                          bank(dc, 0, 128, tw), SCR[:, t0:t0 + tw]))
                    bo = colpack[:, 136 + dc:137 + dc]
                    ins = v.scalar_tensor_tensor(xsm(dc, t0, tw),
                                                 SCR[:, 2048:2048 + tw], bo,
                                                 xsm(dc, t0, tw), ALU.add, ALU.add)
                    if dc < NCH - 1:
                        vsafe(v, ins)
                wo_m[(bi, "ev")] = inc("ve", ins)

            emit("tensor", pe)
            emit("vector", ve)

    def emit_store():
        def sp(sync):
            wait(sync, "ve")
            wait(sync, "ac")
            wait(sync, "pe")
            for c in range(NCH):
                inc("st", sync.dma_start(out=out_x.ap()[c * 128:(c + 1) * 128, :],
                                         in_=xsm(c)), 16)
            for name, (ap_, od) in dbg_outs.items():
                inc("st", sync.dma_start(out=od.ap(), in_=ap_), 16)
            sync.wait_ge(sem["st"], cnt["st"])

        emit("sync", sp)

    with nc.Block() as block:
        bh["block"] = block
        emit("sync", ph_loads)
        emit("gpsimd", ph_wloads_early)
        emit("scalar", ph_colprep_ac)
        emit("vector", ph_colprep_ve)

        if do_l0:
            with nc.named_scope("ln0"):
                emit_layernorm(full=True)
            dbg_add("xn0", xn[:, :], (128, NCH * TOKH), BF16)
            if globals()["DBG_STOP"] != "ln":
                with nc.named_scope("rt0"):
                    emit_routing(0)
                dbg_add("comb0", comb[:, :], (128, 256))
            if globals()["DBG_STOP"] not in ("ln", "rt"):
                with nc.named_scope("moe0"):
                    emit_moe(0)
                dbg_snap("x_moe0", x_res[:, :], (128, NCH * TOKH))
        if do_l0 and globals()["DBG_STOP"] is None:
            with nc.named_scope("passA"):
                emit_l0_passA()
            dbg_add("dtlow", AR[0:64, DTL_OFF:DTL_OFF + TOK], (64, TOK), BF16)
            dbg_snap("xi", AR[:, U_OFF:U_OFF + NCI * TOK], (128, NCI * TOK), BF16)
            with nc.named_scope("prepass"):
                emit_l0_prepass()
            dbg_add("packF", packF[:, :], (128, 16))
            dbg_add("packP", packP[:, :], (128, 16))
            dbg_add("S0", S0[:, :], (128, 16))
            with nc.named_scope("passC1"):
                emit_l0_passC1()
            dbg_add("y0", AR[:, U_OFF:U_OFF + NCI * TOK], (128, NCI * TOK), BF16)
            with nc.named_scope("passC2"):
                emit_l0_passC2()
            with nc.named_scope("outproj"):
                emit_l0_outproj()
            dbg_snap("x_l0s", x_res[:, :], (128, NCH * TOKH))

        if do_l1 and globals()["DBG_STOP"] is None:
            with nc.named_scope("ln1"):
                emit_layernorm(full=False)
            dbg_snap("xn1", xn[:, :], (128, NCH * TOKH), BF16)
            with nc.named_scope("rt1"):
                emit_routing(1)
            dbg_add("comb1", comb[:, :], (128, 256))
            with nc.named_scope("attn1pre"):
                emit_l1_attn_pre()
            with nc.named_scope("moe1"):
                emit_moe(1)
            dbg_snap("x_moe1", x_res[:, :], (128, NCH * TOKH))
            with nc.named_scope("attn1fin"):
                emit_l1_attn_fin()

        emit_store()

    es.close()
    return nc, dbg_outs


_CACHE = {}


def _prep_const():
    c = {}
    c["identity"] = np.eye(128, dtype=np.float32)
    c["ones_col_bf"] = np.ones((128, 1), dtype=ml_dtypes.bfloat16)
    c["ones_row_bf"] = np.ones((1, 128), dtype=ml_dtypes.bfloat16)
    c["zeros_col"] = np.zeros((128, 1), dtype=np.float32)
    c["one_col"] = np.ones((128, 1), dtype=np.float32)
    c["iota8eps"] = np.broadcast_to((np.arange(8) * 1e-6).astype(np.float32),
                                    (128, 8)).copy()
    c["iota16eps"] = np.broadcast_to((np.arange(16) * 1e-6).astype(np.float32),
                                     (128, 16)).copy()
    c["iota4eps"] = np.broadcast_to((np.arange(4) * 1e-6).astype(np.float32),
                                    (128, 4)).copy()
    c["neg1p"] = np.broadcast_to((-1.0 - np.arange(16) * 1e-6).astype(np.float32),
                                 (128, 16)).copy()
    return c


def _colpack(inputs, do_l0, do_l1):
    cp = np.zeros((128, 160), dtype=np.float32)
    if do_l0:
        cp[:, 0:16] = inputs["L0_attn_A_log"].reshape(16, 128).T
        cp[:, 16:32] = inputs["L0_attn_D"].reshape(16, 128).T
        cp[:, 32:48] = inputs["L0_attn_conv_b"].reshape(16, 128).T
        cp[:, 48:64] = inputs["L0_attn_dt_proj_b"].reshape(16, 128).T
        cw = inputs["L0_attn_conv_w"].reshape(2048, 4)
        for k in range(4):
            cp[:, 64 + 16 * k:64 + 16 * (k + 1)] = cw[:, k].reshape(16, 128).T
    if do_l1:
        cp[:, 128:136] = inputs["L1_attn_bd"].reshape(8, 128).T
        cp[:, 136:144] = inputs["L1_attn_bo"].reshape(8, 128).T
        cp[0:16, 144] = inputs["L1_attn_bb"]
        cp[0:16, 145] = inputs["L1_attn_bc"]
    return cp


def make_in_maps(inputs, do_l0=True, do_l1=True, x_mid=None):
    const = _prep_const()
    shared = dict(const)
    shared["colpack_in"] = _colpack(inputs, do_l0, do_l1)
    if do_l0:
        shared["l0_ip_w"] = np.ascontiguousarray(inputs["L0_attn_in_proj_w"].T)
        shared["l0_xp_w"] = np.ascontiguousarray(inputs["L0_attn_x_proj_w"].T)
        shared["l0_dt_w"] = np.ascontiguousarray(inputs["L0_attn_dt_proj_w"].T)
        shared["l0_op_w"] = np.ascontiguousarray(inputs["L0_attn_out_proj_w"].T)
        shared["l0_gate_w"] = np.ascontiguousarray(inputs["L0_ffn_gate_w"].T)
        shared["l0_bbias_bc"] = np.broadcast_to(
            inputs["L0_ffn_balance_bias"], (128, 8)).copy()
        shared["l0_w1"] = np.ascontiguousarray(inputs["L0_ffn_w1"].transpose(0, 2, 1))
        shared["l0_w2"] = np.ascontiguousarray(inputs["L0_ffn_w2"].transpose(0, 2, 1))
    if do_l1:
        shared["l1_Wd"] = np.ascontiguousarray(inputs["L1_attn_Wd"])
        shared["l1_Wb"] = np.ascontiguousarray(inputs["L1_attn_Wb"])
        shared["l1_Wc"] = np.ascontiguousarray(inputs["L1_attn_Wc"])
        shared["l1_Wo"] = np.ascontiguousarray(inputs["L1_attn_Wo"])
        shared["l1_A"] = np.ascontiguousarray(inputs["L1_attn_A"])
        shared["l1_router_w"] = np.ascontiguousarray(inputs["L1_ffn_router_w"].T)
        shared["l1_rbias_bc"] = np.broadcast_to(inputs["L1_ffn_bias"],
                                                (128, 16)).copy()
        shared["l1_gate_w"] = np.ascontiguousarray(inputs["L1_ffn_gate_w"])
        shared["l1_up_w"] = np.ascontiguousarray(inputs["L1_ffn_up_w"])
        shared["l1_down_w"] = np.ascontiguousarray(inputs["L1_ffn_down_w"])
        shared["l1_shg_w"] = np.ascontiguousarray(inputs["L1_ffn_sh_gate_w"])
        shared["l1_shu_w"] = np.ascontiguousarray(inputs["L1_ffn_sh_up_w"])
        shared["l1_shd_w"] = np.ascontiguousarray(inputs["L1_ffn_sh_down_w"])

    if do_l0:
        x = np.ascontiguousarray(inputs["L0_attn_x"][0].T)
        xpad = np.concatenate([np.zeros((DM, HALO), np.float32), x], axis=1)

    in_maps = []
    for c in range(NCORE):
        m = dict(shared)
        oh = np.zeros((128, 8), np.float32)
        oh[:, c] = 1.0
        m["onehot8"] = oh
        if do_l0:
            m["x_in"] = np.ascontiguousarray(xpad[:, c * TOK:c * TOK + TOKH])
        else:
            m["x_mid_in"] = np.ascontiguousarray(x_mid[c])
        in_maps.append(m)
    return in_maps


def assemble_out(res):
    full = np.concatenate([res.results[c]["out_x"] for c in range(NCORE)], axis=1)
    return np.ascontiguousarray(full.T)[None]


def kernel(**inputs):
    inputs = {k: np.asarray(v) for k, v in inputs.items()}
    do_l1 = DO_L1
    key = (True, do_l1)
    if key not in _CACHE:
        _CACHE[key] = build_program(True, do_l1)
    nc, _ = _CACHE[key]
    in_maps = make_in_maps(inputs, True, do_l1)
    res = run_bass_kernel_spmd(nc, in_maps, core_ids=list(range(NCORE)))
    return assemble_out(res)


# version 05
# speedup vs baseline: 1.0859x; 1.0859x over previous
import numpy as np
import ml_dtypes
from contextlib import ExitStack

import concourse.bass as bass
import concourse.mybir as mybir
from concourse.bass_utils import run_bass_kernel_spmd

F32 = mybir.dt.float32
BF16 = mybir.dt.bfloat16
AF = mybir.ActivationFunctionType
ALU = mybir.AluOpType
AX = mybir.AxisListType

NCORE = 8
DM = 1024
TOK = 2048
HALO = 3
TOKH = TOK + HALO
NCH = DM // 128
DI = 2048
NCI = DI // 128
DT = 64
TB = 512
NTB = TOK // TB
TBS = [(i * TB, TB) for i in range(NTB)]
TBS_H = TBS + [(1540, 511)]
BIG = 1.0e30
EPS_LN = 1e-5

L0_NE, L0_DH = 8, 2048
DBG_NEX = None
DBG_NOSCALE = False
DBG_ALPHA_C = None
DBG_STOP = None
L1_NE, L1_DH = 16, 1024

DO_L1 = True


def build_program(do_l0=True, do_l1=True, dbg=()):
    nc = bass.Bass("TRN2", target_bir_lowering=False, debug=False, num_devices=NCORE)
    dbg = set(dbg)
    es = ExitStack()

    sem = {}
    cnt = {}
    for nm in ["ld", "wl", "wla", "wlb", "wl1", "wlu", "wl2",
               "pe", "ve", "ac", "po", "cc", "st"]:
        sem[nm] = es.enter_context(nc.semaphore(nm))
        cnt[nm] = 0

    def inc(name, inst, k=1):
        inst.then_inc(sem[name], k)
        cnt[name] += k
        return cnt[name]

    def wait(eng, name, val=None):
        eng.wait_ge(sem[name], cnt[name] if val is None else val)

    def vsafe(v, ins):
        v.wait_ge(sem["ve"], inc("ve", ins))

    inp = {}

    def dram_in(name, shape, dtype=F32):
        inp[name] = nc.dram_tensor(name, list(shape), dtype, kind="ExternalInput")
        return inp[name]

    if do_l0:
        dram_in("x_in", (DM, TOKH))
    else:
        dram_in("x_mid_in", (DM, TOK))
    out_x = nc.dram_tensor("out_x", [DM, TOK], F32, kind="ExternalOutput")

    dram_in("identity", (128, 128))
    dram_in("ones_col_bf", (128, 1), BF16)
    dram_in("ones_row_bf", (1, 128), BF16)
    dram_in("zeros_col", (128, 1))
    dram_in("one_col", (128, 1))
    dram_in("onehot8", (128, 8))
    dram_in("iota8eps", (128, 8))
    dram_in("iota16eps", (128, 16))
    dram_in("iota4eps", (128, 4))
    dram_in("neg1p", (128, 16))
    dram_in("colpack_in", (128, 160))

    if do_l0:
        dram_in("l0_ip_w", (DM, 2 * DI))
        dram_in("l0_xp_w", (DI, DT))
        dram_in("l0_dt_w", (DT, DI))
        dram_in("l0_op_w", (DI, DM))
        dram_in("l0_gate_w", (DM, L0_NE))
        dram_in("l0_bbias_bc", (128, L0_NE))
        dram_in("l0_w1", (L0_NE, DM, L0_DH))
        dram_in("l0_w2", (L0_NE, L0_DH, DM))
        cc0_in = nc.dram_tensor("cc0_in", [128, 32], F32, kind="Internal")
        cc0_out = nc.dram_tensor("cc0_out", [128 * NCORE, 32], F32, kind="Internal",
                                 addr_space="Local" if globals()["DBG_STOP"]
                                 else "Shared")
    if do_l1:
        dram_in("l1_Wd", (DM, DM))
        dram_in("l1_Wb", (DM, 16))
        dram_in("l1_Wc", (DM, 16))
        dram_in("l1_Wo", (DM, DM))
        dram_in("l1_A", (DM, 16))
        dram_in("l1_router_w", (DM, L1_NE))
        dram_in("l1_rbias_bc", (128, L1_NE))
        dram_in("l1_gate_w", (L1_NE, DM, L1_DH))
        dram_in("l1_up_w", (L1_NE, DM, L1_DH))
        dram_in("l1_down_w", (L1_NE, L1_DH, DM))
        dram_in("l1_shg_w", (DM, 2 * L1_DH))
        dram_in("l1_shu_w", (DM, 2 * L1_DH))
        dram_in("l1_shd_w", (2 * L1_DH, DM))
        cc1_in = nc.dram_tensor("cc1_in", [16, 2], F32, kind="Internal")
        cc1_out = nc.dram_tensor("cc1_out", [16 * NCORE, 2], F32, kind="Internal",
                                 addr_space="Shared")
    combT_d = nc.dram_tensor("combT_d", [16, TOK], F32, kind="Internal")

    dbg_outs = {}

    def dbg_add(name, ap_, shape, dtype=F32):
        if name in dbg:
            od = nc.dram_tensor("dbg_" + name, list(shape), dtype,
                                kind="ExternalOutput")
            dbg_outs[name] = (ap_, od)

    def dbg_snap(name, ap_, shape, dtype=F32):
        if name not in dbg:
            return
        od = nc.dram_tensor("dbg_" + name, list(shape), dtype,
                            kind="ExternalOutput")

        def sp(sync):
            wait(sync, "ve")
            wait(sync, "ac")
            wait(sync, "pe")
            inc("st", sync.dma_start(out=od.ap(), in_=ap_), 16)

        emit("sync", sp)
        stv = cnt["st"]
        emit("vector", lambda v: v.wait_ge(sem["st"], stv))
        emit("scalar", lambda s: s.wait_ge(sem["st"], stv))
        emit("tensor", lambda t: t.wait_ge(sem["st"], stv))
        emit("gpsimd", lambda g: g.wait_ge(sem["st"], stv))

    def sbt(name, shape, dtype):
        return es.enter_context(nc.sbuf_tensor(name, shape, dtype))

    x_res = sbt("x_res", [128, NCH * TOKH], F32)
    xn = sbt("xn", [128, NCH * TOKH], BF16)
    AR = sbt("AR", [128, 40960], BF16)
    SCR = sbt("SCR", [128, 4096], F32)
    ident = sbt("identity_sb", [128, 128], F32)
    ones_col = sbt("ones_col_sb", [128, 1], BF16)
    ones_row = sbt("ones_row_sb", [1, 128], BF16)
    zeros_col = sbt("zeros_col_sb", [128, 1], F32)
    one_col = sbt("one_col_sb", [128, 1], F32)
    onehot8 = sbt("onehot8_sb", [128, 8], F32)
    iota8 = sbt("iota8_sb", [128, 8], F32)
    iota16 = sbt("iota16_sb", [128, 16], F32)
    iota4 = sbt("iota4_sb", [128, 4], F32)
    neg1p = sbt("neg1p_sb", [128, 16], F32)
    colpack = sbt("colpack", [128, 160], F32)
    packF = sbt("packF", [128, 16], F32)
    packP = sbt("packP", [128, 16], F32)
    packR = sbt("packR", [128, 16], F32)
    S0 = sbt("S0", [128, 16], F32)
    agg = sbt("agg", [128, 32 * NCORE], F32)
    bias_bc = sbt("bias_bc", [128, 16], F32)
    comb = sbt("comb", [128, 16 * 16], F32)
    combT = sbt("combT", [16, 128], F32)
    cbc = sbt("cbc", [128, TOK], BF16)
    crow_bf = sbt("crow_bf", [1, TOK], BF16)
    aggL1 = sbt("aggL1", [16, 16], F32) if do_l1 else None
    cc1pack = sbt("cc1pack", [16, 2], F32) if do_l1 else None
    gw_sb = sbt("gw_sb", [128, NCH * 16], BF16)
    smallw = sbt("smallw", [128, 384], BF16) if do_l1 else None
    l1H = sbt("l1H", [16, 2], F32) if do_l1 else None

    psum = es.enter_context(nc.psum_tensor("psum", [128, 4096], F32))

    def bank(i, p0=0, np_=128, w=512):
        return psum[p0:p0 + np_, i * 512:i * 512 + w]

    U_OFF = 0
    DTL_OFF = 32768
    RING_OFF = 34816
    SZ_OFF = 38912
    CBT_OFF = 39936

    def Uc(j):
        return AR[:, U_OFF + j * TOK:U_OFF + (j + 1) * TOK]

    def ar(off, n):
        return AR[:, off:off + n]

    XP_OFF = 38912

    def xp_sb(j):
        return AR[:, XP_OFF + j * DT:XP_OFF + (j + 1) * DT]

    ROWS_OFF = 34816

    def row_b(i, n=TOKH, o=0):
        return AR[0:1, ROWS_OFF + i * TOKH + o:ROWS_OFF + i * TOKH + o + n]

    def s0(o=0, n=TOK):
        return SCR[:, o:o + n]

    def s1(o=0, n=TOK):
        return SCR[:, TOK + o:TOK + o + n]

    def sxz(o=0, n=TOKH):
        return SCR[:, o:o + n]

    def sconv(o=0, n=TOK):
        return SCR[:, o:o + n]



    def xs(c, o=0, n=TOKH):
        return x_res[:, c * TOKH + o:c * TOKH + o + n]

    def xsm(c, o=0, n=TOK):
        return x_res[:, c * TOKH + HALO + o:c * TOKH + HALO + o + n]

    def xnf(c, o=0, n=TOKH):
        return xn[:, c * TOKH + o:c * TOKH + o + n]

    def xnm(c, o=0, n=TOK):
        return xn[:, c * TOKH + HALO + o:c * TOKH + HALO + o + n]

    bh = {}

    _selfsem = {"vector": "ve", "scalar": "ac"}

    def emit(engine, fn):
        nm = _selfsem.get(engine)
        if nm is None:
            getattr(bh["block"], engine)(fn)
            return

        def wrapped(eng):
            eng.wait_ge(sem[nm], cnt[nm])
            fn(eng)

        getattr(bh["block"], engine)(wrapped)


    def ph_loads(sync):
        if do_l0:
            i0 = inp["x_in"].ap().rearrange("(c p) t -> p c t", p=128)
            d0 = x_res[:, :].rearrange("p (c t) -> p c t", c=NCH)
            inc("ld", sync.dma_start(out=d0, in_=i0), 16)
        else:
            for c in range(NCH):
                i0 = inp["x_mid_in"].ap()[c * 128:(c + 1) * 128, :]
                inc("ld", sync.dma_start(out=xsm(c), in_=i0), 16)
        for nm, dst in [("identity", ident), ("ones_col_bf", ones_col),
                        ("ones_row_bf", ones_row), ("zeros_col", zeros_col),
                        ("one_col", one_col), ("onehot8", onehot8),
                        ("iota8eps", iota8), ("iota16eps", iota16),
                        ("iota4eps", iota4), ("neg1p", neg1p),
                        ("colpack_in", colpack)]:
            inc("ld", sync.dma_start(out=dst[:, :], in_=inp[nm].ap()), 16)

    def ph_wloads_early(g):
        if do_l0:
            inc("wl", g.dma_start(
                out=gw_sb[:, 0:NCH * 8].rearrange("p (c e) -> p c e", c=NCH),
                in_=inp["l0_gate_w"].ap().rearrange("(c p) e -> p c e", p=128)), 16)

    def ph_colprep_ac(s):
        wait(s, "ld")
        inc("ac", s.activation(colpack[:, 0:16], colpack[:, 0:16], AF.Exp,
                               bias=0.0, scale=1.0))

    def ph_colprep_ve(v):
        wait(v, "ac")
        vsafe(v, v.tensor_scalar(colpack[:, 0:16], colpack[:, 0:16], -1.0, None,
                                 ALU.mult))
        inc("ve", v.tensor_scalar(colpack[:, 0:16], colpack[:, 0:16], -10.0, -1e-6,
                                  ALU.max, ALU.min))

    def emit_layernorm(full):
        tbs = TBS_H if full else TBS
        tokh = TOKH if full else TOK
        off = 0 if full else HALO
        SQ = 16408

        def xbf(c):
            return AR[:, c * tokh:(c + 1) * tokh]

        def sqbf(c):
            return AR[:, SQ + c * tokh:SQ + (c + 1) * tokh]

        def ve0(v):
            wait(v, "ld")
            wait(v, "ve")
            wait(v, "pe")
            for c in range(NCH):
                ins = v.tensor_copy(xbf(c), xs(c, off, tokh))
            inc("ve", ins)

        def ac0(s):
            wait(s, "ld")
            wait(s, "pe")
            wait(s, "ve")
            for c in range(NCH):
                ins = s.activation(sqbf(c), xs(c, off, tokh), AF.Square,
                                   bias=0.0, scale=1.0)
            inc("ac", ins)

        emit("vector", ve0)
        emit("scalar", ac0)
        ve_in, ac_in = cnt["ve"], cnt["ac"]

        pe_marks, ve_marks = [], []
        for bi, (t0, tw) in enumerate(tbs):
            bs, bq = bi % 2, 2 + bi % 2

            def pe(t, bi=bi, t0=t0, tw=tw, bs=bs, bq=bq):
                if bi == 0:
                    t.wait_ge(sem["ve"], ve_in)
                    t.wait_ge(sem["ac"], ac_in)
                if bi >= 2:
                    t.wait_ge(sem["ve"], ve_marks[bi - 2])
                for c in range(NCH):
                    t.matmul(bank(bs, 0, 1, tw), ones_col[:, :], xbf(c)[:, t0:t0 + tw],
                             start=(c == 0), stop=(c == NCH - 1))
                for c in range(NCH):
                    ins = t.matmul(bank(bq, 0, 1, tw), ones_col[:, :],
                                   sqbf(c)[:, t0:t0 + tw],
                                   start=(c == 0), stop=(c == NCH - 1))
                pe_marks.append(inc("pe", ins))

            def ve(v, bi=bi, t0=t0, tw=tw, bs=bs, bq=bq):
                v.wait_ge(sem["pe"], pe_marks[bi])
                m = row_b(0, tw, t0)
                q = row_b(1, tw, t0)
                t2 = SCR[0:1, 2048:2048 + tw]
                vsafe(v, v.tensor_scalar(m, bank(bs, 0, 1, tw), 1.0 / DM, None,
                                         ALU.mult))
                vsafe(v, v.tensor_scalar(q, bank(bq, 0, 1, tw), 1.0 / DM, None,
                                         ALU.mult))
                vsafe(v, v.tensor_mul(t2, m, m))
                vsafe(v, v.tensor_sub(q, q, t2))
                ins = v.tensor_scalar(q, q, EPS_LN, None, ALU.add)
                ve_marks.append(inc("ve", ins))

            emit("tensor", pe)
            emit("vector", ve)

        def ac1(s):
            wait(s, "ve")
            ins = s.activation(row_b(1, tokh), row_b(1, tokh), AF.Ln,
                               bias=0.0, scale=1.0)
            s.wait_ge(sem["ac"], inc("ac", ins))
            inc("ac", s.activation(row_b(1, tokh), row_b(1, tokh), AF.Exp,
                                   bias=0.0, scale=-0.5))

        emit("scalar", ac1)
        ac_rows = cnt["ac"]

        bc_marks, nrm_marks = [], []
        for bi, (t0, tw) in enumerate(tbs):
            bm, br = 4 + (bi % 2) * 2, 5 + (bi % 2) * 2

            def pe(t, bi=bi, t0=t0, tw=tw, bm=bm, br=br):
                if bi == 0:
                    t.wait_ge(sem["ac"], ac_rows)
                if bi >= 2:
                    t.wait_ge(sem["ve"], nrm_marks[bi - 2])
                t.matmul(bank(bm, 0, 128, tw), ones_row[:, :], row_b(0, tw, t0),
                         start=True, stop=True)
                ins = t.matmul(bank(br, 0, 128, tw), ones_row[:, :], row_b(1, tw, t0),
                               start=True, stop=True)
                bc_marks.append(inc("pe", ins))

            def ve(v, bi=bi, t0=t0, tw=tw, bm=bm, br=br):
                v.wait_ge(sem["pe"], bc_marks[bi])
                for c in range(NCH):
                    vsafe(v, v.tensor_sub(s0(0, tw), xs(c, off + t0, tw),
                                          bank(bm, 0, 128, tw)))
                    ins = v.tensor_mul(
                        xn[:, c * TOKH + off + t0:c * TOKH + off + t0 + tw],
                        s0(0, tw), bank(br, 0, 128, tw))
                    if c < NCH - 1:
                        vsafe(v, ins)
                nrm_marks.append(inc("ve", ins))

            emit("tensor", pe)
            emit("vector", ve)

    def emit_routing(layer):
        ne = L0_NE if layer == 0 else L1_NE
        bname = "l0_bbias_bc" if layer == 0 else "l1_rbias_bc"
        gwc = 8 if layer == 0 else 16
        ln_ve = cnt["ve"]

        def po(g):
            wait(g, "ve", ln_ve)
            if layer == 1:
                inc("wl", g.dma_start(
                    out=gw_sb[:, 0:NCH * 16].rearrange("p (c e) -> p c e", c=NCH),
                    in_=inp["l1_router_w"].ap().rearrange("(c p) e -> p c e", p=128)),
                    16)
            inc("wl", g.dma_start(out=bias_bc[:, 0:ne], in_=inp[bname].ap()), 16)

        emit("gpsimd", po)
        wl_done = cnt["wl"]

        def rb(tt, o, n):
            return SCR[:, tt * 256 + o:tt * 256 + o + n]

        pe_m, ve1_m, ac_m = {}, {}, {}
        for tt in range(16):
            bk = tt % 2

            def pe(t, tt=tt, bk=bk):
                if tt == 0:
                    t.wait_ge(sem["wl"], wl_done)
                    t.wait_ge(sem["ve"], ln_ve)
                if tt >= 2:
                    t.wait_ge(sem["ve"], ve1_m[tt - 2])
                for c in range(NCH):
                    ins = t.matmul(bank(bk, 0, 128, ne), xnm(c, tt * 128, 128),
                                   gw_sb[:, c * gwc:c * gwc + ne],
                                   start=(c == 0), stop=(c == NCH - 1))
                pe_m[tt] = inc("pe", ins)

            def ve1(v, tt=tt, bk=bk):
                v.wait_ge(sem["pe"], pe_m[tt])
                if layer == 0:
                    ins = v.tensor_add(rb(tt, 0, ne), bank(bk, 0, 128, ne),
                                       bias_bc[:, 0:ne])
                else:
                    ins = v.tensor_copy(rb(tt, 0, ne), bank(bk, 0, 128, ne))
                ve1_m[tt] = inc("ve", ins)

            def ac1(s, tt=tt):
                s.wait_ge(sem["ve"], ve1_m[tt])
                ins = s.activation(rb(tt, 0, ne), rb(tt, 0, ne), AF.Sigmoid,
                                   bias=0.0, scale=1.0)
                ac_m[tt] = inc("ac", ins)

            emit("tensor", pe)
            emit("vector", ve1)
            emit("scalar", ac1)

        def ve2(v):
            wait(v, "ac", ac_m[15])
            T16 = range(16)

            def sc(tt):
                return rb(tt, 0, ne)

            def each(f):
                ins = None
                for tt in T16:
                    ins = f(tt)
                return ins

            def step(f):
                ins = each(f)
                v.wait_ge(sem["ve"], inc("ve", ins))
                return ins

            if layer == 0:
                sp = lambda tt: rb(tt, 32, ne)
                u = lambda tt: rb(tt, 64, ne)
                m0 = lambda tt: rb(tt, 96, 1)
                m0b = lambda tt: rb(tt, 97, 1)
                den = lambda tt: rb(tt, 98, 1)
                rec = lambda tt: rb(tt, 99, 1)
                step(lambda tt: v.tensor_sub(sp(tt), sc(tt), iota8[:, 0:ne]))
                step(lambda tt: v.tensor_reduce(
                    m0(tt), sp(tt).rearrange("p (a e) -> p a e", a=1), AX.X, ALU.max))
                step(lambda tt: v.tensor_single_scalar(u(tt), sp(tt), m0(tt),
                                                       ALU.is_ge))
                step(lambda tt: v.scalar_tensor_tensor(sp(tt), u(tt), -BIG, sp(tt),
                                                       ALU.mult, ALU.add))
                step(lambda tt: v.tensor_reduce(
                    m0b(tt), sp(tt).rearrange("p (a e) -> p a e", a=1), AX.X, ALU.max))
                step(lambda tt: v.tensor_single_scalar(sp(tt), sp(tt), m0b(tt),
                                                       ALU.is_ge))
                step(lambda tt: v.tensor_add(u(tt), u(tt), sp(tt)))
                step(lambda tt: v.tensor_mul(sp(tt), sc(tt), u(tt)))
                step(lambda tt: v.tensor_reduce(
                    den(tt), sp(tt).rearrange("p (a e) -> p a e", a=1), AX.X, ALU.add))
                step(lambda tt: v.tensor_scalar(den(tt), den(tt), 1e-6, None,
                                                ALU.add))
                step(lambda tt: v.reciprocal(rec(tt), den(tt)))
                ins = each(lambda tt: v.tensor_single_scalar(
                    comb[:, tt * 16:tt * 16 + ne], sp(tt), rec(tt), ALU.mult))
            else:
                ss = lambda tt: rb(tt, 32, ne)
                u = lambda tt: rb(tt, 64, ne)
                ge = lambda tt: rb(tt, 96, ne)
                wk = lambda tt: rb(tt, 128, ne)
                g1 = lambda tt: rb(tt, 160, 4)
                g2 = lambda tt: rb(tt, 168, 4)
                g3 = lambda tt: rb(tt, 176, 4)
                m0 = lambda tt: rb(tt, 184, 1)
                m1 = lambda tt: rb(tt, 185, 1)
                step(lambda tt: v.tensor_add(ss(tt), sc(tt), bias_bc[:, 0:ne]))
                step(lambda tt: v.tensor_sub(ss(tt), ss(tt), iota16[:, 0:ne]))
                step(lambda tt: v.tensor_reduce(
                    g1(tt), ss(tt).rearrange("p (g r) -> p g r", g=4), AX.X, ALU.max))
                for jj in range(4):
                    step(lambda tt, jj=jj: v.tensor_copy(
                        ge(tt)[:, 4 * jj:4 * jj + 4],
                        g1(tt)[:, jj:jj + 1].broadcast_to([128, 4])))
                step(lambda tt: v.tensor_sub(wk(tt), ss(tt), ge(tt)))
                step(lambda tt: v.tensor_scalar(u(tt), wk(tt), -1e-12, None,
                                                ALU.is_ge))
                step(lambda tt: v.scalar_tensor_tensor(wk(tt), u(tt), -BIG, ss(tt),
                                                       ALU.mult, ALU.add))
                step(lambda tt: v.tensor_reduce(
                    g2(tt), wk(tt).rearrange("p (g r) -> p g r", g=4), AX.X, ALU.max))
                step(lambda tt: v.tensor_add(g1(tt), g1(tt), g2(tt)))
                step(lambda tt: v.tensor_sub(g1(tt), g1(tt), iota4[:, :]))
                step(lambda tt: v.tensor_reduce(
                    m0(tt), g1(tt).rearrange("p (a e) -> p a e", a=1), AX.X, ALU.max))
                step(lambda tt: v.tensor_single_scalar(g2(tt), g1(tt), m0(tt),
                                                       ALU.is_ge))
                step(lambda tt: v.scalar_tensor_tensor(g3(tt), g2(tt), -BIG, g1(tt),
                                                       ALU.mult, ALU.add))
                step(lambda tt: v.tensor_reduce(
                    m0(tt), g3(tt).rearrange("p (a e) -> p a e", a=1), AX.X, ALU.max))
                step(lambda tt: v.tensor_single_scalar(g3(tt), g1(tt), m0(tt),
                                                       ALU.is_ge))
                for jj in range(4):
                    step(lambda tt, jj=jj: v.tensor_copy(
                        ge(tt)[:, 4 * jj:4 * jj + 4],
                        g3(tt)[:, jj:jj + 1].broadcast_to([128, 4])))
                step(lambda tt: v.tensor_sub(wk(tt), ss(tt), neg1p[:, 0:ne]))
                step(lambda tt: v.tensor_mul(wk(tt), wk(tt), ge(tt)))
                step(lambda tt: v.tensor_add(wk(tt), wk(tt), neg1p[:, 0:ne]))
                step(lambda tt: v.tensor_single_scalar(u(tt), wk(tt), 0.0, ALU.add))
                for _ in range(3):
                    step(lambda tt: v.tensor_reduce(
                        m0(tt), u(tt).rearrange("p (a e) -> p a e", a=1), AX.X,
                        ALU.max))
                    step(lambda tt: v.tensor_single_scalar(ge(tt), u(tt), m0(tt),
                                                           ALU.is_ge))
                    step(lambda tt: v.scalar_tensor_tensor(u(tt), ge(tt), -BIG,
                                                           u(tt), ALU.mult, ALU.add))
                step(lambda tt: v.tensor_reduce(
                    m0(tt), u(tt).rearrange("p (a e) -> p a e", a=1), AX.X, ALU.max))
                step(lambda tt: v.tensor_single_scalar(u(tt), wk(tt), m0(tt),
                                                       ALU.is_ge))
                step(lambda tt: v.tensor_mul(wk(tt), sc(tt), u(tt)))
                step(lambda tt: v.tensor_reduce(
                    m0(tt), wk(tt).rearrange("p (a e) -> p a e", a=1), AX.X, ALU.add))
                step(lambda tt: v.tensor_scalar(m0(tt), m0(tt), 1e-20, None,
                                                ALU.add))
                step(lambda tt: v.reciprocal(m1(tt), m0(tt)))
                ins = each(lambda tt: v.tensor_scalar(
                    comb[:, tt * 16:tt * 16 + ne], wk(tt), m1(tt), 2.5,
                    ALU.mult, ALU.mult))
            inc("ve", ins)

        emit("vector", ve2)
        ve2_all = cnt["ve"]

        tr_pe, tr_ve, tr_po = {}, {}, {}
        for tt in range(16):
            bk = 4 + tt % 2

            def pe(t, tt=tt, bk=bk):
                if tt == 0:
                    t.wait_ge(sem["ve"], ve2_all)
                if tt >= 2:
                    t.wait_ge(sem["ve"], tr_ve[tt - 2])
                ins = t.transpose(bank(bk, 0, ne, 128), comb[:, tt * 16:tt * 16 + ne],
                                  ident[:, :])
                tr_pe[tt] = inc("pe", ins)

            def ve(v, tt=tt, bk=bk):
                v.wait_ge(sem["pe"], tr_pe[tt])
                if tt >= 1:
                    v.wait_ge(sem["po"], tr_po[tt - 1])
                ins = v.tensor_copy(combT[0:ne, :], bank(bk, 0, ne, 128))
                tr_ve[tt] = inc("ve", ins)

            def po2(g, tt=tt):
                g.wait_ge(sem["ve"], tr_ve[tt])
                ins = g.dma_start(out=combT_d.ap()[0:ne, tt * 128:(tt + 1) * 128],
                                  in_=combT[0:ne, :])
                tr_po[tt] = inc("po", ins, 16)

            emit("tensor", pe)
            emit("vector", ve)
            emit("gpsimd", po2)

    def emit_moe(layer):
        if layer == 0:
            nhc = L0_DH // 128
            W1_OFF, W2_OFF, H_OFF, SIL_OFF = 0, 16384, 32768, None
            WU_OFF = None
            experts = [("l0_w1", "l0_w2", None, e, True) for e in range(L0_NE)]
            if DBG_NEX:
                experts = experts[:DBG_NEX]
            dh = L0_DH
        else:
            nhc = L1_DH // 128
            W1_OFF, WU_OFF, W2_OFF, H_OFF, SIL_OFF = 0, 8192, 16384, 24576, 32768
            experts = [("l1_gate_w", "l1_down_w", "l1_up_w", e, True)
                       for e in range(L1_NE)]
            experts += [("l1_shg_w", "l1_shd_w", "l1_shu_w", sh, False)
                        for sh in range(2)]
            dh = L1_DH
        nex = len(experts)

        npar = 1 if layer == 0 else 2

        def Htile(par, hc, tw):
            o = H_OFF + (par % npar) * nhc * TB + hc * TB
            return AR[:, o:o + tw]

        ld_m = [None] * nex
        ld2_m = [None] * nex
        last_a = [0, 0]
        emit_moe.crow_ct = [0]
        emit_moe.crow_war = {0: 0, 1: 0}
        emit_moe.crow_po = {0: 0, 1: 0}
        alpha_done = [None] * nex
        beta_done = [None] * nex
        ev_m = {}
        hready = {}
        hfree = {}
        cb_last = [0]

        def emit_load(ei):
            g1n, g2n, gun, e, is_real = experts[ei]

            def po(g, ei=ei, g1n=g1n, g2n=g2n, gun=gun, e=e, is_real=is_real):
                if ei == 0:
                    wait(g, "pe")
                    wait(g, "ve")
                if ei >= 1 and alpha_done[ei - 1] is not None:
                    g.wait_ge(sem["pe"], alpha_done[ei - 1])
                if layer == 0 or is_real:
                    src = inp[g1n].ap()[e]
                else:
                    src = inp[g1n].ap()[:, e * L1_DH:(e + 1) * L1_DH]
                ins = g.dma_start(
                    out=ar(W1_OFF, NCH * dh).rearrange("p (c h) -> p c h", c=NCH),
                    in_=src.rearrange("(c p) h -> p c h", p=128))
                inc("wl1", ins, 16)
                if layer == 1:
                    srcu = inp[gun].ap()[e] if is_real else \
                        inp[gun].ap()[:, e * L1_DH:(e + 1) * L1_DH]
                    ins = g.dma_start(
                        out=ar(WU_OFF, NCH * dh).rearrange("p (c h) -> p c h", c=NCH),
                        in_=srcu.rearrange("(c p) h -> p c h", p=128))
                    inc("wlu", ins, 16)
                ld_m[ei] = (cnt["wl1"], cnt["wlu"])
                if ei >= 1 and beta_done[ei - 1] is not None:
                    g.wait_ge(sem["pe"], beta_done[ei - 1])
                if layer == 0 or is_real:
                    srcd = inp[g2n].ap()[e]
                else:
                    srcd = inp[g2n].ap()[e * L1_DH:(e + 1) * L1_DH, :]
                ins = g.dma_start(
                    out=ar(W2_OFF, nhc * DM).rearrange("p (h d) -> p h d", h=nhc),
                    in_=srcd.rearrange("(h p) d -> p h d", p=128))
                ld2_m[ei] = inc("wl2", ins, 16)

            emit("gpsimd", po)

        emit_load(0)
        rt_po = cnt["po"]

        for ei in range(nex):
            g1n, g2n, gun, e, is_real = experts[ei]

            if is_real:
                cb_pe = {}
                cb_vem = {}
                par_e = 0

                def poc(g, ei=ei, e=e, par_e=par_e):
                    if ei == 0:
                        g.wait_ge(sem["po"], rt_po)
                    if emit_moe.crow_war[par_e]:
                        g.wait_ge(sem["pe"], emit_moe.crow_war[par_e])
                    emit_moe.crow_po[par_e] = inc("po", g.dma_start(
                        out=crow_bf[0:1, :], in_=combT_d.ap()[e:e + 1, :]), 16)

                emit("gpsimd", poc)

                def pec(t, ei=ei, par_e=par_e):
                    ve_base = cnt["ve"]
                    t.wait_ge(sem["po"], emit_moe.crow_po[par_e])
                    for bi, (t0, tw) in enumerate(TBS):
                        if ei == 0 and bi < 2:
                            wait(t, "ve")
                        elif bi >= 2:
                            t.wait_ge(sem["ve"], ve_base + bi - 1)
                        elif cb_last[0]:
                            t.wait_ge(sem["ve"], cb_last[0])
                        ins = t.matmul(bank(6 + bi % 2, 0, 128, tw), ones_row[:, :],
                                       crow_bf[0:1, t0:t0 + tw],
                                       start=True, stop=True)
                        cb_pe[bi] = inc("pe", ins)
                    emit_moe.crow_war[par_e] = cnt["pe"]

                def vec(v):
                    for bi, (t0, tw) in enumerate(TBS):
                        v.wait_ge(sem["pe"], cb_pe[bi])
                        ins = v.tensor_copy(cbc[:, t0:t0 + tw],
                                            bank(6 + bi % 2, 0, 128, tw))
                        cb_vem[bi] = inc("ve", ins)
                    cb_last[0] = cnt["ve"]

                emit("tensor", pec)
                emit("vector", vec)
                emit_moe.crow_ct[0] += 1
            cb_ve = cnt["ve"]

            if layer == 0:
                sched = [("a", 0), ("b", 0), ("a", 1), ("b", 1), ("a", 2),
                         ("b", 2), ("a", 3), ("b", 3)]
            else:
                sched = [("a", 0), ("a", 1), ("b", 0), ("a", 2), ("b", 1),
                         ("a", 3), ("b", 2), ("b", 3)]
            for kind, bi in sched:
                t0, tw = TBS[bi]
                par = bi % 2
                if kind == "a":
                    al_pe, al_ac, al_ve = {}, {}, {}

                    def pea(t, ei=ei, bi=bi, t0=t0, tw=tw,
                            al_pe=al_pe, cb_ve=cb_ve, is_real=is_real):
                        ac_base = cnt["ac"]
                        ve_base = cnt["ve"]
                        if layer == 0:
                            nve = 1 if is_real else 0
                        else:
                            nve = 2 if is_real else 1
                        if bi == 0:
                            t.wait_ge(sem["wl1"], ld_m[ei][0])
                            if layer == 1:
                                t.wait_ge(sem["wlu"], ld_m[ei][1])
                            t.wait_ge(sem["ve"], cb_ve)
                        if bi >= npar:
                            t.wait_ge(sem["pe"], hfree[(ei, bi - npar)])
                        if last_a[0]:
                            t.wait_ge(sem["ac"], last_a[0])
                        if last_a[1]:
                            t.wait_ge(sem["ve"], last_a[1])
                        for hc in range(nhc):
                            if layer == 0:
                                bkg, bku = 4 + (hc % 2), None
                            else:
                                bkg, bku = 4 + (hc % 2) * 2, 5 + (hc % 2) * 2
                            if hc >= 2:
                                t.wait_ge(sem["ac"], ac_base + hc - 1)
                                if layer == 1 and nve:
                                    t.wait_ge(sem["ve"], ve_base + (hc - 1) * nve)
                            crange = (list(range(NCH)) if globals()["DBG_ALPHA_C"]
                                      is None else globals()["DBG_ALPHA_C"])
                            for ci, c in enumerate(crange):
                                ins = t.matmul(
                                    bank(bkg, 0, 128, tw),
                                    AR[:, W1_OFF + c * dh + hc * 128:
                                       W1_OFF + c * dh + hc * 128 + 128],
                                    xnm(c, t0, tw),
                                    start=(ci == 0), stop=(ci == len(crange) - 1))
                            al_pe[hc] = inc("pe", ins)
                            if layer == 1:
                                for c in range(NCH):
                                    ins = t.matmul(
                                        bank(bku, 0, 128, tw),
                                        AR[:, WU_OFF + c * dh + hc * 128:
                                           WU_OFF + c * dh + hc * 128 + 128],
                                        xnm(c, t0, tw),
                                        start=(c == 0), stop=(c == NCH - 1))
                                al_pe[hc] = inc("pe", ins)
                        if bi == 3:
                            alpha_done[ei] = cnt["pe"]

                    def aca(s, ei=ei, bi=bi, tw=tw, par=par,
                            al_pe=al_pe, al_ac=al_ac):
                        if layer == 1 and bi >= 2 and (ei, bi - 2) in hready:
                            s.wait_ge(sem["ve"], hready[(ei, bi - 2)][1])
                        if layer == 0 and bi >= 1:
                            s.wait_ge(sem["pe"], hfree[(ei, bi - 1)])
                        for hc in range(nhc):
                            bkg = 4 + (hc % 2) if layer == 0 else 4 + (hc % 2) * 2
                            s.wait_ge(sem["pe"], al_pe[hc])
                            if layer == 0:
                                dst = Htile(par, hc, tw)
                            else:
                                dst = AR[:, SIL_OFF + hc * TB:SIL_OFF + hc * TB + tw]
                            ins = s.activation(dst, bank(bkg, 0, 128, tw), AF.Silu,
                                               bias=0.0, scale=1.0)
                            al_ac[hc] = inc("ac", ins)

                    def vea(v, ei=ei, bi=bi, t0=t0, tw=tw, par=par, is_real=is_real,
                            al_ac=al_ac, al_ve=al_ve):
                        ins = None
                        for hc in range(nhc):
                            v.wait_ge(sem["ac"], al_ac[hc])
                            h = Htile(par, hc, tw)
                            if layer == 0:
                                if is_real and not globals()['DBG_NOSCALE']:
                                    ins = v.tensor_mul(h, h, cbc[:, t0:t0 + tw])
                            else:
                                bku = 5 + (hc % 2) * 2
                                sg = AR[:, SIL_OFF + hc * TB:SIL_OFF + hc * TB + tw]
                                ins = v.tensor_mul(h, bank(bku, 0, 128, tw), sg)
                                if is_real:
                                    vsafe(v, ins)
                                    ins = v.tensor_mul(h, h, cbc[:, t0:t0 + tw])
                            if ins is not None:
                                al_ve[hc] = inc("ve", ins)
                            else:
                                al_ve[hc] = cnt["ve"]

                    emit("tensor", pea)
                    emit("scalar", aca)
                    emit("vector", vea)
                    hready[(ei, bi)] = (cnt["ac"], cnt["ve"])
                    last_a[0], last_a[1] = cnt["ac"], cnt["ve"]
                    if ei == 0 and bi == 0 and layer == 0:
                        dbg_snap("H00", AR[:, H_OFF:H_OFF + nhc * TB],
                                 (128, nhc * TB), BF16)
                        dbg_snap("xnsnap", xn[:, :], (128, NCH * TOKH), BF16)
                        dbg_snap("w1snap", AR[:, 0:8192], (128, 8192), BF16)
                else:
                    for dq in range(2):
                        be_pe = {}

                        def peb(t, ei=ei, bi=bi, t0=t0, tw=tw, par=par, dq=dq,
                                be_pe=be_pe):
                            if dq == 0:
                                hac, hve = hready[(ei, bi)]
                                t.wait_ge(sem["ac"], hac)
                                t.wait_ge(sem["ve"], hve)
                                if bi == 0:
                                    t.wait_ge(sem["wl2"], ld2_m[ei])
                            for dc4 in range(4):
                                if dq == 1:
                                    km = ev_m.get((ei, bi, "q0"))
                                elif bi > 0:
                                    km = ev_m.get((ei, bi - 1, dc4))
                                elif ei > 0:
                                    km = ev_m.get((ei - 1, NTB - 1, dc4))
                                else:
                                    km = None
                                if km:
                                    t.wait_ge(sem["ve"], km)
                                    break
                            for hc in range(nhc):
                                for dc4 in range(4):
                                    dc = dq * 4 + dc4
                                    ins = t.matmul(
                                        bank(dc4, 0, 128, tw),
                                        AR[:, W2_OFF + hc * DM + dc * 128:
                                           W2_OFF + hc * DM + dc * 128 + 128],
                                        Htile(par, hc, tw),
                                        start=(hc == 0), stop=(hc == nhc - 1))
                            be_pe[0] = inc("pe", ins)
                            if dq == 1:
                                hfree[(ei, bi)] = cnt["pe"]
                                if bi == 3:
                                    beta_done[ei] = cnt["pe"]

                        def veb(v, ei=ei, bi=bi, t0=t0, tw=tw, dq=dq, be_pe=be_pe):
                            v.wait_ge(sem["pe"], be_pe[0])
                            for dc4 in range(4):
                                dc = dq * 4 + dc4
                                ins = v.tensor_add(xsm(dc, t0, tw), xsm(dc, t0, tw),
                                                   bank(dc4, 0, 128, tw))
                                inc("ve", ins)
                            if dq == 0:
                                ev_m[(ei, bi, "q0")] = cnt["ve"]
                            else:
                                for dc4 in range(4):
                                    ev_m[(ei, bi, dc4)] = cnt["ve"]

                        emit("tensor", peb)
                        emit("vector", veb)

            if ei + 1 < nex:
                emit_load(ei + 1)

    def emit_l0_passA():
        ln_ve = cnt["ve"]
        ipw_m = {}
        pe_m, ac_m, ve_m = {}, {}, {}
        def xzh(j):
            return SCR[:, 2560 + 3 * (j % 2):2560 + 3 * (j % 2) + 3]

        def po0(g):
            wait(g, "ve", ln_ve)
            wait(g, "pe")
            inc("wl", g.dma_start(
                out=AR[:, XP_OFF:XP_OFF + NCI * DT]
                .rearrange("p (c r) -> p c r", c=NCI),
                in_=inp["l0_xp_w"].ap().rearrange("(c p) r -> p c r", p=128)), 16)
            for j in range(2):
                ins = g.dma_start(
                    out=ar(RING_OFF + (j % 2) * 1024, 1024)
                    .rearrange("p (c m) -> p c m", c=NCH),
                    in_=inp["l0_ip_w"].ap()[:, j * 128:(j + 1) * 128]
                    .rearrange("(c p) m -> p c m", p=128))
                ipw_m[j] = inc("wla" if j % 2 == 0 else "wlb", ins, 16)

        emit("gpsimd", po0)

        for j in range(NCI):
            if j >= 2:
                def po(g, j=j):
                    g.wait_ge(sem["pe"], pe_m[(j - 2, len(TBS_H) - 1)])
                    ins = g.dma_start(
                        out=ar(RING_OFF + (j % 2) * 1024, 1024)
                        .rearrange("p (c m) -> p c m", c=NCH),
                        in_=inp["l0_ip_w"].ap()[:, j * 128:(j + 1) * 128]
                        .rearrange("(c p) m -> p c m", p=128))
                    ipw_m[j] = inc("wla" if j % 2 == 0 else "wlb", ins, 16)

                emit("gpsimd", po)

            for bi, (t0, tw) in enumerate(TBS_H):
                g_idx = j * len(TBS_H) + bi
                bk = g_idx % 3

                def pe(t, j=j, bi=bi, t0=t0, tw=tw, bk=bk, g_idx=g_idx):
                    if bi == 0:
                        t.wait_ge(sem["wla" if j % 2 == 0 else "wlb"], ipw_m[j])
                        if j == 0:
                            t.wait_ge(sem["ve"], ln_ve)
                        if j >= 1:
                            km = ve_m.get((j - 2, "conv"))
                            if km:
                                t.wait_ge(sem["ve"], km)
                    if g_idx >= 3:
                        pj, pb = divmod(g_idx - 3, len(TBS_H))
                        t.wait_ge(sem["ac"], ac_m[(pj, pb)])
                    for c in range(NCH):
                        ins = t.matmul(bank(bk, 0, 128, tw),
                                       ar(RING_OFF + (j % 2) * 1024, 1024)
                                       [:, c * 128:(c + 1) * 128],
                                       xnf(c, t0, tw),
                                       start=(c == 0), stop=(c == NCH - 1))
                    pe_m[(j, bi)] = inc("pe", ins)

                def ac(s, j=j, bi=bi, t0=t0, tw=tw, bk=bk):
                    s.wait_ge(sem["pe"], pe_m[(j, bi)])
                    if bi == 0 and j >= 1:
                        km = ve_m.get((j - 2, "conv"))
                        if km:
                            s.wait_ge(sem["ve"], km)
                    if bi == len(TBS_H) - 1:
                        ins = s.copy(Uc(j)[:, 1537:2048], bank(bk, 0, 128, tw))
                    elif bi == 0:
                        s.copy(xzh(j), bank(bk, 0, 128, 3))
                        ins = s.copy(Uc(j)[:, 0:509], bank(bk, 0, 128, tw)[:, 3:512])
                    else:
                        ins = s.copy(Uc(j)[:, t0 - 3:t0 - 3 + tw],
                                     bank(bk, 0, 128, tw))
                    ac_m[(j, bi)] = inc("ac", ins)

                emit("tensor", pe)
                emit("scalar", ac)

            def ve(v, j=j):
                wait(v, "ac", ac_m[(j, len(TBS_H) - 1)])
                if j >= 1:
                    v.wait_ge(sem["ac"], ac_m[(j - 1, "silu")])
                first = True
                for k in range(4):
                    cwk = colpack[:, 64 + 16 * k + j:64 + 16 * k + j + 1]
                    hw = 3 - k
                    if hw > 0:
                        if first:
                            vsafe(v, v.tensor_single_scalar(
                                sconv(0, hw), xzh(j)[:, k:3], cwk, ALU.mult))
                        else:
                            vsafe(v, v.scalar_tensor_tensor(
                                sconv(0, hw), xzh(j)[:, k:3], cwk, sconv(0, hw),
                                ALU.mult, ALU.add))
                    mw = TOK - max(0, hw)
                    if first:
                        vsafe(v, v.tensor_single_scalar(
                            sconv(max(0, hw), mw), Uc(j)[:, 0:mw], cwk, ALU.mult))
                        first = False
                    else:
                        ins = v.scalar_tensor_tensor(
                            sconv(max(0, hw), mw), Uc(j)[:, 0:mw], cwk,
                            sconv(max(0, hw), mw), ALU.mult, ALU.add)
                        if k < 3:
                            vsafe(v, ins)
                ve_m[(j, "conv")] = inc("ve", ins)

            def ac2(s, j=j):
                wait(s, "ve", ve_m[(j, "conv")])
                cb = colpack[:, 32 + j:32 + j + 1]
                ins = s.activation(Uc(j), sconv(), AF.Silu, bias=cb, scale=1.0)
                ac_m[(j, "silu")] = inc("ac", ins)

            emit("vector", ve)
            emit("scalar", ac2)

            if j >= 1:
                def pexp(t, jj=j - 1):
                    if jj == 0:
                        wait(t, "wl")
                    t.wait_ge(sem["ac"], ac_m[(jj, "silu")])
                    for bi, (t0, tw) in enumerate(TBS):
                        ins = t.matmul(bank(4 + bi, 0, 64, tw),
                                       xp_sb(jj),
                                       Uc(jj)[:, t0:t0 + tw],
                                       start=(jj == 0), stop=(jj == NCI - 1))
                    inc("pe", ins)

                emit("tensor", pexp)

        def pexpl(t):
            jj = NCI - 1
            t.wait_ge(sem["ac"], ac_m[(jj, "silu")])
            for bi, (t0, tw) in enumerate(TBS):
                ins = t.matmul(bank(4 + bi, 0, 64, tw),
                               xp_sb(jj),
                               Uc(jj)[:, t0:t0 + tw],
                               start=(jj == 0), stop=(jj == NCI - 1))
            inc("pe", ins)

        emit("tensor", pexpl)
        pe_xp = cnt["pe"]

        def podtw(g):
            g.wait_ge(sem["pe"], pe_xp)
            inc("wl", g.dma_start(out=AR[0:64, 38912:38912 + DI],
                                  in_=inp["l0_dt_w"].ap()), 16)

        emit("gpsimd", podtw)

        def vedt(v):
            v.wait_ge(sem["pe"], pe_xp)
            for bi, (t0, tw) in enumerate(TBS):
                ins = v.tensor_copy(AR[0:64, DTL_OFF + t0:DTL_OFF + t0 + tw],
                                    bank(4 + bi, 0, 64, tw))
            inc("ve", ins)

        emit("vector", vedt)

    dc_war = {}

    def emit_delta_chain(j, accum_R, s0free=None):
        bks = (0, 4)[j % 2]
        marks = {}

        def pe(t, j=j, bks=bks):
            if bks in dc_war:
                t.wait_ge(sem["ac"], dc_war[bks])
            sw = dc_war.get("scan%d" % (j % 2))
            if sw:
                t.wait_ge(sem["ve"], sw)
            if j <= 1:
                wait(t, "ve")
                wait(t, "wl")
            for bi, (t0, tw) in enumerate(TBS):
                ins = t.matmul(bank(bks + bi, 0, 128, tw),
                               AR[0:64, 38912 + j * 128:38912 + (j + 1) * 128],
                               AR[0:64, DTL_OFF + t0:DTL_OFF + t0 + tw],
                               start=True, stop=True)
            marks["pe"] = inc("pe", ins)

        def ac(s, j=j, bks=bks):
            s.wait_ge(sem["pe"], marks["pe"])
            if s0free is not None:
                s.wait_ge(sem["ve"], s0free)
            dtb = colpack[:, 48 + j:48 + j + 1]
            for bi, (t0, tw) in enumerate(TBS):
                ins = s.activation(s0(t0, tw), bank(bks + bi, 0, 128, tw), AF.Exp,
                                   bias=dtb, scale=1.0)
            s.wait_ge(sem["ac"], inc("ac", ins))
            ins = s.activation(s0(), s0(), AF.Ln, bias=1.0, scale=1.0)
            marks["ac"] = inc("ac", ins)
            dc_war[bks] = marks["ac"]

        def ve(v, j=j):
            wait(v, "ac", marks["ac"])
            if accum_R:
                vsafe(v, v.tensor_scalar(s0(), s0(), 1e-6, None, ALU.max))
                vsafe(v, v.tensor_scalar(s0(), s0(), 10.0, None, ALU.min, ALU.add,
                                         accum_out=packR[:, j:j + 1]))
            else:
                vsafe(v, v.tensor_scalar(s0(), s0(), 1e-6, 10.0, ALU.max, ALU.min))
            Acol = colpack[:, j:j + 1]
            ins = v.tensor_single_scalar(s0(), s0(), Acol, ALU.mult)
            marks["ve"] = inc("ve", ins)

        def ac2(s, j=j):
            wait(s, "ve", marks["ve"])
            ins = s.activation(s0(), s0(), AF.Exp, bias=0.0, scale=1.0)
            marks["ac2"] = inc("ac", ins)

        emit("tensor", pe)
        emit("scalar", ac)
        emit("vector", ve)
        emit("scalar", ac2)
        return marks

    def emit_l0_prepass():
        prev_s0free = None
        for j in range(NCI):
            marks = emit_delta_chain(j, accum_R=True, s0free=prev_s0free)

            def ve(v, j=j, marks=marks):
                bks = (0, 4)[j % 2]
                wait(v, "ac", marks["ac2"])
                pscan = psum[:, bks * 512:bks * 512 + TOK]
                vsafe(v, v.tensor_tensor_scan(pscan, s0(), Uc(j), zeros_col[:, 0:1],
                                              ALU.mult, ALU.add))
                ins = v.tensor_copy(packF[:, j:j + 1], pscan[:, TOK - 1:TOK])
                marks["scan"] = inc("ve", ins)

            emit("vector", ve)
            prev_s0free = cnt["ve"]
            dc_war["scan%d" % (j % 2)] = cnt["ve"]

        def vp(v):
            wait(v, "ve")
            ins = v.tensor_mul(packP[:, :], packR[:, :], colpack[:, 0:16])
            inc("ve", ins)

        def acp(s):
            wait(s, "ve")
            inc("ac", s.activation(packP[:, :], packP[:, :], AF.Exp,
                                   bias=0.0, scale=1.0))

        def vpack(v):
            wait(v, "ac")
            wait(v, "ve")
            v.tensor_copy(agg[:, 0:16], packF[:, :])
            inc("ve", v.tensor_copy(agg[:, 16:32], packP[:, :]))

        emit("vector", vp)
        emit("scalar", acp)
        emit("vector", vpack)
        ve_pack = cnt["ve"]

        def po(g):
            g.wait_ge(sem["ve"], ve_pack)
            inc("po", g.dma_start(out=cc0_in.ap(), in_=agg[:, 0:32]), 16)
            wait(g, "po")
            inc("cc", g.collective_compute(
                "AllGather", ALU.bypass, replica_groups=[list(range(NCORE))],
                ins=[cc0_in.ap().opt()], outs=[cc0_out.ap().opt()]), 1)
            wait(g, "cc")
            for r in range(NCORE):
                inc("po", g.dma_start(out=agg[:, 32 * r:32 * (r + 1)],
                                      in_=cc0_out.ap()[128 * r:128 * (r + 1), :]), 16)

        emit("gpsimd", po)
        po_done = cnt["po"]

        def vcomb(v):
            v.wait_ge(sem["po"], po_done)
            vsafe(v, v.memset(S0[:, :], 0.0))
            vsafe(v, v.memset(packR[:, :], 0.0))
            for r in range(NCORE):
                vsafe(v, v.scalar_tensor_tensor(S0[:, :], packR[:, :],
                                                onehot8[:, r:r + 1], S0[:, :],
                                                ALU.mult, ALU.add))
                if r < NCORE - 1:
                    vsafe(v, v.tensor_mul(packP[:, :],
                                          agg[:, 32 * r + 16:32 * r + 32],
                                          packR[:, :]))
                    vsafe(v, v.tensor_add(packR[:, :], packP[:, :],
                                          agg[:, 32 * r:32 * r + 16]))

        emit("vector", vcomb)

    def emit_l0_passC1():
        emit_l0_passC1.s0_ready = cnt["ve"]
        prev_s0free = None
        for j in range(NCI):
            marks = emit_delta_chain(j, accum_R=False, s0free=prev_s0free)

            def ve(v, j=j, marks=marks):
                bks = (0, 4)[j % 2]
                wait(v, "ac", marks["ac2"])
                if j == 0:
                    wait(v, "ve", emit_l0_passC1.s0_ready)
                pscan = psum[:, bks * 512:bks * 512 + TOK]
                vsafe(v, v.tensor_tensor_scan(pscan, s0(), Uc(j), S0[:, j:j + 1],
                                              ALU.mult, ALU.add))
                Dcol = colpack[:, 16 + j:16 + j + 1]
                vsafe(v, v.scalar_tensor_tensor(s1(), Uc(j), Dcol, pscan,
                                                ALU.mult, ALU.add))
                vsafe(v, v.tensor_scalar(s1(), s1(), -1e4, 1e4, ALU.max, ALU.min))
                ins = v.tensor_copy(Uc(j), s1())
                inc("ve", ins)

            emit("vector", ve)
            prev_s0free = cnt["ve"]
            dc_war["scan%d" % (j % 2)] = cnt["ve"]

    def emit_l0_passC2():
        zm = {}
        for j in range(NCI):
            def po(g, j=j):
                if j < 2:
                    wait(g, "pe")
                else:
                    g.wait_ge(sem["pe"], zm[(j - 2, NTB - 1, "pe")])
                ins = g.dma_start(
                    out=ar(RING_OFF + (j % 2) * 1024, 1024)
                    .rearrange("p (c m) -> p c m", c=NCH),
                    in_=inp["l0_ip_w"].ap()[:, (NCI + j) * 128:(NCI + j + 1) * 128]
                    .rearrange("(c p) m -> p c m", p=128))
                zm[(j, "wl")] = inc("wla" if j % 2 == 0 else "wlb", ins, 16)

            emit("gpsimd", po)

            for bi, (t0, tw) in enumerate(TBS):
                bk = bi % 2
                sz = SZ_OFF + (bi % 2) * TB

                def pe(t, j=j, bi=bi, t0=t0, tw=tw, bk=bk):
                    if bi == 0:
                        t.wait_ge(sem["wla" if j % 2 == 0 else "wlb"],
                                  zm[(j, "wl")])
                        if j == 0:
                            wait(t, "ve")
                            wait(t, "ac")
                    prev = (j, bi - 2, "ac") if bi >= 2 else (j - 1, bi + 2, "ac")
                    if prev in zm:
                        t.wait_ge(sem["ac"], zm[prev])
                    for c in range(NCH):
                        ins = t.matmul(bank(bk, 0, 128, tw),
                                       ar(RING_OFF + (j % 2) * 1024, 1024)
                                       [:, c * 128:(c + 1) * 128],
                                       xnm(c, t0, tw),
                                       start=(c == 0), stop=(c == NCH - 1))
                    zm[(j, bi, "pe")] = inc("pe", ins)

                def ac(s, j=j, bi=bi, tw=tw, bk=bk, sz=sz):
                    s.wait_ge(sem["pe"], zm[(j, bi, "pe")])
                    prev = (j, bi - 2, "ve") if bi >= 2 else (j - 1, bi + 2, "ve")
                    if prev in zm:
                        s.wait_ge(sem["ve"], zm[prev])
                    ins = s.activation(ar(sz, tw), bank(bk, 0, 128, tw),
                                       AF.Silu, bias=0.0, scale=1.0)
                    zm[(j, bi, "ac")] = inc("ac", ins)

                def ve(v, j=j, bi=bi, t0=t0, tw=tw, sz=sz):
                    wait(v, "ac", zm[(j, bi, "ac")])
                    ins = v.tensor_mul(Uc(j)[:, t0:t0 + tw], Uc(j)[:, t0:t0 + tw],
                                       ar(sz, tw))
                    zm[(j, bi, "ve")] = inc("ve", ins)

                emit("tensor", pe)
                emit("scalar", ac)
                emit("vector", ve)

    def emit_l0_outproj():
        opm = {}
        cve = cnt["ve"]
        for bi, (t0, tw) in enumerate(TBS):
            for j in range(NCI):
                def po(g, bi=bi, j=j):
                    prev = (bi, j - 2) if j >= 2 else ((bi - 1, NCI - 2 + j)
                                                       if bi > 0 else None)
                    if prev and (prev[0], prev[1], "pe") in opm:
                        g.wait_ge(sem["pe"], opm[(prev[0], prev[1], "pe")])
                    elif prev is None:
                        wait(g, "pe")
                    ins = g.dma_start(
                        out=ar(RING_OFF + (j % 2) * 1024, 1024),
                        in_=inp["l0_op_w"].ap()[j * 128:(j + 1) * 128, :])
                    opm[(bi, j, "wl")] = inc("wla" if j % 2 == 0 else "wlb", ins, 16)

                def pe(t, bi=bi, j=j, t0=t0, tw=tw):
                    t.wait_ge(sem["wla" if j % 2 == 0 else "wlb"],
                              opm[(bi, j, "wl")])
                    if bi == 0 and j == 0:
                        wait(t, "ve", cve)
                    if j == 0 and bi > 0:
                        for dc in range(NCH):
                            t.wait_ge(sem["ve"], opm[(bi - 1, dc, "ev")])
                    for dc in range(NCH):
                        ins = t.matmul(bank(dc, 0, 128, tw),
                                       ar(RING_OFF + (j % 2) * 1024, 1024)
                                       [:, dc * 128:(dc + 1) * 128],
                                       Uc(j)[:, t0:t0 + tw],
                                       start=(j == 0), stop=(j == NCI - 1))
                    opm[(bi, j, "pe")] = inc("pe", ins)

                emit("gpsimd", po)
                emit("tensor", pe)

            def ve(v, bi=bi, t0=t0, tw=tw):
                v.wait_ge(sem["pe"], opm[(bi, NCI - 1, "pe")])
                for dc in range(NCH):
                    ins = v.tensor_add(xsm(dc, t0, tw), xsm(dc, t0, tw),
                                       bank(dc, 0, 128, tw))
                    opm[(bi, dc, "ev")] = inc("ve", ins)

            emit("vector", ve)

    def emit_l1_attn_pre():
        WD_OFF = 0
        DBAR_R = 8192
        A_R = 10240
        SZT = 12288
        A16_OFF = 12800
        B16_OFF = 14848
        PROW_OFF = 16896
        ZROW_OFF = 18944
        PC_OFF = 36864

        def Rhs_(o, n=TOK):
            return SCR[0:16, o:o + n]

        R_hs = SCR[0:16, 0:TOK]
        R_ct = SCR[0:16, TOK:2 * TOK]

        ve_at = cnt["ve"]

        def po(g):
            wait(g, "ve", ve_at)
            wait(g, "pe")
            inc("wl1", g.dma_start(
                out=ar(WD_OFF, NCH * 1024).rearrange("p (c o) -> p c o", c=NCH),
                in_=inp["l1_Wd"].ap().rearrange("(c p) o -> p c o", p=128)), 16)
            inc("wl", g.dma_start(
                out=smallw[:, 0:128].rearrange("p (c e) -> p c e", c=NCH),
                in_=inp["l1_Wb"].ap().rearrange("(c p) e -> p c e", p=128)), 16)
            inc("wl", g.dma_start(
                out=smallw[:, 128:256].rearrange("p (c e) -> p c e", c=NCH),
                in_=inp["l1_Wc"].ap().rearrange("(c p) e -> p c e", p=128)), 16)
            inc("wl", g.dma_start(
                out=smallw[:, 256:384].rearrange("p (c e) -> p c e", c=NCH),
                in_=inp["l1_A"].ap().rearrange("(c p) e -> p c e", p=128)), 16)

        emit("gpsimd", po)
        wl1_wd = cnt["wl1"]
        wl_small = cnt["wl"]

        def vz(v):
            inc("ve", v.memset(AR[0:1, ZROW_OFF:ZROW_OFF + TOK], 0.0))

        emit("vector", vz)

        sa_pe = {}

        def pesa(t):
            t.wait_ge(sem["wl"], wl_small)
            wait(t, "ve")
            wait(t, "ac")
            for c in range(NCH):
                ins = t.matmul(bank(3, 0, 16, 1), smallw[:, 256 + 16 * c:272 + 16 * c],
                               ones_col[:, :], start=(c == 0), stop=(c == NCH - 1))
            sa_pe[0] = inc("pe", ins)

        def acsa(s):
            s.wait_ge(sem["pe"], sa_pe[0])
            ins = s.activation(l1H[:, 1:2], bank(3, 0, 16, 1), AF.Sigmoid,
                               bias=0.0, scale=1.0 / DM)
            inc("ac", ins)

        emit("tensor", pesa)
        emit("scalar", acsa)
        sa_ac = cnt["ac"]

        dm = {}
        for bi, (t0, tw) in enumerate(TBS):
            for dc in range(NCH):
                bk = dc % 2

                def pe(t, bi=bi, dc=dc, t0=t0, tw=tw, bk=bk):
                    if bi == 0 and dc == 0:
                        t.wait_ge(sem["wl1"], wl1_wd)
                    prev = dm.get((bi, dc - 2, "ac")) or \
                        (dm.get((bi - 1, NCH - 2 + dc, "ac")) if bi > 0 else None)
                    if prev:
                        t.wait_ge(sem["ac"], prev)
                    for c in range(NCH):
                        ins = t.matmul(bank(bk, 0, 128, tw),
                                       AR[:, WD_OFF + c * 1024 + dc * 128:
                                          WD_OFF + c * 1024 + dc * 128 + 128],
                                       xnm(c, t0, tw),
                                       start=(c == 0), stop=(c == NCH - 1))
                    dm[(bi, dc, "pe")] = inc("pe", ins)

                def ac(s, bi=bi, dc=dc, tw=tw, bk=bk):
                    s.wait_ge(sem["pe"], dm[(bi, dc, "pe")])
                    bd = colpack[:, 128 + dc:129 + dc]
                    ins = s.activation(SCR[:, 2048:2048 + tw], bank(bk, 0, 128, tw),
                                       AF.Exp, bias=bd, scale=1.0)
                    s.wait_ge(sem["ac"], inc("ac", ins))
                    ins = s.activation(AR[:, SZT:SZT + tw], SCR[:, 2048:2048 + tw],
                                       AF.Ln, bias=1.0, scale=1.0)
                    dm[(bi, dc, "ac")] = inc("ac", ins)

                def pe2(t, bi=bi, dc=dc, t0=t0, tw=tw):
                    t.wait_ge(sem["ac"], dm[(bi, dc, "ac")])
                    if dc == 0 and bi > 0:
                        t.wait_ge(sem["ac"], dm[(bi - 1, "dbar")])
                    ins = t.matmul(bank(2, 0, 1, tw), ones_col[:, :],
                                   AR[:, SZT:SZT + tw],
                                   start=(dc == 0), stop=(dc == NCH - 1))
                    dm[(bi, dc, "pe2")] = inc("pe", ins)

                emit("tensor", pe)
                emit("scalar", ac)
                emit("tensor", pe2)

            def acd(s, bi=bi, t0=t0, tw=tw):
                s.wait_ge(sem["pe"], dm[(bi, NCH - 1, "pe2")])
                ins = s.activation(AR[0:1, DBAR_R + t0:DBAR_R + t0 + tw],
                                   bank(2, 0, 1, tw), AF.Copy, bias=0.0,
                                   scale=1.0 / DM)
                dm[(bi, "dbar")] = inc("ac", ins)

            emit("scalar", acd)

        def va(v):
            wait(v, "ac")
            ins = v.tensor_scalar(AR[0:1, A_R:A_R + TOK],
                                  AR[0:1, DBAR_R:DBAR_R + TOK], -1.0, 1.0,
                                  ALU.mult, ALU.add)
            inc("ve", ins)

        emit("vector", va)

        bc_m = {}
        for which, woff, bcol, dst in [("b", 0, 144, R_hs), ("c", 128, 145, R_ct)]:
            for bi, (t0, tw) in enumerate(TBS):
                bk = 4 + bi % 2

                def pe(t, which=which, woff=woff, bi=bi, t0=t0, tw=tw, bk=bk):
                    prev = bc_m.get((which, bi - 2, "ac")) or \
                        (bc_m.get(("b", bi + 2, "ac")) if which == "c" else None)
                    if prev:
                        t.wait_ge(sem["ve"], prev)
                    for c in range(NCH):
                        ins = t.matmul(bank(bk, 0, 16, tw),
                                       smallw[:, woff + 16 * c:woff + 16 * c + 16],
                                       xnm(c, t0, tw),
                                       start=(c == 0), stop=(c == NCH - 1))
                    bc_m[(which, bi, "pe")] = inc("pe", ins)

                def vv(v, which=which, bi=bi, t0=t0, tw=tw, bk=bk, bcol=bcol,
                       dst=dst):
                    v.wait_ge(sem["pe"], bc_m[(which, bi, "pe")])
                    ins = v.tensor_scalar(dst[:, t0:t0 + tw], bank(bk, 0, 16, tw),
                                          colpack[0:16, bcol:bcol + 1], None,
                                          ALU.add)
                    bc_m[(which, bi, "ac")] = inc("ve", ins)

                emit("tensor", pe)
                emit("vector", vv)

        d16_m = {}
        for bi, (t0, tw) in enumerate(TBS):
            bk = 6 + bi % 2

            def pe(t, bi=bi, t0=t0, tw=tw, bk=bk):
                wait(t, "ac") if bi == 0 else None
                prev = d16_m.get((bi - 2, "ve"))
                if prev:
                    t.wait_ge(sem["ve"], prev)
                ins = t.matmul(bank(bk, 0, 16, tw), ones_row[0:1, 0:16],
                               AR[0:1, DBAR_R + t0:DBAR_R + t0 + tw],
                               start=True, stop=True)
                d16_m[(bi, "pe")] = inc("pe", ins)

            def ve(v, bi=bi, t0=t0, tw=tw, bk=bk):
                v.wait_ge(sem["pe"], d16_m[(bi, "pe")])
                if bi == 0:
                    wait(v, "ac")
                vsafe(v, v.tensor_scalar(
                    AR[0:16, A16_OFF + t0:A16_OFF + t0 + tw],
                    bank(bk, 0, 16, tw), -1.0, 1.0, ALU.mult, ALU.add))
                b16 = AR[0:16, B16_OFF + t0:B16_OFF + t0 + tw]
                vsafe(v, v.scalar_tensor_tensor(
                    b16, R_hs[:, t0:t0 + tw], l1H[:, 1:2], bank(bk, 0, 16, tw),
                    ALU.mult, ALU.mult))
                ins = v.engine_nop()
                d16_m[(bi, "ve")] = cnt["ve"]

            emit("tensor", pe)
            emit("vector", ve)

        def vsc(v):
            wait(v, "ac")
            wait(v, "ve")
            vsafe(v, v.tensor_tensor_scan(
                R_hs[:, :], AR[0:16, A16_OFF:A16_OFF + TOK],
                AR[0:16, B16_OFF:B16_OFF + TOK], zeros_col[0:16, 0:1],
                ALU.mult, ALU.add))
            vsafe(v, v.tensor_tensor_scan(
                AR[0:1, PROW_OFF:PROW_OFF + TOK], AR[0:1, A_R:A_R + TOK],
                AR[0:1, ZROW_OFF:ZROW_OFF + TOK], one_col[0:1, 0:1],
                ALU.mult, ALU.add))
            ins = v.tensor_copy(cc1pack[:, 0:1], R_hs[:, TOK - 1:TOK])
            inc("ve", ins)

        emit("vector", vsc)
        sc_ve = cnt["ve"]

        pc_m = {}
        for bi, (t0, tw) in enumerate(TBS):
            bk = 6 + bi % 2

            def pe(t, bi=bi, t0=t0, tw=tw, bk=bk):
                if bi == 0:
                    t.wait_ge(sem["ve"], sc_ve)
                prev = pc_m.get((bi - 2, "ve"))
                if prev:
                    t.wait_ge(sem["ve"], prev)
                ins = t.matmul(bank(bk, 0, 16, tw), ones_row[0:1, 0:16],
                               AR[0:1, PROW_OFF + t0:PROW_OFF + t0 + tw],
                               start=True, stop=True)
                pc_m[(bi, "pe")] = inc("pe", ins)

            def ve(v, bi=bi, t0=t0, tw=tw, bk=bk):
                v.wait_ge(sem["pe"], pc_m[(bi, "pe")])
                ins = v.tensor_copy(AR[0:16, PC_OFF + t0:PC_OFF + t0 + tw],
                                    bank(bk, 0, 16, tw))
                pc_m[(bi, "ve")] = inc("ve", ins)

            emit("tensor", pe)
            emit("vector", ve)

        def pct(t):
            wait(t, "ve")
            ins = t.matmul(bank(3, 0, 16, 1), ones_row[0:1, 0:16],
                           AR[0:1, PROW_OFF + TOK - 1:PROW_OFF + TOK],
                           start=True, stop=True)
            pc_m["ct"] = inc("pe", ins)

        def vct(v):
            v.wait_ge(sem["pe"], pc_m["ct"])
            ins = v.tensor_copy(cc1pack[:, 1:2], bank(3, 0, 16, 1))
            inc("ve", ins)

        emit("tensor", pct)
        emit("vector", vct)
        pack_ve = cnt["ve"]

        def po2(g):
            g.wait_ge(sem["ve"], pack_ve)
            inc("po", g.dma_start(out=cc1_in.ap(), in_=cc1pack[:, :]), 16)
            wait(g, "po")
            inc("cc", g.collective_compute(
                "AllGather", ALU.bypass, replica_groups=[list(range(NCORE))],
                ins=[cc1_in.ap().opt()], outs=[cc1_out.ap().opt()]), 1)

        emit("gpsimd", po2)

        def spagg(sync):
            wait(sync, "cc")
            ins = sync.dma_start(
                out=aggL1[:, :].rearrange("p (r j) -> p r j", r=NCORE),
                in_=cc1_out.ap().rearrange("(r p) j -> p r j", p=16))
            inc("ld", ins, 16)

        emit("sync", spagg)

    def emit_l1_attn_fin():
        PC_OFF = 36864
        YSCBF_OFF = 36864
        PRD_OFF = 38912
        R_hs = SCR[0:16, 0:TOK]
        R_ct = SCR[0:16, TOK:2 * TOK]

        def vcomb(v):
            wait(v, "ld")
            vsafe(v, v.memset(l1H[:, 0:1], 0.0))
            vsafe(v, v.memset(cc1pack[:, 0:1], 0.0))
            for r in range(NCORE):
                vsafe(v, v.scalar_tensor_tensor(l1H[:, 0:1], cc1pack[:, 0:1],
                                                onehot8[0:16, r:r + 1], l1H[:, 0:1],
                                                ALU.mult, ALU.add))
                if r < NCORE - 1:
                    vsafe(v, v.tensor_mul(cc1pack[:, 1:2],
                                          aggL1[:, 2 * r + 1:2 * r + 2],
                                          cc1pack[:, 0:1]))
                    vsafe(v, v.tensor_add(cc1pack[:, 0:1], cc1pack[:, 1:2],
                                          aggL1[:, 2 * r:2 * r + 1]))
            for bi, (t0, tw) in enumerate(TBS):
                vsafe(v, v.scalar_tensor_tensor(
                    R_hs[:, t0:t0 + tw], AR[0:16, PC_OFF + t0:PC_OFF + t0 + tw],
                    l1H[:, 0:1], R_hs[:, t0:t0 + tw], ALU.mult, ALU.add))
            ins = v.tensor_mul(AR[0:16, PRD_OFF:PRD_OFF + TOK], R_hs[:, :],
                               R_ct[:, :])
            inc("ve", ins)

        emit("vector", vcomb)
        prod_ve = cnt["ve"]

        ys_m = {}
        for bi, (t0, tw) in enumerate(TBS):
            def pe(t, bi=bi, t0=t0, tw=tw):
                if bi == 0:
                    t.wait_ge(sem["ve"], prod_ve)
                ins = t.matmul(bank(bi, 0, 1, tw), ones_col[0:16, 0:1],
                               AR[0:16, PRD_OFF + t0:PRD_OFF + t0 + tw],
                               start=True, stop=True)
                ys_m[(bi, "pe")] = inc("pe", ins)

            def ve(v, bi=bi, t0=t0, tw=tw):
                v.wait_ge(sem["pe"], ys_m[(bi, "pe")])
                ins = v.tensor_copy(AR[0:1, YSCBF_OFF + t0:YSCBF_OFF + t0 + tw],
                                    bank(bi, 0, 1, tw))
                ys_m[(bi, "ve")] = inc("ve", ins)

            emit("tensor", pe)
            emit("vector", ve)
        ysc_ve = cnt["ve"]

        yb_m = {}
        for bi, (t0, tw) in enumerate(TBS):
            bk = 4 + bi % 2

            def pe(t, bi=bi, t0=t0, tw=tw, bk=bk):
                if bi == 0:
                    t.wait_ge(sem["ve"], ysc_ve)
                prev = yb_m.get((bi - 2, "ve"))
                if prev:
                    t.wait_ge(sem["ve"], prev)
                ins = t.matmul(bank(bk, 0, 128, tw), ones_row[:, :],
                               AR[0:1, YSCBF_OFF + t0:YSCBF_OFF + t0 + tw],
                               start=True, stop=True)
                yb_m[(bi, "pe")] = inc("pe", ins)

            def ve(v, bi=bi, t0=t0, tw=tw, bk=bk):
                v.wait_ge(sem["pe"], yb_m[(bi, "pe")])
                ins = v.tensor_copy(SCR[:, t0:t0 + tw], bank(bk, 0, 128, tw))
                yb_m[(bi, "ve")] = inc("ve", ins)

            emit("tensor", pe)
            emit("vector", ve)

        def po(g):
            wait(g, "pe")
            wait(g, "ve")
            inc("wl1", g.dma_start(
                out=ar(0, NCH * 1024).rearrange("p (c o) -> p c o", c=NCH),
                in_=inp["l1_Wo"].ap().rearrange("(c p) o -> p c o", p=128)), 16)

        emit("gpsimd", po)
        wo_wl = cnt["wl1"]

        wo_m = {}
        for bi, (t0, tw) in enumerate(TBS):
            def pe(t, bi=bi, t0=t0, tw=tw):
                if bi == 0:
                    t.wait_ge(sem["wl1"], wo_wl)
                    wait(t, "ve")
                if bi > 0:
                    for dc in range(NCH):
                        t.wait_ge(sem["ve"], wo_m[(bi - 1, "ev")])
                        break
                for dj in range(NCH):
                    for dc in range(NCH):
                        ins = t.matmul(bank(dc, 0, 128, tw),
                                       AR[:, dj * 1024 + dc * 128:
                                          dj * 1024 + dc * 128 + 128],
                                       xnm(dj, t0, tw),
                                       start=(dj == 0), stop=(dj == NCH - 1))
                wo_m[(bi, "pe")] = inc("pe", ins)

            def ve(v, bi=bi, t0=t0, tw=tw):
                v.wait_ge(sem["pe"], wo_m[(bi, "pe")])
                for dc in range(NCH):
                    vsafe(v, v.tensor_mul(SCR[:, 2048:2048 + tw],
                                          bank(dc, 0, 128, tw), SCR[:, t0:t0 + tw]))
                    bo = colpack[:, 136 + dc:137 + dc]
                    ins = v.scalar_tensor_tensor(xsm(dc, t0, tw),
                                                 SCR[:, 2048:2048 + tw], bo,
                                                 xsm(dc, t0, tw), ALU.add, ALU.add)
                    if dc < NCH - 1:
                        vsafe(v, ins)
                wo_m[(bi, "ev")] = inc("ve", ins)

            emit("tensor", pe)
            emit("vector", ve)

    def emit_store():
        def sp(sync):
            wait(sync, "ve")
            wait(sync, "ac")
            wait(sync, "pe")
            for c in range(NCH):
                inc("st", sync.dma_start(out=out_x.ap()[c * 128:(c + 1) * 128, :],
                                         in_=xsm(c)), 16)
            for name, (ap_, od) in dbg_outs.items():
                inc("st", sync.dma_start(out=od.ap(), in_=ap_), 16)
            sync.wait_ge(sem["st"], cnt["st"])

        emit("sync", sp)

    with nc.Block() as block:
        bh["block"] = block
        emit("sync", ph_loads)
        emit("gpsimd", ph_wloads_early)
        emit("scalar", ph_colprep_ac)
        emit("vector", ph_colprep_ve)

        if do_l0:
            with nc.named_scope("ln0"):
                emit_layernorm(full=True)
            dbg_add("xn0", xn[:, :], (128, NCH * TOKH), BF16)
            if globals()["DBG_STOP"] != "ln":
                with nc.named_scope("rt0"):
                    emit_routing(0)
                dbg_add("comb0", comb[:, :], (128, 256))
            if globals()["DBG_STOP"] not in ("ln", "rt"):
                with nc.named_scope("moe0"):
                    emit_moe(0)
                dbg_snap("x_moe0", x_res[:, :], (128, NCH * TOKH))
        if do_l0 and globals()["DBG_STOP"] is None:
            with nc.named_scope("passA"):
                emit_l0_passA()
            dbg_add("dtlow", AR[0:64, DTL_OFF:DTL_OFF + TOK], (64, TOK), BF16)
            dbg_snap("xi", AR[:, U_OFF:U_OFF + NCI * TOK], (128, NCI * TOK), BF16)
            with nc.named_scope("prepass"):
                emit_l0_prepass()
            dbg_add("packF", packF[:, :], (128, 16))
            dbg_add("packP", packP[:, :], (128, 16))
            dbg_add("S0", S0[:, :], (128, 16))
            with nc.named_scope("passC1"):
                emit_l0_passC1()
            dbg_add("y0", AR[:, U_OFF:U_OFF + NCI * TOK], (128, NCI * TOK), BF16)
            with nc.named_scope("passC2"):
                emit_l0_passC2()
            with nc.named_scope("outproj"):
                emit_l0_outproj()
            dbg_snap("x_l0s", x_res[:, :], (128, NCH * TOKH))

        if do_l1 and globals()["DBG_STOP"] is None:
            with nc.named_scope("ln1"):
                emit_layernorm(full=False)
            dbg_snap("xn1", xn[:, :], (128, NCH * TOKH), BF16)
            with nc.named_scope("rt1"):
                emit_routing(1)
            dbg_add("comb1", comb[:, :], (128, 256))
            with nc.named_scope("attn1pre"):
                emit_l1_attn_pre()
            with nc.named_scope("moe1"):
                emit_moe(1)
            dbg_snap("x_moe1", x_res[:, :], (128, NCH * TOKH))
            with nc.named_scope("attn1fin"):
                emit_l1_attn_fin()

        emit_store()

    es.close()
    return nc, dbg_outs


_CACHE = {}


def _prep_const():
    c = {}
    c["identity"] = np.eye(128, dtype=np.float32)
    c["ones_col_bf"] = np.ones((128, 1), dtype=ml_dtypes.bfloat16)
    c["ones_row_bf"] = np.ones((1, 128), dtype=ml_dtypes.bfloat16)
    c["zeros_col"] = np.zeros((128, 1), dtype=np.float32)
    c["one_col"] = np.ones((128, 1), dtype=np.float32)
    c["iota8eps"] = np.broadcast_to((np.arange(8) * 1e-6).astype(np.float32),
                                    (128, 8)).copy()
    c["iota16eps"] = np.broadcast_to((np.arange(16) * 1e-6).astype(np.float32),
                                     (128, 16)).copy()
    c["iota4eps"] = np.broadcast_to((np.arange(4) * 1e-6).astype(np.float32),
                                    (128, 4)).copy()
    c["neg1p"] = np.broadcast_to((-1.0 - np.arange(16) * 1e-6).astype(np.float32),
                                 (128, 16)).copy()
    return c


def _colpack(inputs, do_l0, do_l1):
    cp = np.zeros((128, 160), dtype=np.float32)
    if do_l0:
        cp[:, 0:16] = inputs["L0_attn_A_log"].reshape(16, 128).T
        cp[:, 16:32] = inputs["L0_attn_D"].reshape(16, 128).T
        cp[:, 32:48] = inputs["L0_attn_conv_b"].reshape(16, 128).T
        cp[:, 48:64] = inputs["L0_attn_dt_proj_b"].reshape(16, 128).T
        cw = inputs["L0_attn_conv_w"].reshape(2048, 4)
        for k in range(4):
            cp[:, 64 + 16 * k:64 + 16 * (k + 1)] = cw[:, k].reshape(16, 128).T
    if do_l1:
        cp[:, 128:136] = inputs["L1_attn_bd"].reshape(8, 128).T
        cp[:, 136:144] = inputs["L1_attn_bo"].reshape(8, 128).T
        cp[0:16, 144] = inputs["L1_attn_bb"]
        cp[0:16, 145] = inputs["L1_attn_bc"]
    return cp


def make_in_maps(inputs, do_l0=True, do_l1=True, x_mid=None):
    const = _prep_const()
    shared = dict(const)
    shared["colpack_in"] = _colpack(inputs, do_l0, do_l1)
    if do_l0:
        shared["l0_ip_w"] = np.ascontiguousarray(inputs["L0_attn_in_proj_w"].T)
        shared["l0_xp_w"] = np.ascontiguousarray(inputs["L0_attn_x_proj_w"].T)
        shared["l0_dt_w"] = np.ascontiguousarray(inputs["L0_attn_dt_proj_w"].T)
        shared["l0_op_w"] = np.ascontiguousarray(inputs["L0_attn_out_proj_w"].T)
        shared["l0_gate_w"] = np.ascontiguousarray(inputs["L0_ffn_gate_w"].T)
        shared["l0_bbias_bc"] = np.broadcast_to(
            inputs["L0_ffn_balance_bias"], (128, 8)).copy()
        shared["l0_w1"] = np.ascontiguousarray(inputs["L0_ffn_w1"].transpose(0, 2, 1))
        shared["l0_w2"] = np.ascontiguousarray(inputs["L0_ffn_w2"].transpose(0, 2, 1))
    if do_l1:
        shared["l1_Wd"] = np.ascontiguousarray(inputs["L1_attn_Wd"])
        shared["l1_Wb"] = np.ascontiguousarray(inputs["L1_attn_Wb"])
        shared["l1_Wc"] = np.ascontiguousarray(inputs["L1_attn_Wc"])
        shared["l1_Wo"] = np.ascontiguousarray(inputs["L1_attn_Wo"])
        shared["l1_A"] = np.ascontiguousarray(inputs["L1_attn_A"])
        shared["l1_router_w"] = np.ascontiguousarray(inputs["L1_ffn_router_w"].T)
        shared["l1_rbias_bc"] = np.broadcast_to(inputs["L1_ffn_bias"],
                                                (128, 16)).copy()
        shared["l1_gate_w"] = np.ascontiguousarray(inputs["L1_ffn_gate_w"])
        shared["l1_up_w"] = np.ascontiguousarray(inputs["L1_ffn_up_w"])
        shared["l1_down_w"] = np.ascontiguousarray(inputs["L1_ffn_down_w"])
        shared["l1_shg_w"] = np.ascontiguousarray(inputs["L1_ffn_sh_gate_w"])
        shared["l1_shu_w"] = np.ascontiguousarray(inputs["L1_ffn_sh_up_w"])
        shared["l1_shd_w"] = np.ascontiguousarray(inputs["L1_ffn_sh_down_w"])

    if do_l0:
        x = np.ascontiguousarray(inputs["L0_attn_x"][0].T)
        xpad = np.concatenate([np.zeros((DM, HALO), np.float32), x], axis=1)

    in_maps = []
    for c in range(NCORE):
        m = dict(shared)
        oh = np.zeros((128, 8), np.float32)
        oh[:, c] = 1.0
        m["onehot8"] = oh
        if do_l0:
            m["x_in"] = np.ascontiguousarray(xpad[:, c * TOK:c * TOK + TOKH])
        else:
            m["x_mid_in"] = np.ascontiguousarray(x_mid[c])
        in_maps.append(m)
    return in_maps


def assemble_out(res):
    full = np.concatenate([res.results[c]["out_x"] for c in range(NCORE)], axis=1)
    return np.ascontiguousarray(full.T)[None]


def kernel(**inputs):
    inputs = {k: np.asarray(v) for k, v in inputs.items()}
    do_l1 = DO_L1
    key = (True, do_l1)
    if key not in _CACHE:
        _CACHE[key] = build_program(True, do_l1)
    nc, _ = _CACHE[key]
    in_maps = make_in_maps(inputs, True, do_l1)
    res = run_bass_kernel_spmd(nc, in_maps, core_ids=list(range(NCORE)))
    return assemble_out(res)


# version 06
# speedup vs baseline: 1.1037x; 1.0164x over previous
import numpy as np
import ml_dtypes
from contextlib import ExitStack

import concourse.bass as bass
import concourse.mybir as mybir
from concourse.bass_utils import run_bass_kernel_spmd

F32 = mybir.dt.float32
BF16 = mybir.dt.bfloat16
AF = mybir.ActivationFunctionType
ALU = mybir.AluOpType
AX = mybir.AxisListType

NCORE = 8
DM = 1024
TOK = 2048
HALO = 3
TOKH = TOK + HALO
NCH = DM // 128
DI = 2048
NCI = DI // 128
DT = 64
TB = 512
NTB = TOK // TB
TBS = [(i * TB, TB) for i in range(NTB)]
TBS_H = TBS + [(1540, 511)]
BIG = 1.0e30
EPS_LN = 1e-5

L0_NE, L0_DH = 8, 2048
DBG_NEX = None
DBG_NOSCALE = False
DBG_ALPHA_C = None
DBG_STOP = None
L1_NE, L1_DH = 16, 1024

DO_L1 = True


def build_program(do_l0=True, do_l1=True, dbg=()):
    nc = bass.Bass("TRN2", target_bir_lowering=False, debug=False, num_devices=NCORE)
    dbg = set(dbg)
    es = ExitStack()

    sem = {}
    cnt = {}
    for nm in ["ld", "wl", "wla", "wlb", "wl1", "wlu", "wl2",
               "pe", "ve", "ac", "po", "cc", "st"]:
        sem[nm] = es.enter_context(nc.semaphore(nm))
        cnt[nm] = 0

    def inc(name, inst, k=1):
        inst.then_inc(sem[name], k)
        cnt[name] += k
        return cnt[name]

    def wait(eng, name, val=None):
        eng.wait_ge(sem[name], cnt[name] if val is None else val)

    def vsafe(v, ins):
        v.wait_ge(sem["ve"], inc("ve", ins))

    inp = {}

    def dram_in(name, shape, dtype=F32):
        inp[name] = nc.dram_tensor(name, list(shape), dtype, kind="ExternalInput")
        return inp[name]

    if do_l0:
        dram_in("x_in", (DM, TOKH))
    else:
        dram_in("x_mid_in", (DM, TOK))
    out_x = nc.dram_tensor("out_x", [DM, TOK], F32, kind="ExternalOutput")

    dram_in("identity", (128, 128))
    dram_in("ones_col_bf", (128, 1), BF16)
    dram_in("ones_row_bf", (1, 128), BF16)
    dram_in("zeros_col", (128, 1))
    dram_in("one_col", (128, 1))
    dram_in("onehot8", (128, 8))
    dram_in("iota8eps", (128, 8))
    dram_in("iota16eps", (128, 16))
    dram_in("iota4eps", (128, 4))
    dram_in("neg1p", (128, 16))
    dram_in("colpack_in", (128, 160))

    if do_l0:
        dram_in("l0_ip_w", (DM, 2 * DI))
        dram_in("l0_xp_w", (DI, DT))
        dram_in("l0_dt_w", (DT, DI))
        dram_in("l0_op_w", (DI, DM))
        dram_in("l0_gate_w", (DM, L0_NE))
        dram_in("l0_bbias_bc", (128, L0_NE))
        dram_in("l0_w1", (L0_NE, DM, L0_DH))
        dram_in("l0_w2", (L0_NE, L0_DH, DM))
        cc0_in = nc.dram_tensor("cc0_in", [128, 32], F32, kind="Internal")
        cc0_out = nc.dram_tensor("cc0_out", [128 * NCORE, 32], F32, kind="Internal",
                                 addr_space="Local" if globals()["DBG_STOP"]
                                 else "Shared")
    if do_l1:
        dram_in("l1_Wd", (DM, DM))
        dram_in("l1_Wb", (DM, 16))
        dram_in("l1_Wc", (DM, 16))
        dram_in("l1_Wo", (DM, DM))
        dram_in("l1_A", (DM, 16))
        dram_in("l1_router_w", (DM, L1_NE))
        dram_in("l1_rbias_bc", (128, L1_NE))
        dram_in("l1_gate_w", (L1_NE, DM, L1_DH))
        dram_in("l1_up_w", (L1_NE, DM, L1_DH))
        dram_in("l1_down_w", (L1_NE, L1_DH, DM))
        dram_in("l1_shg_w", (DM, 2 * L1_DH))
        dram_in("l1_shu_w", (DM, 2 * L1_DH))
        dram_in("l1_shd_w", (2 * L1_DH, DM))
        cc1_in = nc.dram_tensor("cc1_in", [16, 2], F32, kind="Internal")
        cc1_out = nc.dram_tensor("cc1_out", [16 * NCORE, 2], F32, kind="Internal",
                                 addr_space="Shared")
    combT_d = nc.dram_tensor("combT_d", [16, TOK], F32, kind="Internal")

    dbg_outs = {}

    def dbg_add(name, ap_, shape, dtype=F32):
        if name in dbg:
            od = nc.dram_tensor("dbg_" + name, list(shape), dtype,
                                kind="ExternalOutput")
            dbg_outs[name] = (ap_, od)

    def dbg_snap(name, ap_, shape, dtype=F32):
        if name not in dbg:
            return
        od = nc.dram_tensor("dbg_" + name, list(shape), dtype,
                            kind="ExternalOutput")

        def sp(sync):
            wait(sync, "ve")
            wait(sync, "ac")
            wait(sync, "pe")
            inc("st", sync.dma_start(out=od.ap(), in_=ap_), 16)

        emit("sync", sp)
        stv = cnt["st"]
        emit("vector", lambda v: v.wait_ge(sem["st"], stv))
        emit("scalar", lambda s: s.wait_ge(sem["st"], stv))
        emit("tensor", lambda t: t.wait_ge(sem["st"], stv))
        emit("gpsimd", lambda g: g.wait_ge(sem["st"], stv))

    def sbt(name, shape, dtype):
        return es.enter_context(nc.sbuf_tensor(name, shape, dtype))

    x_res = sbt("x_res", [128, NCH * TOKH], F32)
    xn = sbt("xn", [128, NCH * TOKH], BF16)
    AR = sbt("AR", [128, 40960], BF16)
    SCR = sbt("SCR", [128, 4096], F32)
    ident = sbt("identity_sb", [128, 128], F32)
    ones_col = sbt("ones_col_sb", [128, 1], BF16)
    ones_row = sbt("ones_row_sb", [1, 128], BF16)
    zeros_col = sbt("zeros_col_sb", [128, 1], F32)
    one_col = sbt("one_col_sb", [128, 1], F32)
    onehot8 = sbt("onehot8_sb", [128, 8], F32)
    iota8 = sbt("iota8_sb", [128, 8], F32)
    iota16 = sbt("iota16_sb", [128, 16], F32)
    iota4 = sbt("iota4_sb", [128, 4], F32)
    neg1p = sbt("neg1p_sb", [128, 16], F32)
    colpack = sbt("colpack", [128, 160], F32)
    packF = sbt("packF", [128, 16], F32)
    packP = sbt("packP", [128, 16], F32)
    packR = sbt("packR", [128, 16], F32)
    S0 = sbt("S0", [128, 16], F32)
    agg = sbt("agg", [128, 32 * NCORE], F32)
    bias_bc = sbt("bias_bc", [128, 16], F32)
    comb = sbt("comb", [128, 16 * 16], F32)
    combT = sbt("combT", [16, 128], F32)
    cbc = sbt("cbc", [128, TOK], BF16)
    crow_bf = sbt("crow_bf", [1, TOK], BF16)
    aggL1 = sbt("aggL1", [16, 16], F32) if do_l1 else None
    cc1pack = sbt("cc1pack", [16, 2], F32) if do_l1 else None
    gw_sb = sbt("gw_sb", [128, NCH * 16], BF16)
    smallw = sbt("smallw", [128, 384], BF16) if do_l1 else None
    l1H = sbt("l1H", [16, 2], F32) if do_l1 else None

    psum = es.enter_context(nc.psum_tensor("psum", [128, 4096], F32))

    def bank(i, p0=0, np_=128, w=512):
        return psum[p0:p0 + np_, i * 512:i * 512 + w]

    U_OFF = 0
    DTL_OFF = 32768
    RING_OFF = 34816
    SZ_OFF = 38912
    CBT_OFF = 39936

    def Uc(j):
        return AR[:, U_OFF + j * TOK:U_OFF + (j + 1) * TOK]

    def ar(off, n):
        return AR[:, off:off + n]

    XP_OFF = 38912

    def xp_sb(j):
        return AR[:, XP_OFF + j * DT:XP_OFF + (j + 1) * DT]

    ROWS_OFF = 34816

    def row_b(i, n=TOKH, o=0):
        return AR[0:1, ROWS_OFF + i * TOKH + o:ROWS_OFF + i * TOKH + o + n]

    def s0(o=0, n=TOK):
        return SCR[:, o:o + n]

    def s1(o=0, n=TOK):
        return SCR[:, TOK + o:TOK + o + n]

    def sxz(o=0, n=TOKH):
        return SCR[:, o:o + n]

    def sconv(o=0, n=TOK):
        return SCR[:, o:o + n]



    def xs(c, o=0, n=TOKH):
        return x_res[:, c * TOKH + o:c * TOKH + o + n]

    def xsm(c, o=0, n=TOK):
        return x_res[:, c * TOKH + HALO + o:c * TOKH + HALO + o + n]

    def xnf(c, o=0, n=TOKH):
        return xn[:, c * TOKH + o:c * TOKH + o + n]

    def xnm(c, o=0, n=TOK):
        return xn[:, c * TOKH + HALO + o:c * TOKH + HALO + o + n]

    bh = {}

    _selfsem = {"vector": "ve", "scalar": "ac"}

    def emit(engine, fn):
        nm = _selfsem.get(engine)
        if nm is None:
            getattr(bh["block"], engine)(fn)
            return

        def wrapped(eng):
            eng.wait_ge(sem[nm], cnt[nm])
            fn(eng)

        getattr(bh["block"], engine)(wrapped)


    def ph_loads(sync):
        if do_l0:
            i0 = inp["x_in"].ap().rearrange("(c p) t -> p c t", p=128)
            d0 = x_res[:, :].rearrange("p (c t) -> p c t", c=NCH)
            inc("ld", sync.dma_start(out=d0, in_=i0), 16)
        else:
            for c in range(NCH):
                i0 = inp["x_mid_in"].ap()[c * 128:(c + 1) * 128, :]
                inc("ld", sync.dma_start(out=xsm(c), in_=i0), 16)
        for nm, dst in [("identity", ident), ("ones_col_bf", ones_col),
                        ("ones_row_bf", ones_row), ("zeros_col", zeros_col),
                        ("one_col", one_col), ("onehot8", onehot8),
                        ("iota8eps", iota8), ("iota16eps", iota16),
                        ("iota4eps", iota4), ("neg1p", neg1p),
                        ("colpack_in", colpack)]:
            inc("ld", sync.dma_start(out=dst[:, :], in_=inp[nm].ap()), 16)

    def ph_wloads_early(g):
        if do_l0:
            inc("wl", g.dma_start(
                out=gw_sb[:, 0:NCH * 8].rearrange("p (c e) -> p c e", c=NCH),
                in_=inp["l0_gate_w"].ap().rearrange("(c p) e -> p c e", p=128)), 16)

    def ph_colprep_ac(s):
        wait(s, "ld")
        inc("ac", s.activation(colpack[:, 0:16], colpack[:, 0:16], AF.Exp,
                               bias=0.0, scale=1.0))

    def ph_colprep_ve(v):
        wait(v, "ac")
        vsafe(v, v.tensor_scalar(colpack[:, 0:16], colpack[:, 0:16], -1.0, None,
                                 ALU.mult))
        inc("ve", v.tensor_scalar(colpack[:, 0:16], colpack[:, 0:16], -10.0, -1e-6,
                                  ALU.max, ALU.min))

    def emit_layernorm(full):
        tbs = TBS_H if full else TBS
        tokh = TOKH if full else TOK
        off = 0 if full else HALO
        SQ = 16408

        def xbf(c):
            return AR[:, c * tokh:(c + 1) * tokh]

        def sqbf(c):
            return AR[:, SQ + c * tokh:SQ + (c + 1) * tokh]

        def ve0(v):
            wait(v, "ld")
            wait(v, "ve")
            wait(v, "pe")
            for c in range(NCH):
                ins = v.tensor_copy(xbf(c), xs(c, off, tokh))
            inc("ve", ins)

        def ac0(s):
            wait(s, "ld")
            wait(s, "pe")
            wait(s, "ve")
            for c in range(NCH):
                ins = s.activation(sqbf(c), xs(c, off, tokh), AF.Square,
                                   bias=0.0, scale=1.0)
            inc("ac", ins)

        emit("vector", ve0)
        emit("scalar", ac0)
        ve_in, ac_in = cnt["ve"], cnt["ac"]

        pe_marks, ve_marks = [], []
        for bi, (t0, tw) in enumerate(tbs):
            bs, bq = bi % 2, 2 + bi % 2

            def pe(t, bi=bi, t0=t0, tw=tw, bs=bs, bq=bq):
                if bi == 0:
                    t.wait_ge(sem["ve"], ve_in)
                    t.wait_ge(sem["ac"], ac_in)
                if bi >= 2:
                    t.wait_ge(sem["ve"], ve_marks[bi - 2])
                for c in range(NCH):
                    t.matmul(bank(bs, 0, 1, tw), ones_col[:, :], xbf(c)[:, t0:t0 + tw],
                             start=(c == 0), stop=(c == NCH - 1))
                for c in range(NCH):
                    ins = t.matmul(bank(bq, 0, 1, tw), ones_col[:, :],
                                   sqbf(c)[:, t0:t0 + tw],
                                   start=(c == 0), stop=(c == NCH - 1))
                pe_marks.append(inc("pe", ins))

            def ve(v, bi=bi, t0=t0, tw=tw, bs=bs, bq=bq):
                v.wait_ge(sem["pe"], pe_marks[bi])
                m = row_b(0, tw, t0)
                q = row_b(1, tw, t0)
                t2 = SCR[0:1, 2048:2048 + tw]
                vsafe(v, v.tensor_scalar(m, bank(bs, 0, 1, tw), 1.0 / DM, None,
                                         ALU.mult))
                vsafe(v, v.tensor_scalar(q, bank(bq, 0, 1, tw), 1.0 / DM, None,
                                         ALU.mult))
                vsafe(v, v.tensor_mul(t2, m, m))
                vsafe(v, v.tensor_sub(q, q, t2))
                ins = v.tensor_scalar(q, q, EPS_LN, None, ALU.add)
                ve_marks.append(inc("ve", ins))

            emit("tensor", pe)
            emit("vector", ve)

        def ac1(s):
            wait(s, "ve")
            ins = s.activation(row_b(1, tokh), row_b(1, tokh), AF.Ln,
                               bias=0.0, scale=1.0)
            s.wait_ge(sem["ac"], inc("ac", ins))
            inc("ac", s.activation(row_b(1, tokh), row_b(1, tokh), AF.Exp,
                                   bias=0.0, scale=-0.5))

        emit("scalar", ac1)
        ac_rows = cnt["ac"]

        bc_marks, nrm_marks = [], []
        for bi, (t0, tw) in enumerate(tbs):
            bm, br = 4 + (bi % 2) * 2, 5 + (bi % 2) * 2

            def pe(t, bi=bi, t0=t0, tw=tw, bm=bm, br=br):
                if bi == 0:
                    t.wait_ge(sem["ac"], ac_rows)
                if bi >= 2:
                    t.wait_ge(sem["ve"], nrm_marks[bi - 2])
                t.matmul(bank(bm, 0, 128, tw), ones_row[:, :], row_b(0, tw, t0),
                         start=True, stop=True)
                ins = t.matmul(bank(br, 0, 128, tw), ones_row[:, :], row_b(1, tw, t0),
                               start=True, stop=True)
                bc_marks.append(inc("pe", ins))

            def ve(v, bi=bi, t0=t0, tw=tw, bm=bm, br=br):
                v.wait_ge(sem["pe"], bc_marks[bi])
                for c in range(NCH):
                    vsafe(v, v.tensor_sub(s0(0, tw), xs(c, off + t0, tw),
                                          bank(bm, 0, 128, tw)))
                    ins = v.tensor_mul(
                        xn[:, c * TOKH + off + t0:c * TOKH + off + t0 + tw],
                        s0(0, tw), bank(br, 0, 128, tw))
                    if c < NCH - 1:
                        vsafe(v, ins)
                nrm_marks.append(inc("ve", ins))

            emit("tensor", pe)
            emit("vector", ve)

    def emit_routing(layer):
        ne = L0_NE if layer == 0 else L1_NE
        bname = "l0_bbias_bc" if layer == 0 else "l1_rbias_bc"
        gwc = 8 if layer == 0 else 16
        ln_ve = cnt["ve"]

        def po(g):
            wait(g, "ve", ln_ve)
            if layer == 1:
                inc("wl", g.dma_start(
                    out=gw_sb[:, 0:NCH * 16].rearrange("p (c e) -> p c e", c=NCH),
                    in_=inp["l1_router_w"].ap().rearrange("(c p) e -> p c e", p=128)),
                    16)
            inc("wl", g.dma_start(out=bias_bc[:, 0:ne], in_=inp[bname].ap()), 16)

        emit("gpsimd", po)
        wl_done = cnt["wl"]

        def rb(tt, o, n):
            return SCR[:, tt * 256 + o:tt * 256 + o + n]

        pe_m, ve1_m, ac_m = {}, {}, {}
        for tt in range(16):
            bk = tt % 2

            def pe(t, tt=tt, bk=bk):
                if tt == 0:
                    t.wait_ge(sem["wl"], wl_done)
                    t.wait_ge(sem["ve"], ln_ve)
                if tt >= 2:
                    t.wait_ge(sem["ve"], ve1_m[tt - 2])
                for c in range(NCH):
                    ins = t.matmul(bank(bk, 0, 128, ne), xnm(c, tt * 128, 128),
                                   gw_sb[:, c * gwc:c * gwc + ne],
                                   start=(c == 0), stop=(c == NCH - 1))
                pe_m[tt] = inc("pe", ins)

            def ve1(v, tt=tt, bk=bk):
                v.wait_ge(sem["pe"], pe_m[tt])
                if layer == 0:
                    ins = v.tensor_add(rb(tt, 0, ne), bank(bk, 0, 128, ne),
                                       bias_bc[:, 0:ne])
                else:
                    ins = v.tensor_copy(rb(tt, 0, ne), bank(bk, 0, 128, ne))
                ve1_m[tt] = inc("ve", ins)

            def ac1(s, tt=tt):
                s.wait_ge(sem["ve"], ve1_m[tt])
                ins = s.activation(rb(tt, 0, ne), rb(tt, 0, ne), AF.Sigmoid,
                                   bias=0.0, scale=1.0)
                ac_m[tt] = inc("ac", ins)

            emit("tensor", pe)
            emit("vector", ve1)
            emit("scalar", ac1)

        def ve2(v):
            wait(v, "ac", ac_m[15])
            T16 = range(16)

            def sc(tt):
                return rb(tt, 0, ne)

            def each(f):
                ins = None
                for tt in T16:
                    ins = f(tt)
                return ins

            def step(f):
                ins = each(f)
                v.wait_ge(sem["ve"], inc("ve", ins))
                return ins

            if layer == 0:
                sp = lambda tt: rb(tt, 32, ne)
                u = lambda tt: rb(tt, 64, ne)
                m0 = lambda tt: rb(tt, 96, 1)
                m0b = lambda tt: rb(tt, 97, 1)
                den = lambda tt: rb(tt, 98, 1)
                rec = lambda tt: rb(tt, 99, 1)
                step(lambda tt: v.tensor_sub(sp(tt), sc(tt), iota8[:, 0:ne]))
                step(lambda tt: v.tensor_reduce(
                    m0(tt), sp(tt).rearrange("p (a e) -> p a e", a=1), AX.X, ALU.max))
                step(lambda tt: v.tensor_single_scalar(u(tt), sp(tt), m0(tt),
                                                       ALU.is_ge))
                step(lambda tt: v.scalar_tensor_tensor(sp(tt), u(tt), -BIG, sp(tt),
                                                       ALU.mult, ALU.add))
                step(lambda tt: v.tensor_reduce(
                    m0b(tt), sp(tt).rearrange("p (a e) -> p a e", a=1), AX.X, ALU.max))
                step(lambda tt: v.tensor_single_scalar(sp(tt), sp(tt), m0b(tt),
                                                       ALU.is_ge))
                step(lambda tt: v.tensor_add(u(tt), u(tt), sp(tt)))
                step(lambda tt: v.tensor_mul(sp(tt), sc(tt), u(tt)))
                step(lambda tt: v.tensor_reduce(
                    den(tt), sp(tt).rearrange("p (a e) -> p a e", a=1), AX.X, ALU.add))
                step(lambda tt: v.tensor_scalar(den(tt), den(tt), 1e-6, None,
                                                ALU.add))
                step(lambda tt: v.reciprocal(rec(tt), den(tt)))
                ins = each(lambda tt: v.tensor_single_scalar(
                    comb[:, tt * 16:tt * 16 + ne], sp(tt), rec(tt), ALU.mult))
            else:
                ss = lambda tt: rb(tt, 32, ne)
                u = lambda tt: rb(tt, 64, ne)
                ge = lambda tt: rb(tt, 96, ne)
                wk = lambda tt: rb(tt, 128, ne)
                g1 = lambda tt: rb(tt, 160, 4)
                g2 = lambda tt: rb(tt, 168, 4)
                g3 = lambda tt: rb(tt, 176, 4)
                m0 = lambda tt: rb(tt, 184, 1)
                m1 = lambda tt: rb(tt, 185, 1)
                step(lambda tt: v.tensor_add(ss(tt), sc(tt), bias_bc[:, 0:ne]))
                step(lambda tt: v.tensor_sub(ss(tt), ss(tt), iota16[:, 0:ne]))
                step(lambda tt: v.tensor_reduce(
                    g1(tt), ss(tt).rearrange("p (g r) -> p g r", g=4), AX.X, ALU.max))
                for jj in range(4):
                    step(lambda tt, jj=jj: v.tensor_copy(
                        ge(tt)[:, 4 * jj:4 * jj + 4],
                        g1(tt)[:, jj:jj + 1].broadcast_to([128, 4])))
                step(lambda tt: v.tensor_sub(wk(tt), ss(tt), ge(tt)))
                step(lambda tt: v.tensor_scalar(u(tt), wk(tt), -1e-12, None,
                                                ALU.is_ge))
                step(lambda tt: v.scalar_tensor_tensor(wk(tt), u(tt), -BIG, ss(tt),
                                                       ALU.mult, ALU.add))
                step(lambda tt: v.tensor_reduce(
                    g2(tt), wk(tt).rearrange("p (g r) -> p g r", g=4), AX.X, ALU.max))
                step(lambda tt: v.tensor_add(g1(tt), g1(tt), g2(tt)))
                step(lambda tt: v.tensor_sub(g1(tt), g1(tt), iota4[:, :]))
                step(lambda tt: v.tensor_reduce(
                    m0(tt), g1(tt).rearrange("p (a e) -> p a e", a=1), AX.X, ALU.max))
                step(lambda tt: v.tensor_single_scalar(g2(tt), g1(tt), m0(tt),
                                                       ALU.is_ge))
                step(lambda tt: v.scalar_tensor_tensor(g3(tt), g2(tt), -BIG, g1(tt),
                                                       ALU.mult, ALU.add))
                step(lambda tt: v.tensor_reduce(
                    m0(tt), g3(tt).rearrange("p (a e) -> p a e", a=1), AX.X, ALU.max))
                step(lambda tt: v.tensor_single_scalar(g3(tt), g1(tt), m0(tt),
                                                       ALU.is_ge))
                for jj in range(4):
                    step(lambda tt, jj=jj: v.tensor_copy(
                        ge(tt)[:, 4 * jj:4 * jj + 4],
                        g3(tt)[:, jj:jj + 1].broadcast_to([128, 4])))
                step(lambda tt: v.tensor_sub(wk(tt), ss(tt), neg1p[:, 0:ne]))
                step(lambda tt: v.tensor_mul(wk(tt), wk(tt), ge(tt)))
                step(lambda tt: v.tensor_add(wk(tt), wk(tt), neg1p[:, 0:ne]))
                step(lambda tt: v.tensor_single_scalar(u(tt), wk(tt), 0.0, ALU.add))
                for _ in range(3):
                    step(lambda tt: v.tensor_reduce(
                        m0(tt), u(tt).rearrange("p (a e) -> p a e", a=1), AX.X,
                        ALU.max))
                    step(lambda tt: v.tensor_single_scalar(ge(tt), u(tt), m0(tt),
                                                           ALU.is_ge))
                    step(lambda tt: v.scalar_tensor_tensor(u(tt), ge(tt), -BIG,
                                                           u(tt), ALU.mult, ALU.add))
                step(lambda tt: v.tensor_reduce(
                    m0(tt), u(tt).rearrange("p (a e) -> p a e", a=1), AX.X, ALU.max))
                step(lambda tt: v.tensor_single_scalar(u(tt), wk(tt), m0(tt),
                                                       ALU.is_ge))
                step(lambda tt: v.tensor_mul(wk(tt), sc(tt), u(tt)))
                step(lambda tt: v.tensor_reduce(
                    m0(tt), wk(tt).rearrange("p (a e) -> p a e", a=1), AX.X, ALU.add))
                step(lambda tt: v.tensor_scalar(m0(tt), m0(tt), 1e-20, None,
                                                ALU.add))
                step(lambda tt: v.reciprocal(m1(tt), m0(tt)))
                ins = each(lambda tt: v.tensor_scalar(
                    comb[:, tt * 16:tt * 16 + ne], wk(tt), m1(tt), 2.5,
                    ALU.mult, ALU.mult))
            inc("ve", ins)

        emit("vector", ve2)
        ve2_all = cnt["ve"]

        tr_pe, tr_ve, tr_po = {}, {}, {}
        for tt in range(16):
            bk = 4 + tt % 2

            def pe(t, tt=tt, bk=bk):
                if tt == 0:
                    t.wait_ge(sem["ve"], ve2_all)
                if tt >= 2:
                    t.wait_ge(sem["ve"], tr_ve[tt - 2])
                ins = t.transpose(bank(bk, 0, ne, 128), comb[:, tt * 16:tt * 16 + ne],
                                  ident[:, :])
                tr_pe[tt] = inc("pe", ins)

            def ve(v, tt=tt, bk=bk):
                v.wait_ge(sem["pe"], tr_pe[tt])
                if tt >= 1:
                    v.wait_ge(sem["po"], tr_po[tt - 1])
                ins = v.tensor_copy(combT[0:ne, :], bank(bk, 0, ne, 128))
                tr_ve[tt] = inc("ve", ins)

            def po2(g, tt=tt):
                g.wait_ge(sem["ve"], tr_ve[tt])
                ins = g.dma_start(out=combT_d.ap()[0:ne, tt * 128:(tt + 1) * 128],
                                  in_=combT[0:ne, :])
                tr_po[tt] = inc("po", ins, 16)

            emit("tensor", pe)
            emit("vector", ve)
            emit("gpsimd", po2)

    def emit_moe(layer):
        if layer == 0:
            nhc = L0_DH // 128
            W1_OFF, W2_OFF, H_OFF, SIL_OFF = 0, 16384, 32768, None
            WU_OFF = None
            experts = [("l0_w1", "l0_w2", None, e, True) for e in range(L0_NE)]
            if DBG_NEX:
                experts = experts[:DBG_NEX]
            dh = L0_DH
        else:
            nhc = L1_DH // 128
            W1_OFF, WU_OFF, W2_OFF, H_OFF, SIL_OFF = 0, 8192, 16384, 24576, 32768
            experts = [("l1_gate_w", "l1_down_w", "l1_up_w", e, True)
                       for e in range(L1_NE)]
            experts += [("l1_shg_w", "l1_shd_w", "l1_shu_w", sh, False)
                        for sh in range(2)]
            dh = L1_DH
        nex = len(experts)

        npar = 1 if layer == 0 else 2

        def Htile(par, hc, tw):
            o = H_OFF + (par % npar) * nhc * TB + hc * TB
            return AR[:, o:o + tw]

        ld_m = [None] * nex
        ld2_m = [None] * nex
        last_a = [0, 0]
        emit_moe.crow_ct = [0]
        emit_moe.crow_war = {0: 0, 1: 0}
        emit_moe.crow_po = {0: 0, 1: 0}
        alpha_done = [None] * nex
        beta_done = [None] * nex
        ev_m = {}
        hready = {}
        hfree = {}
        cb_last = [0]

        def emit_load(ei):
            g1n, g2n, gun, e, is_real = experts[ei]

            def po(g, ei=ei, g1n=g1n, g2n=g2n, gun=gun, e=e, is_real=is_real):
                if ei == 0:
                    wait(g, "pe")
                    wait(g, "ve")
                if ei >= 1 and alpha_done[ei - 1] is not None:
                    g.wait_ge(sem["pe"], alpha_done[ei - 1])
                if layer == 0 or is_real:
                    src = inp[g1n].ap()[e]
                else:
                    src = inp[g1n].ap()[:, e * L1_DH:(e + 1) * L1_DH]
                ins = g.dma_start(
                    out=ar(W1_OFF, NCH * dh).rearrange("p (c h) -> p c h", c=NCH),
                    in_=src.rearrange("(c p) h -> p c h", p=128))
                inc("wl1", ins, 16)
                if layer == 1:
                    srcu = inp[gun].ap()[e] if is_real else \
                        inp[gun].ap()[:, e * L1_DH:(e + 1) * L1_DH]
                    ins = g.dma_start(
                        out=ar(WU_OFF, NCH * dh).rearrange("p (c h) -> p c h", c=NCH),
                        in_=srcu.rearrange("(c p) h -> p c h", p=128))
                    inc("wlu", ins, 16)
                ld_m[ei] = (cnt["wl1"], cnt["wlu"])
                if ei >= 1 and beta_done[ei - 1] is not None:
                    g.wait_ge(sem["pe"], beta_done[ei - 1])
                if layer == 0 or is_real:
                    srcd = inp[g2n].ap()[e]
                else:
                    srcd = inp[g2n].ap()[e * L1_DH:(e + 1) * L1_DH, :]
                ins = g.dma_start(
                    out=ar(W2_OFF, nhc * DM).rearrange("p (h d) -> p h d", h=nhc),
                    in_=srcd.rearrange("(h p) d -> p h d", p=128))
                ld2_m[ei] = inc("wl2", ins, 16)

            emit("gpsimd", po)

        emit_load(0)
        rt_po = cnt["po"]

        for ei in range(nex):
            g1n, g2n, gun, e, is_real = experts[ei]

            if is_real:
                cb_pe = {}
                cb_vem = {}
                par_e = 0

                def poc(g, ei=ei, e=e, par_e=par_e):
                    if ei == 0:
                        g.wait_ge(sem["po"], rt_po)
                    if emit_moe.crow_war[par_e]:
                        g.wait_ge(sem["pe"], emit_moe.crow_war[par_e])
                    emit_moe.crow_po[par_e] = inc("po", g.dma_start(
                        out=crow_bf[0:1, :], in_=combT_d.ap()[e:e + 1, :]), 16)

                emit("gpsimd", poc)

                def pec(t, ei=ei, par_e=par_e):
                    ve_base = cnt["ve"]
                    t.wait_ge(sem["po"], emit_moe.crow_po[par_e])
                    for bi, (t0, tw) in enumerate(TBS):
                        if ei == 0 and bi < 2:
                            wait(t, "ve")
                        elif bi >= 2:
                            t.wait_ge(sem["ve"], ve_base + bi - 1)
                        elif cb_last[0]:
                            t.wait_ge(sem["ve"], cb_last[0])
                        ins = t.matmul(bank(6 + bi % 2, 0, 128, tw), ones_row[:, :],
                                       crow_bf[0:1, t0:t0 + tw],
                                       start=True, stop=True)
                        cb_pe[bi] = inc("pe", ins)
                    emit_moe.crow_war[par_e] = cnt["pe"]

                def vec(v):
                    for bi, (t0, tw) in enumerate(TBS):
                        v.wait_ge(sem["pe"], cb_pe[bi])
                        ins = v.tensor_copy(cbc[:, t0:t0 + tw],
                                            bank(6 + bi % 2, 0, 128, tw))
                        cb_vem[bi] = inc("ve", ins)
                    cb_last[0] = cnt["ve"]

                emit("tensor", pec)
                emit("vector", vec)
                emit_moe.crow_ct[0] += 1
            cb_ve = cnt["ve"]

            if layer == 0:
                sched = [("a", 0), ("b", 0), ("a", 1), ("b", 1), ("a", 2),
                         ("b", 2), ("a", 3), ("b", 3)]
            else:
                sched = [("a", 0), ("a", 1), ("b", 0), ("a", 2), ("b", 1),
                         ("a", 3), ("b", 2), ("b", 3)]
            al_marks = {}
            for kind, bi in sched:
                t0, tw = TBS[bi]
                par = bi % 2
                if kind == "a":
                    al_pe, al_ac, al_ve = {}, {}, {}
                    al_marks[bi] = (al_ac, al_ve)

                    def pea(t, ei=ei, bi=bi, t0=t0, tw=tw,
                            al_pe=al_pe, cb_ve=cb_ve, is_real=is_real):
                        ac_base = cnt["ac"]
                        ve_base = cnt["ve"]
                        if layer == 0:
                            nve = 1 if is_real else 0
                        else:
                            nve = 2 if is_real else 1
                        if bi == 0:
                            t.wait_ge(sem["wl1"], ld_m[ei][0])
                            if layer == 1:
                                t.wait_ge(sem["wlu"], ld_m[ei][1])
                            t.wait_ge(sem["ve"], cb_ve)
                        if bi >= npar:
                            t.wait_ge(sem["pe"], hfree[(ei, bi - npar)])
                        if last_a[0]:
                            t.wait_ge(sem["ac"], last_a[0])
                        if last_a[1]:
                            t.wait_ge(sem["ve"], last_a[1])
                        for hc in range(nhc):
                            if layer == 0:
                                bkg, bku = 4 + (hc % 2), None
                            else:
                                bkg, bku = 4 + (hc % 2) * 2, 5 + (hc % 2) * 2
                            if hc >= 2:
                                t.wait_ge(sem["ac"], ac_base + hc - 1)
                                if layer == 1 and nve:
                                    t.wait_ge(sem["ve"], ve_base + (hc - 1) * nve)
                            crange = (list(range(NCH)) if globals()["DBG_ALPHA_C"]
                                      is None else globals()["DBG_ALPHA_C"])
                            for ci, c in enumerate(crange):
                                ins = t.matmul(
                                    bank(bkg, 0, 128, tw),
                                    AR[:, W1_OFF + c * dh + hc * 128:
                                       W1_OFF + c * dh + hc * 128 + 128],
                                    xnm(c, t0, tw),
                                    start=(ci == 0), stop=(ci == len(crange) - 1))
                            al_pe[hc] = inc("pe", ins)
                            if layer == 1:
                                for c in range(NCH):
                                    ins = t.matmul(
                                        bank(bku, 0, 128, tw),
                                        AR[:, WU_OFF + c * dh + hc * 128:
                                           WU_OFF + c * dh + hc * 128 + 128],
                                        xnm(c, t0, tw),
                                        start=(c == 0), stop=(c == NCH - 1))
                                al_pe[hc] = inc("pe", ins)
                        if bi == 3:
                            alpha_done[ei] = cnt["pe"]

                    def aca(s, ei=ei, bi=bi, tw=tw, par=par,
                            al_pe=al_pe, al_ac=al_ac):
                        if layer == 1 and bi >= 2 and (ei, bi - 2) in hready:
                            s.wait_ge(sem["ve"], hready[(ei, bi - 2)][1])
                        if layer == 0 and bi >= 1:
                            s.wait_ge(sem["pe"], hfree[(ei, bi - 1)])
                        for hc in range(nhc):
                            bkg = 4 + (hc % 2) if layer == 0 else 4 + (hc % 2) * 2
                            s.wait_ge(sem["pe"], al_pe[hc])
                            if layer == 0:
                                dst = Htile(par, hc, tw)
                            else:
                                dst = AR[:, SIL_OFF + hc * TB:SIL_OFF + hc * TB + tw]
                            ins = s.activation(dst, bank(bkg, 0, 128, tw), AF.Silu,
                                               bias=0.0, scale=1.0)
                            al_ac[hc] = inc("ac", ins)

                    def vea(v, ei=ei, bi=bi, t0=t0, tw=tw, par=par, is_real=is_real,
                            al_ac=al_ac, al_ve=al_ve):
                        ins = None
                        for hc in range(nhc):
                            v.wait_ge(sem["ac"], al_ac[hc])
                            h = Htile(par, hc, tw)
                            if layer == 0:
                                if is_real and not globals()['DBG_NOSCALE']:
                                    ins = v.tensor_mul(h, h, cbc[:, t0:t0 + tw])
                            else:
                                bku = 5 + (hc % 2) * 2
                                sg = AR[:, SIL_OFF + hc * TB:SIL_OFF + hc * TB + tw]
                                ins = v.tensor_mul(h, bank(bku, 0, 128, tw), sg)
                                if is_real:
                                    vsafe(v, ins)
                                    ins = v.tensor_mul(h, h, cbc[:, t0:t0 + tw])
                            if ins is not None:
                                al_ve[hc] = inc("ve", ins)
                            else:
                                al_ve[hc] = cnt["ve"]

                    emit("tensor", pea)
                    emit("scalar", aca)
                    emit("vector", vea)
                    hready[(ei, bi)] = (cnt["ac"], cnt["ve"])
                    last_a[0], last_a[1] = cnt["ac"], cnt["ve"]
                    if ei == 0 and bi == 0 and layer == 0:
                        dbg_snap("H00", AR[:, H_OFF:H_OFF + nhc * TB],
                                 (128, nhc * TB), BF16)
                        dbg_snap("xnsnap", xn[:, :], (128, NCH * TOKH), BF16)
                        dbg_snap("w1snap", AR[:, 0:8192], (128, 8192), BF16)
                else:
                    for dq in range(2):
                        be_pe = {}

                        def peb(t, ei=ei, bi=bi, t0=t0, tw=tw, par=par, dq=dq,
                                be_pe=be_pe):
                            al_ac2, al_ve2 = al_marks[bi]
                            if dq == 0 and bi == 0:
                                t.wait_ge(sem["wl2"], ld2_m[ei])
                            for dc4 in range(4):
                                if dq == 1:
                                    km = ev_m.get((ei, bi, "q0"))
                                elif bi > 0:
                                    km = ev_m.get((ei, bi - 1, dc4))
                                elif ei > 0:
                                    km = ev_m.get((ei - 1, NTB - 1, dc4))
                                else:
                                    km = None
                                if km:
                                    t.wait_ge(sem["ve"], km)
                                    break
                            for hc in range(nhc):
                                if dq == 0:
                                    t.wait_ge(sem["ac"], al_ac2[hc])
                                    t.wait_ge(sem["ve"], al_ve2[hc])
                                for dc4 in range(4):
                                    dc = dq * 4 + dc4
                                    ins = t.matmul(
                                        bank(dc4, 0, 128, tw),
                                        AR[:, W2_OFF + hc * DM + dc * 128:
                                           W2_OFF + hc * DM + dc * 128 + 128],
                                        Htile(par, hc, tw),
                                        start=(hc == 0), stop=(hc == nhc - 1))
                            be_pe[0] = inc("pe", ins)
                            if dq == 1:
                                hfree[(ei, bi)] = cnt["pe"]
                                if bi == 3:
                                    beta_done[ei] = cnt["pe"]

                        def veb(v, ei=ei, bi=bi, t0=t0, tw=tw, dq=dq, be_pe=be_pe):
                            v.wait_ge(sem["pe"], be_pe[0])
                            for dc4 in range(4):
                                dc = dq * 4 + dc4
                                ins = v.tensor_add(xsm(dc, t0, tw), xsm(dc, t0, tw),
                                                   bank(dc4, 0, 128, tw))
                                inc("ve", ins)
                            if dq == 0:
                                ev_m[(ei, bi, "q0")] = cnt["ve"]
                            else:
                                for dc4 in range(4):
                                    ev_m[(ei, bi, dc4)] = cnt["ve"]

                        emit("tensor", peb)
                        emit("vector", veb)

            if ei + 1 < nex:
                emit_load(ei + 1)

    def emit_l0_passA():
        ln_ve = cnt["ve"]
        ipw_m = {}
        pe_m, ac_m, ve_m = {}, {}, {}
        def xzh(j):
            return SCR[:, 2560 + 3 * (j % 2):2560 + 3 * (j % 2) + 3]

        def po0(g):
            wait(g, "ve", ln_ve)
            wait(g, "pe")
            inc("wl", g.dma_start(
                out=AR[:, XP_OFF:XP_OFF + NCI * DT]
                .rearrange("p (c r) -> p c r", c=NCI),
                in_=inp["l0_xp_w"].ap().rearrange("(c p) r -> p c r", p=128)), 16)
            for j in range(2):
                ins = g.dma_start(
                    out=ar(RING_OFF + (j % 2) * 1024, 1024)
                    .rearrange("p (c m) -> p c m", c=NCH),
                    in_=inp["l0_ip_w"].ap()[:, j * 128:(j + 1) * 128]
                    .rearrange("(c p) m -> p c m", p=128))
                ipw_m[j] = inc("wla" if j % 2 == 0 else "wlb", ins, 16)

        emit("gpsimd", po0)

        for j in range(NCI):
            if j >= 2:
                def po(g, j=j):
                    g.wait_ge(sem["pe"], pe_m[(j - 2, len(TBS_H) - 1)])
                    ins = g.dma_start(
                        out=ar(RING_OFF + (j % 2) * 1024, 1024)
                        .rearrange("p (c m) -> p c m", c=NCH),
                        in_=inp["l0_ip_w"].ap()[:, j * 128:(j + 1) * 128]
                        .rearrange("(c p) m -> p c m", p=128))
                    ipw_m[j] = inc("wla" if j % 2 == 0 else "wlb", ins, 16)

                emit("gpsimd", po)

            for bi, (t0, tw) in enumerate(TBS_H):
                g_idx = j * len(TBS_H) + bi
                bk = g_idx % 3

                def pe(t, j=j, bi=bi, t0=t0, tw=tw, bk=bk, g_idx=g_idx):
                    if bi == 0:
                        t.wait_ge(sem["wla" if j % 2 == 0 else "wlb"], ipw_m[j])
                        if j == 0:
                            t.wait_ge(sem["ve"], ln_ve)
                        if j >= 1:
                            km = ve_m.get((j - 2, "conv"))
                            if km:
                                t.wait_ge(sem["ve"], km)
                    if g_idx >= 3:
                        pj, pb = divmod(g_idx - 3, len(TBS_H))
                        t.wait_ge(sem["ac"], ac_m[(pj, pb)])
                    for c in range(NCH):
                        ins = t.matmul(bank(bk, 0, 128, tw),
                                       ar(RING_OFF + (j % 2) * 1024, 1024)
                                       [:, c * 128:(c + 1) * 128],
                                       xnf(c, t0, tw),
                                       start=(c == 0), stop=(c == NCH - 1))
                    pe_m[(j, bi)] = inc("pe", ins)

                def ac(s, j=j, bi=bi, t0=t0, tw=tw, bk=bk):
                    s.wait_ge(sem["pe"], pe_m[(j, bi)])
                    if bi == 0 and j >= 1:
                        km = ve_m.get((j - 2, "conv"))
                        if km:
                            s.wait_ge(sem["ve"], km)
                    if bi == len(TBS_H) - 1:
                        ins = s.copy(Uc(j)[:, 1537:2048], bank(bk, 0, 128, tw))
                    elif bi == 0:
                        s.copy(xzh(j), bank(bk, 0, 128, 3))
                        ins = s.copy(Uc(j)[:, 0:509], bank(bk, 0, 128, tw)[:, 3:512])
                    else:
                        ins = s.copy(Uc(j)[:, t0 - 3:t0 - 3 + tw],
                                     bank(bk, 0, 128, tw))
                    ac_m[(j, bi)] = inc("ac", ins)

                emit("tensor", pe)
                emit("scalar", ac)

            def ve(v, j=j):
                wait(v, "ac", ac_m[(j, len(TBS_H) - 1)])
                if j >= 1:
                    v.wait_ge(sem["ac"], ac_m[(j - 1, "silu")])
                first = True
                for k in range(4):
                    cwk = colpack[:, 64 + 16 * k + j:64 + 16 * k + j + 1]
                    hw = 3 - k
                    if hw > 0:
                        if first:
                            vsafe(v, v.tensor_single_scalar(
                                sconv(0, hw), xzh(j)[:, k:3], cwk, ALU.mult))
                        else:
                            vsafe(v, v.scalar_tensor_tensor(
                                sconv(0, hw), xzh(j)[:, k:3], cwk, sconv(0, hw),
                                ALU.mult, ALU.add))
                    mw = TOK - max(0, hw)
                    if first:
                        vsafe(v, v.tensor_single_scalar(
                            sconv(max(0, hw), mw), Uc(j)[:, 0:mw], cwk, ALU.mult))
                        first = False
                    else:
                        ins = v.scalar_tensor_tensor(
                            sconv(max(0, hw), mw), Uc(j)[:, 0:mw], cwk,
                            sconv(max(0, hw), mw), ALU.mult, ALU.add)
                        if k < 3:
                            vsafe(v, ins)
                ve_m[(j, "conv")] = inc("ve", ins)

            def ac2(s, j=j):
                wait(s, "ve", ve_m[(j, "conv")])
                cb = colpack[:, 32 + j:32 + j + 1]
                ins = s.activation(Uc(j), sconv(), AF.Silu, bias=cb, scale=1.0)
                ac_m[(j, "silu")] = inc("ac", ins)

            emit("vector", ve)
            emit("scalar", ac2)

            if j >= 1:
                def pexp(t, jj=j - 1):
                    if jj == 0:
                        wait(t, "wl")
                    t.wait_ge(sem["ac"], ac_m[(jj, "silu")])
                    for bi, (t0, tw) in enumerate(TBS):
                        ins = t.matmul(bank(4 + bi, 0, 64, tw),
                                       xp_sb(jj),
                                       Uc(jj)[:, t0:t0 + tw],
                                       start=(jj == 0), stop=(jj == NCI - 1))
                    inc("pe", ins)

                emit("tensor", pexp)

        def pexpl(t):
            jj = NCI - 1
            t.wait_ge(sem["ac"], ac_m[(jj, "silu")])
            for bi, (t0, tw) in enumerate(TBS):
                ins = t.matmul(bank(4 + bi, 0, 64, tw),
                               xp_sb(jj),
                               Uc(jj)[:, t0:t0 + tw],
                               start=(jj == 0), stop=(jj == NCI - 1))
            inc("pe", ins)

        emit("tensor", pexpl)
        pe_xp = cnt["pe"]

        def podtw(g):
            g.wait_ge(sem["pe"], pe_xp)
            inc("wl", g.dma_start(out=AR[0:64, 38912:38912 + DI],
                                  in_=inp["l0_dt_w"].ap()), 16)

        emit("gpsimd", podtw)

        def vedt(v):
            v.wait_ge(sem["pe"], pe_xp)
            for bi, (t0, tw) in enumerate(TBS):
                ins = v.tensor_copy(AR[0:64, DTL_OFF + t0:DTL_OFF + t0 + tw],
                                    bank(4 + bi, 0, 64, tw))
            inc("ve", ins)

        emit("vector", vedt)

    dc_war = {}

    def emit_delta_chain(j, accum_R, s0free=None):
        bks = (0, 4)[j % 2]
        marks = {}

        def pe(t, j=j, bks=bks):
            if bks in dc_war:
                t.wait_ge(sem["ac"], dc_war[bks])
            sw = dc_war.get("scan%d" % (j % 2))
            if sw:
                t.wait_ge(sem["ve"], sw)
            if j <= 1:
                wait(t, "ve")
                wait(t, "wl")
            for bi, (t0, tw) in enumerate(TBS):
                ins = t.matmul(bank(bks + bi, 0, 128, tw),
                               AR[0:64, 38912 + j * 128:38912 + (j + 1) * 128],
                               AR[0:64, DTL_OFF + t0:DTL_OFF + t0 + tw],
                               start=True, stop=True)
            marks["pe"] = inc("pe", ins)

        def ac(s, j=j, bks=bks):
            s.wait_ge(sem["pe"], marks["pe"])
            if s0free is not None:
                s.wait_ge(sem["ve"], s0free)
            dtb = colpack[:, 48 + j:48 + j + 1]
            for bi, (t0, tw) in enumerate(TBS):
                ins = s.activation(s0(t0, tw), bank(bks + bi, 0, 128, tw), AF.Exp,
                                   bias=dtb, scale=1.0)
            s.wait_ge(sem["ac"], inc("ac", ins))
            ins = s.activation(s0(), s0(), AF.Ln, bias=1.0, scale=1.0)
            marks["ac"] = inc("ac", ins)
            dc_war[bks] = marks["ac"]

        def ve(v, j=j):
            wait(v, "ac", marks["ac"])
            if accum_R:
                vsafe(v, v.tensor_scalar(s0(), s0(), 1e-6, None, ALU.max))
                vsafe(v, v.tensor_scalar(s0(), s0(), 10.0, None, ALU.min, ALU.add,
                                         accum_out=packR[:, j:j + 1]))
            else:
                vsafe(v, v.tensor_scalar(s0(), s0(), 1e-6, 10.0, ALU.max, ALU.min))
            Acol = colpack[:, j:j + 1]
            ins = v.tensor_single_scalar(s0(), s0(), Acol, ALU.mult)
            marks["ve"] = inc("ve", ins)

        def ac2(s, j=j):
            wait(s, "ve", marks["ve"])
            ins = s.activation(s0(), s0(), AF.Exp, bias=0.0, scale=1.0)
            marks["ac2"] = inc("ac", ins)

        emit("tensor", pe)
        emit("scalar", ac)
        emit("vector", ve)
        emit("scalar", ac2)
        return marks

    def emit_l0_prepass():
        prev_s0free = None
        for j in range(NCI):
            marks = emit_delta_chain(j, accum_R=True, s0free=prev_s0free)

            def ve(v, j=j, marks=marks):
                bks = (0, 4)[j % 2]
                wait(v, "ac", marks["ac2"])
                pscan = psum[:, bks * 512:bks * 512 + TOK]
                vsafe(v, v.tensor_tensor_scan(pscan, s0(), Uc(j), zeros_col[:, 0:1],
                                              ALU.mult, ALU.add))
                ins = v.tensor_copy(packF[:, j:j + 1], pscan[:, TOK - 1:TOK])
                marks["scan"] = inc("ve", ins)

            emit("vector", ve)
            prev_s0free = cnt["ve"]
            dc_war["scan%d" % (j % 2)] = cnt["ve"]

        def vp(v):
            wait(v, "ve")
            ins = v.tensor_mul(packP[:, :], packR[:, :], colpack[:, 0:16])
            inc("ve", ins)

        def acp(s):
            wait(s, "ve")
            inc("ac", s.activation(packP[:, :], packP[:, :], AF.Exp,
                                   bias=0.0, scale=1.0))

        def vpack(v):
            wait(v, "ac")
            wait(v, "ve")
            v.tensor_copy(agg[:, 0:16], packF[:, :])
            inc("ve", v.tensor_copy(agg[:, 16:32], packP[:, :]))

        emit("vector", vp)
        emit("scalar", acp)
        emit("vector", vpack)
        ve_pack = cnt["ve"]

        def po(g):
            g.wait_ge(sem["ve"], ve_pack)
            inc("po", g.dma_start(out=cc0_in.ap(), in_=agg[:, 0:32]), 16)
            wait(g, "po")
            inc("cc", g.collective_compute(
                "AllGather", ALU.bypass, replica_groups=[list(range(NCORE))],
                ins=[cc0_in.ap().opt()], outs=[cc0_out.ap().opt()]), 1)
            wait(g, "cc")
            for r in range(NCORE):
                inc("po", g.dma_start(out=agg[:, 32 * r:32 * (r + 1)],
                                      in_=cc0_out.ap()[128 * r:128 * (r + 1), :]), 16)

        emit("gpsimd", po)
        po_done = cnt["po"]

        def vcomb(v):
            v.wait_ge(sem["po"], po_done)
            vsafe(v, v.memset(S0[:, :], 0.0))
            vsafe(v, v.memset(packR[:, :], 0.0))
            for r in range(NCORE):
                vsafe(v, v.scalar_tensor_tensor(S0[:, :], packR[:, :],
                                                onehot8[:, r:r + 1], S0[:, :],
                                                ALU.mult, ALU.add))
                if r < NCORE - 1:
                    vsafe(v, v.tensor_mul(packP[:, :],
                                          agg[:, 32 * r + 16:32 * r + 32],
                                          packR[:, :]))
                    vsafe(v, v.tensor_add(packR[:, :], packP[:, :],
                                          agg[:, 32 * r:32 * r + 16]))

        emit("vector", vcomb)

    def emit_l0_passC1():
        emit_l0_passC1.s0_ready = cnt["ve"]
        prev_s0free = None
        for j in range(NCI):
            marks = emit_delta_chain(j, accum_R=False, s0free=prev_s0free)

            def ve(v, j=j, marks=marks):
                bks = (0, 4)[j % 2]
                wait(v, "ac", marks["ac2"])
                if j == 0:
                    wait(v, "ve", emit_l0_passC1.s0_ready)
                pscan = psum[:, bks * 512:bks * 512 + TOK]
                vsafe(v, v.tensor_tensor_scan(pscan, s0(), Uc(j), S0[:, j:j + 1],
                                              ALU.mult, ALU.add))
                Dcol = colpack[:, 16 + j:16 + j + 1]
                vsafe(v, v.scalar_tensor_tensor(s1(), Uc(j), Dcol, pscan,
                                                ALU.mult, ALU.add))
                vsafe(v, v.tensor_scalar(s1(), s1(), -1e4, 1e4, ALU.max, ALU.min))
                ins = v.tensor_copy(Uc(j), s1())
                inc("ve", ins)

            emit("vector", ve)
            prev_s0free = cnt["ve"]
            dc_war["scan%d" % (j % 2)] = cnt["ve"]

    def emit_l0_passC2():
        zm = {}
        for j in range(NCI):
            def po(g, j=j):
                if j < 2:
                    wait(g, "pe")
                else:
                    g.wait_ge(sem["pe"], zm[(j - 2, NTB - 1, "pe")])
                ins = g.dma_start(
                    out=ar(RING_OFF + (j % 2) * 1024, 1024)
                    .rearrange("p (c m) -> p c m", c=NCH),
                    in_=inp["l0_ip_w"].ap()[:, (NCI + j) * 128:(NCI + j + 1) * 128]
                    .rearrange("(c p) m -> p c m", p=128))
                zm[(j, "wl")] = inc("wla" if j % 2 == 0 else "wlb", ins, 16)

            emit("gpsimd", po)

            for bi, (t0, tw) in enumerate(TBS):
                bk = bi % 2
                sz = SZ_OFF + (bi % 2) * TB

                def pe(t, j=j, bi=bi, t0=t0, tw=tw, bk=bk):
                    if bi == 0:
                        t.wait_ge(sem["wla" if j % 2 == 0 else "wlb"],
                                  zm[(j, "wl")])
                        if j == 0:
                            wait(t, "ve")
                            wait(t, "ac")
                    prev = (j, bi - 2, "ac") if bi >= 2 else (j - 1, bi + 2, "ac")
                    if prev in zm:
                        t.wait_ge(sem["ac"], zm[prev])
                    for c in range(NCH):
                        ins = t.matmul(bank(bk, 0, 128, tw),
                                       ar(RING_OFF + (j % 2) * 1024, 1024)
                                       [:, c * 128:(c + 1) * 128],
                                       xnm(c, t0, tw),
                                       start=(c == 0), stop=(c == NCH - 1))
                    zm[(j, bi, "pe")] = inc("pe", ins)

                def ac(s, j=j, bi=bi, tw=tw, bk=bk, sz=sz):
                    s.wait_ge(sem["pe"], zm[(j, bi, "pe")])
                    prev = (j, bi - 2, "ve") if bi >= 2 else (j - 1, bi + 2, "ve")
                    if prev in zm:
                        s.wait_ge(sem["ve"], zm[prev])
                    ins = s.activation(ar(sz, tw), bank(bk, 0, 128, tw),
                                       AF.Silu, bias=0.0, scale=1.0)
                    zm[(j, bi, "ac")] = inc("ac", ins)

                def ve(v, j=j, bi=bi, t0=t0, tw=tw, sz=sz):
                    wait(v, "ac", zm[(j, bi, "ac")])
                    ins = v.tensor_mul(Uc(j)[:, t0:t0 + tw], Uc(j)[:, t0:t0 + tw],
                                       ar(sz, tw))
                    zm[(j, bi, "ve")] = inc("ve", ins)

                emit("tensor", pe)
                emit("scalar", ac)
                emit("vector", ve)

    def emit_l0_outproj():
        opm = {}
        cve = cnt["ve"]
        for bi, (t0, tw) in enumerate(TBS):
            for j in range(NCI):
                def po(g, bi=bi, j=j):
                    prev = (bi, j - 2) if j >= 2 else ((bi - 1, NCI - 2 + j)
                                                       if bi > 0 else None)
                    if prev and (prev[0], prev[1], "pe") in opm:
                        g.wait_ge(sem["pe"], opm[(prev[0], prev[1], "pe")])
                    elif prev is None:
                        wait(g, "pe")
                    ins = g.dma_start(
                        out=ar(RING_OFF + (j % 2) * 1024, 1024),
                        in_=inp["l0_op_w"].ap()[j * 128:(j + 1) * 128, :])
                    opm[(bi, j, "wl")] = inc("wla" if j % 2 == 0 else "wlb", ins, 16)

                def pe(t, bi=bi, j=j, t0=t0, tw=tw):
                    t.wait_ge(sem["wla" if j % 2 == 0 else "wlb"],
                              opm[(bi, j, "wl")])
                    if bi == 0 and j == 0:
                        wait(t, "ve", cve)
                    if j == 0 and bi > 0:
                        for dc in range(NCH):
                            t.wait_ge(sem["ve"], opm[(bi - 1, dc, "ev")])
                    for dc in range(NCH):
                        ins = t.matmul(bank(dc, 0, 128, tw),
                                       ar(RING_OFF + (j % 2) * 1024, 1024)
                                       [:, dc * 128:(dc + 1) * 128],
                                       Uc(j)[:, t0:t0 + tw],
                                       start=(j == 0), stop=(j == NCI - 1))
                    opm[(bi, j, "pe")] = inc("pe", ins)

                emit("gpsimd", po)
                emit("tensor", pe)

            def ve(v, bi=bi, t0=t0, tw=tw):
                v.wait_ge(sem["pe"], opm[(bi, NCI - 1, "pe")])
                for dc in range(NCH):
                    ins = v.tensor_add(xsm(dc, t0, tw), xsm(dc, t0, tw),
                                       bank(dc, 0, 128, tw))
                    opm[(bi, dc, "ev")] = inc("ve", ins)

            emit("vector", ve)

    def emit_l1_attn_pre():
        WD_OFF = 0
        DBAR_R = 8192
        A_R = 10240
        SZT = 12288
        A16_OFF = 12800
        B16_OFF = 14848
        PROW_OFF = 16896
        ZROW_OFF = 18944
        PC_OFF = 36864

        def Rhs_(o, n=TOK):
            return SCR[0:16, o:o + n]

        R_hs = SCR[0:16, 0:TOK]
        R_ct = SCR[0:16, TOK:2 * TOK]

        ve_at = cnt["ve"]

        def po(g):
            wait(g, "ve", ve_at)
            wait(g, "pe")
            inc("wl1", g.dma_start(
                out=ar(WD_OFF, NCH * 1024).rearrange("p (c o) -> p c o", c=NCH),
                in_=inp["l1_Wd"].ap().rearrange("(c p) o -> p c o", p=128)), 16)
            inc("wl", g.dma_start(
                out=smallw[:, 0:128].rearrange("p (c e) -> p c e", c=NCH),
                in_=inp["l1_Wb"].ap().rearrange("(c p) e -> p c e", p=128)), 16)
            inc("wl", g.dma_start(
                out=smallw[:, 128:256].rearrange("p (c e) -> p c e", c=NCH),
                in_=inp["l1_Wc"].ap().rearrange("(c p) e -> p c e", p=128)), 16)
            inc("wl", g.dma_start(
                out=smallw[:, 256:384].rearrange("p (c e) -> p c e", c=NCH),
                in_=inp["l1_A"].ap().rearrange("(c p) e -> p c e", p=128)), 16)

        emit("gpsimd", po)
        wl1_wd = cnt["wl1"]
        wl_small = cnt["wl"]

        def vz(v):
            inc("ve", v.memset(AR[0:1, ZROW_OFF:ZROW_OFF + TOK], 0.0))

        emit("vector", vz)

        sa_pe = {}

        def pesa(t):
            t.wait_ge(sem["wl"], wl_small)
            wait(t, "ve")
            wait(t, "ac")
            for c in range(NCH):
                ins = t.matmul(bank(3, 0, 16, 1), smallw[:, 256 + 16 * c:272 + 16 * c],
                               ones_col[:, :], start=(c == 0), stop=(c == NCH - 1))
            sa_pe[0] = inc("pe", ins)

        def acsa(s):
            s.wait_ge(sem["pe"], sa_pe[0])
            ins = s.activation(l1H[:, 1:2], bank(3, 0, 16, 1), AF.Sigmoid,
                               bias=0.0, scale=1.0 / DM)
            inc("ac", ins)

        emit("tensor", pesa)
        emit("scalar", acsa)
        sa_ac = cnt["ac"]

        dm = {}
        for bi, (t0, tw) in enumerate(TBS):
            for dc in range(NCH):
                bk = dc % 2

                def pe(t, bi=bi, dc=dc, t0=t0, tw=tw, bk=bk):
                    if bi == 0 and dc == 0:
                        t.wait_ge(sem["wl1"], wl1_wd)
                    prev = dm.get((bi, dc - 2, "ac")) or \
                        (dm.get((bi - 1, NCH - 2 + dc, "ac")) if bi > 0 else None)
                    if prev:
                        t.wait_ge(sem["ac"], prev)
                    for c in range(NCH):
                        ins = t.matmul(bank(bk, 0, 128, tw),
                                       AR[:, WD_OFF + c * 1024 + dc * 128:
                                          WD_OFF + c * 1024 + dc * 128 + 128],
                                       xnm(c, t0, tw),
                                       start=(c == 0), stop=(c == NCH - 1))
                    dm[(bi, dc, "pe")] = inc("pe", ins)

                def ac(s, bi=bi, dc=dc, tw=tw, bk=bk):
                    s.wait_ge(sem["pe"], dm[(bi, dc, "pe")])
                    bd = colpack[:, 128 + dc:129 + dc]
                    ins = s.activation(SCR[:, 2048:2048 + tw], bank(bk, 0, 128, tw),
                                       AF.Exp, bias=bd, scale=1.0)
                    s.wait_ge(sem["ac"], inc("ac", ins))
                    ins = s.activation(AR[:, SZT:SZT + tw], SCR[:, 2048:2048 + tw],
                                       AF.Ln, bias=1.0, scale=1.0)
                    dm[(bi, dc, "ac")] = inc("ac", ins)

                def pe2(t, bi=bi, dc=dc, t0=t0, tw=tw):
                    t.wait_ge(sem["ac"], dm[(bi, dc, "ac")])
                    if dc == 0 and bi > 0:
                        t.wait_ge(sem["ac"], dm[(bi - 1, "dbar")])
                    ins = t.matmul(bank(2, 0, 1, tw), ones_col[:, :],
                                   AR[:, SZT:SZT + tw],
                                   start=(dc == 0), stop=(dc == NCH - 1))
                    dm[(bi, dc, "pe2")] = inc("pe", ins)

                emit("tensor", pe)
                emit("scalar", ac)
                emit("tensor", pe2)

            def acd(s, bi=bi, t0=t0, tw=tw):
                s.wait_ge(sem["pe"], dm[(bi, NCH - 1, "pe2")])
                ins = s.activation(AR[0:1, DBAR_R + t0:DBAR_R + t0 + tw],
                                   bank(2, 0, 1, tw), AF.Copy, bias=0.0,
                                   scale=1.0 / DM)
                dm[(bi, "dbar")] = inc("ac", ins)

            emit("scalar", acd)

        def va(v):
            wait(v, "ac")
            ins = v.tensor_scalar(AR[0:1, A_R:A_R + TOK],
                                  AR[0:1, DBAR_R:DBAR_R + TOK], -1.0, 1.0,
                                  ALU.mult, ALU.add)
            inc("ve", ins)

        emit("vector", va)

        bc_m = {}
        for which, woff, bcol, dst in [("b", 0, 144, R_hs), ("c", 128, 145, R_ct)]:
            for bi, (t0, tw) in enumerate(TBS):
                bk = 4 + bi % 2

                def pe(t, which=which, woff=woff, bi=bi, t0=t0, tw=tw, bk=bk):
                    prev = bc_m.get((which, bi - 2, "ac")) or \
                        (bc_m.get(("b", bi + 2, "ac")) if which == "c" else None)
                    if prev:
                        t.wait_ge(sem["ve"], prev)
                    for c in range(NCH):
                        ins = t.matmul(bank(bk, 0, 16, tw),
                                       smallw[:, woff + 16 * c:woff + 16 * c + 16],
                                       xnm(c, t0, tw),
                                       start=(c == 0), stop=(c == NCH - 1))
                    bc_m[(which, bi, "pe")] = inc("pe", ins)

                def vv(v, which=which, bi=bi, t0=t0, tw=tw, bk=bk, bcol=bcol,
                       dst=dst):
                    v.wait_ge(sem["pe"], bc_m[(which, bi, "pe")])
                    ins = v.tensor_scalar(dst[:, t0:t0 + tw], bank(bk, 0, 16, tw),
                                          colpack[0:16, bcol:bcol + 1], None,
                                          ALU.add)
                    bc_m[(which, bi, "ac")] = inc("ve", ins)

                emit("tensor", pe)
                emit("vector", vv)

        d16_m = {}
        for bi, (t0, tw) in enumerate(TBS):
            bk = 6 + bi % 2

            def pe(t, bi=bi, t0=t0, tw=tw, bk=bk):
                wait(t, "ac") if bi == 0 else None
                prev = d16_m.get((bi - 2, "ve"))
                if prev:
                    t.wait_ge(sem["ve"], prev)
                ins = t.matmul(bank(bk, 0, 16, tw), ones_row[0:1, 0:16],
                               AR[0:1, DBAR_R + t0:DBAR_R + t0 + tw],
                               start=True, stop=True)
                d16_m[(bi, "pe")] = inc("pe", ins)

            def ve(v, bi=bi, t0=t0, tw=tw, bk=bk):
                v.wait_ge(sem["pe"], d16_m[(bi, "pe")])
                if bi == 0:
                    wait(v, "ac")
                vsafe(v, v.tensor_scalar(
                    AR[0:16, A16_OFF + t0:A16_OFF + t0 + tw],
                    bank(bk, 0, 16, tw), -1.0, 1.0, ALU.mult, ALU.add))
                b16 = AR[0:16, B16_OFF + t0:B16_OFF + t0 + tw]
                vsafe(v, v.scalar_tensor_tensor(
                    b16, R_hs[:, t0:t0 + tw], l1H[:, 1:2], bank(bk, 0, 16, tw),
                    ALU.mult, ALU.mult))
                ins = v.engine_nop()
                d16_m[(bi, "ve")] = cnt["ve"]

            emit("tensor", pe)
            emit("vector", ve)

        def vsc(v):
            wait(v, "ac")
            wait(v, "ve")
            vsafe(v, v.tensor_tensor_scan(
                R_hs[:, :], AR[0:16, A16_OFF:A16_OFF + TOK],
                AR[0:16, B16_OFF:B16_OFF + TOK], zeros_col[0:16, 0:1],
                ALU.mult, ALU.add))
            vsafe(v, v.tensor_tensor_scan(
                AR[0:1, PROW_OFF:PROW_OFF + TOK], AR[0:1, A_R:A_R + TOK],
                AR[0:1, ZROW_OFF:ZROW_OFF + TOK], one_col[0:1, 0:1],
                ALU.mult, ALU.add))
            ins = v.tensor_copy(cc1pack[:, 0:1], R_hs[:, TOK - 1:TOK])
            inc("ve", ins)

        emit("vector", vsc)
        sc_ve = cnt["ve"]

        pc_m = {}
        for bi, (t0, tw) in enumerate(TBS):
            bk = 6 + bi % 2

            def pe(t, bi=bi, t0=t0, tw=tw, bk=bk):
                if bi == 0:
                    t.wait_ge(sem["ve"], sc_ve)
                prev = pc_m.get((bi - 2, "ve"))
                if prev:
                    t.wait_ge(sem["ve"], prev)
                ins = t.matmul(bank(bk, 0, 16, tw), ones_row[0:1, 0:16],
                               AR[0:1, PROW_OFF + t0:PROW_OFF + t0 + tw],
                               start=True, stop=True)
                pc_m[(bi, "pe")] = inc("pe", ins)

            def ve(v, bi=bi, t0=t0, tw=tw, bk=bk):
                v.wait_ge(sem["pe"], pc_m[(bi, "pe")])
                ins = v.tensor_copy(AR[0:16, PC_OFF + t0:PC_OFF + t0 + tw],
                                    bank(bk, 0, 16, tw))
                pc_m[(bi, "ve")] = inc("ve", ins)

            emit("tensor", pe)
            emit("vector", ve)

        def pct(t):
            wait(t, "ve")
            ins = t.matmul(bank(3, 0, 16, 1), ones_row[0:1, 0:16],
                           AR[0:1, PROW_OFF + TOK - 1:PROW_OFF + TOK],
                           start=True, stop=True)
            pc_m["ct"] = inc("pe", ins)

        def vct(v):
            v.wait_ge(sem["pe"], pc_m["ct"])
            ins = v.tensor_copy(cc1pack[:, 1:2], bank(3, 0, 16, 1))
            inc("ve", ins)

        emit("tensor", pct)
        emit("vector", vct)
        pack_ve = cnt["ve"]

        def po2(g):
            g.wait_ge(sem["ve"], pack_ve)
            inc("po", g.dma_start(out=cc1_in.ap(), in_=cc1pack[:, :]), 16)
            wait(g, "po")
            inc("cc", g.collective_compute(
                "AllGather", ALU.bypass, replica_groups=[list(range(NCORE))],
                ins=[cc1_in.ap().opt()], outs=[cc1_out.ap().opt()]), 1)

        emit("gpsimd", po2)

        def spagg(sync):
            wait(sync, "cc")
            ins = sync.dma_start(
                out=aggL1[:, :].rearrange("p (r j) -> p r j", r=NCORE),
                in_=cc1_out.ap().rearrange("(r p) j -> p r j", p=16))
            inc("ld", ins, 16)

        emit("sync", spagg)

    def emit_l1_attn_fin():
        PC_OFF = 36864
        YSCBF_OFF = 36864
        PRD_OFF = 38912
        R_hs = SCR[0:16, 0:TOK]
        R_ct = SCR[0:16, TOK:2 * TOK]

        def vcomb(v):
            wait(v, "ld")
            vsafe(v, v.memset(l1H[:, 0:1], 0.0))
            vsafe(v, v.memset(cc1pack[:, 0:1], 0.0))
            for r in range(NCORE):
                vsafe(v, v.scalar_tensor_tensor(l1H[:, 0:1], cc1pack[:, 0:1],
                                                onehot8[0:16, r:r + 1], l1H[:, 0:1],
                                                ALU.mult, ALU.add))
                if r < NCORE - 1:
                    vsafe(v, v.tensor_mul(cc1pack[:, 1:2],
                                          aggL1[:, 2 * r + 1:2 * r + 2],
                                          cc1pack[:, 0:1]))
                    vsafe(v, v.tensor_add(cc1pack[:, 0:1], cc1pack[:, 1:2],
                                          aggL1[:, 2 * r:2 * r + 1]))
            for bi, (t0, tw) in enumerate(TBS):
                vsafe(v, v.scalar_tensor_tensor(
                    R_hs[:, t0:t0 + tw], AR[0:16, PC_OFF + t0:PC_OFF + t0 + tw],
                    l1H[:, 0:1], R_hs[:, t0:t0 + tw], ALU.mult, ALU.add))
            ins = v.tensor_mul(AR[0:16, PRD_OFF:PRD_OFF + TOK], R_hs[:, :],
                               R_ct[:, :])
            inc("ve", ins)

        emit("vector", vcomb)
        prod_ve = cnt["ve"]

        ys_m = {}
        for bi, (t0, tw) in enumerate(TBS):
            def pe(t, bi=bi, t0=t0, tw=tw):
                if bi == 0:
                    t.wait_ge(sem["ve"], prod_ve)
                ins = t.matmul(bank(bi, 0, 1, tw), ones_col[0:16, 0:1],
                               AR[0:16, PRD_OFF + t0:PRD_OFF + t0 + tw],
                               start=True, stop=True)
                ys_m[(bi, "pe")] = inc("pe", ins)

            def ve(v, bi=bi, t0=t0, tw=tw):
                v.wait_ge(sem["pe"], ys_m[(bi, "pe")])
                ins = v.tensor_copy(AR[0:1, YSCBF_OFF + t0:YSCBF_OFF + t0 + tw],
                                    bank(bi, 0, 1, tw))
                ys_m[(bi, "ve")] = inc("ve", ins)

            emit("tensor", pe)
            emit("vector", ve)
        ysc_ve = cnt["ve"]

        yb_m = {}
        for bi, (t0, tw) in enumerate(TBS):
            bk = 4 + bi % 2

            def pe(t, bi=bi, t0=t0, tw=tw, bk=bk):
                if bi == 0:
                    t.wait_ge(sem["ve"], ysc_ve)
                prev = yb_m.get((bi - 2, "ve"))
                if prev:
                    t.wait_ge(sem["ve"], prev)
                ins = t.matmul(bank(bk, 0, 128, tw), ones_row[:, :],
                               AR[0:1, YSCBF_OFF + t0:YSCBF_OFF + t0 + tw],
                               start=True, stop=True)
                yb_m[(bi, "pe")] = inc("pe", ins)

            def ve(v, bi=bi, t0=t0, tw=tw, bk=bk):
                v.wait_ge(sem["pe"], yb_m[(bi, "pe")])
                ins = v.tensor_copy(SCR[:, t0:t0 + tw], bank(bk, 0, 128, tw))
                yb_m[(bi, "ve")] = inc("ve", ins)

            emit("tensor", pe)
            emit("vector", ve)

        def po(g):
            wait(g, "pe")
            wait(g, "ve")
            inc("wl1", g.dma_start(
                out=ar(0, NCH * 1024).rearrange("p (c o) -> p c o", c=NCH),
                in_=inp["l1_Wo"].ap().rearrange("(c p) o -> p c o", p=128)), 16)

        emit("gpsimd", po)
        wo_wl = cnt["wl1"]

        wo_m = {}
        for bi, (t0, tw) in enumerate(TBS):
            def pe(t, bi=bi, t0=t0, tw=tw):
                if bi == 0:
                    t.wait_ge(sem["wl1"], wo_wl)
                    wait(t, "ve")
                if bi > 0:
                    for dc in range(NCH):
                        t.wait_ge(sem["ve"], wo_m[(bi - 1, "ev")])
                        break
                for dj in range(NCH):
                    for dc in range(NCH):
                        ins = t.matmul(bank(dc, 0, 128, tw),
                                       AR[:, dj * 1024 + dc * 128:
                                          dj * 1024 + dc * 128 + 128],
                                       xnm(dj, t0, tw),
                                       start=(dj == 0), stop=(dj == NCH - 1))
                wo_m[(bi, "pe")] = inc("pe", ins)

            def ve(v, bi=bi, t0=t0, tw=tw):
                v.wait_ge(sem["pe"], wo_m[(bi, "pe")])
                for dc in range(NCH):
                    vsafe(v, v.tensor_mul(SCR[:, 2048:2048 + tw],
                                          bank(dc, 0, 128, tw), SCR[:, t0:t0 + tw]))
                    bo = colpack[:, 136 + dc:137 + dc]
                    ins = v.scalar_tensor_tensor(xsm(dc, t0, tw),
                                                 SCR[:, 2048:2048 + tw], bo,
                                                 xsm(dc, t0, tw), ALU.add, ALU.add)
                    if dc < NCH - 1:
                        vsafe(v, ins)
                wo_m[(bi, "ev")] = inc("ve", ins)

            emit("tensor", pe)
            emit("vector", ve)

    def emit_store():
        def sp(sync):
            wait(sync, "ve")
            wait(sync, "ac")
            wait(sync, "pe")
            for c in range(NCH):
                inc("st", sync.dma_start(out=out_x.ap()[c * 128:(c + 1) * 128, :],
                                         in_=xsm(c)), 16)
            for name, (ap_, od) in dbg_outs.items():
                inc("st", sync.dma_start(out=od.ap(), in_=ap_), 16)
            sync.wait_ge(sem["st"], cnt["st"])

        emit("sync", sp)

    with nc.Block() as block:
        bh["block"] = block
        emit("sync", ph_loads)
        emit("gpsimd", ph_wloads_early)
        emit("scalar", ph_colprep_ac)
        emit("vector", ph_colprep_ve)

        if do_l0:
            with nc.named_scope("ln0"):
                emit_layernorm(full=True)
            dbg_add("xn0", xn[:, :], (128, NCH * TOKH), BF16)
            if globals()["DBG_STOP"] != "ln":
                with nc.named_scope("rt0"):
                    emit_routing(0)
                dbg_add("comb0", comb[:, :], (128, 256))
            if globals()["DBG_STOP"] not in ("ln", "rt"):
                with nc.named_scope("moe0"):
                    emit_moe(0)
                dbg_snap("x_moe0", x_res[:, :], (128, NCH * TOKH))
        if do_l0 and globals()["DBG_STOP"] is None:
            with nc.named_scope("passA"):
                emit_l0_passA()
            dbg_add("dtlow", AR[0:64, DTL_OFF:DTL_OFF + TOK], (64, TOK), BF16)
            dbg_snap("xi", AR[:, U_OFF:U_OFF + NCI * TOK], (128, NCI * TOK), BF16)
            with nc.named_scope("prepass"):
                emit_l0_prepass()
            dbg_add("packF", packF[:, :], (128, 16))
            dbg_add("packP", packP[:, :], (128, 16))
            dbg_add("S0", S0[:, :], (128, 16))
            with nc.named_scope("passC1"):
                emit_l0_passC1()
            dbg_add("y0", AR[:, U_OFF:U_OFF + NCI * TOK], (128, NCI * TOK), BF16)
            with nc.named_scope("passC2"):
                emit_l0_passC2()
            with nc.named_scope("outproj"):
                emit_l0_outproj()
            dbg_snap("x_l0s", x_res[:, :], (128, NCH * TOKH))

        if do_l1 and globals()["DBG_STOP"] is None:
            with nc.named_scope("ln1"):
                emit_layernorm(full=False)
            dbg_snap("xn1", xn[:, :], (128, NCH * TOKH), BF16)
            with nc.named_scope("rt1"):
                emit_routing(1)
            dbg_add("comb1", comb[:, :], (128, 256))
            with nc.named_scope("attn1pre"):
                emit_l1_attn_pre()
            with nc.named_scope("moe1"):
                emit_moe(1)
            dbg_snap("x_moe1", x_res[:, :], (128, NCH * TOKH))
            with nc.named_scope("attn1fin"):
                emit_l1_attn_fin()

        emit_store()

    es.close()
    return nc, dbg_outs


_CACHE = {}


def _prep_const():
    c = {}
    c["identity"] = np.eye(128, dtype=np.float32)
    c["ones_col_bf"] = np.ones((128, 1), dtype=ml_dtypes.bfloat16)
    c["ones_row_bf"] = np.ones((1, 128), dtype=ml_dtypes.bfloat16)
    c["zeros_col"] = np.zeros((128, 1), dtype=np.float32)
    c["one_col"] = np.ones((128, 1), dtype=np.float32)
    c["iota8eps"] = np.broadcast_to((np.arange(8) * 1e-6).astype(np.float32),
                                    (128, 8)).copy()
    c["iota16eps"] = np.broadcast_to((np.arange(16) * 1e-6).astype(np.float32),
                                     (128, 16)).copy()
    c["iota4eps"] = np.broadcast_to((np.arange(4) * 1e-6).astype(np.float32),
                                    (128, 4)).copy()
    c["neg1p"] = np.broadcast_to((-1.0 - np.arange(16) * 1e-6).astype(np.float32),
                                 (128, 16)).copy()
    return c


def _colpack(inputs, do_l0, do_l1):
    cp = np.zeros((128, 160), dtype=np.float32)
    if do_l0:
        cp[:, 0:16] = inputs["L0_attn_A_log"].reshape(16, 128).T
        cp[:, 16:32] = inputs["L0_attn_D"].reshape(16, 128).T
        cp[:, 32:48] = inputs["L0_attn_conv_b"].reshape(16, 128).T
        cp[:, 48:64] = inputs["L0_attn_dt_proj_b"].reshape(16, 128).T
        cw = inputs["L0_attn_conv_w"].reshape(2048, 4)
        for k in range(4):
            cp[:, 64 + 16 * k:64 + 16 * (k + 1)] = cw[:, k].reshape(16, 128).T
    if do_l1:
        cp[:, 128:136] = inputs["L1_attn_bd"].reshape(8, 128).T
        cp[:, 136:144] = inputs["L1_attn_bo"].reshape(8, 128).T
        cp[0:16, 144] = inputs["L1_attn_bb"]
        cp[0:16, 145] = inputs["L1_attn_bc"]
    return cp


def make_in_maps(inputs, do_l0=True, do_l1=True, x_mid=None):
    const = _prep_const()
    shared = dict(const)
    shared["colpack_in"] = _colpack(inputs, do_l0, do_l1)
    if do_l0:
        shared["l0_ip_w"] = np.ascontiguousarray(inputs["L0_attn_in_proj_w"].T)
        shared["l0_xp_w"] = np.ascontiguousarray(inputs["L0_attn_x_proj_w"].T)
        shared["l0_dt_w"] = np.ascontiguousarray(inputs["L0_attn_dt_proj_w"].T)
        shared["l0_op_w"] = np.ascontiguousarray(inputs["L0_attn_out_proj_w"].T)
        shared["l0_gate_w"] = np.ascontiguousarray(inputs["L0_ffn_gate_w"].T)
        shared["l0_bbias_bc"] = np.broadcast_to(
            inputs["L0_ffn_balance_bias"], (128, 8)).copy()
        shared["l0_w1"] = np.ascontiguousarray(inputs["L0_ffn_w1"].transpose(0, 2, 1))
        shared["l0_w2"] = np.ascontiguousarray(inputs["L0_ffn_w2"].transpose(0, 2, 1))
    if do_l1:
        shared["l1_Wd"] = np.ascontiguousarray(inputs["L1_attn_Wd"])
        shared["l1_Wb"] = np.ascontiguousarray(inputs["L1_attn_Wb"])
        shared["l1_Wc"] = np.ascontiguousarray(inputs["L1_attn_Wc"])
        shared["l1_Wo"] = np.ascontiguousarray(inputs["L1_attn_Wo"])
        shared["l1_A"] = np.ascontiguousarray(inputs["L1_attn_A"])
        shared["l1_router_w"] = np.ascontiguousarray(inputs["L1_ffn_router_w"].T)
        shared["l1_rbias_bc"] = np.broadcast_to(inputs["L1_ffn_bias"],
                                                (128, 16)).copy()
        shared["l1_gate_w"] = np.ascontiguousarray(inputs["L1_ffn_gate_w"])
        shared["l1_up_w"] = np.ascontiguousarray(inputs["L1_ffn_up_w"])
        shared["l1_down_w"] = np.ascontiguousarray(inputs["L1_ffn_down_w"])
        shared["l1_shg_w"] = np.ascontiguousarray(inputs["L1_ffn_sh_gate_w"])
        shared["l1_shu_w"] = np.ascontiguousarray(inputs["L1_ffn_sh_up_w"])
        shared["l1_shd_w"] = np.ascontiguousarray(inputs["L1_ffn_sh_down_w"])

    if do_l0:
        x = np.ascontiguousarray(inputs["L0_attn_x"][0].T)
        xpad = np.concatenate([np.zeros((DM, HALO), np.float32), x], axis=1)

    in_maps = []
    for c in range(NCORE):
        m = dict(shared)
        oh = np.zeros((128, 8), np.float32)
        oh[:, c] = 1.0
        m["onehot8"] = oh
        if do_l0:
            m["x_in"] = np.ascontiguousarray(xpad[:, c * TOK:c * TOK + TOKH])
        else:
            m["x_mid_in"] = np.ascontiguousarray(x_mid[c])
        in_maps.append(m)
    return in_maps


def assemble_out(res):
    full = np.concatenate([res.results[c]["out_x"] for c in range(NCORE)], axis=1)
    return np.ascontiguousarray(full.T)[None]


def kernel(**inputs):
    inputs = {k: np.asarray(v) for k, v in inputs.items()}
    do_l1 = DO_L1
    key = (True, do_l1)
    if key not in _CACHE:
        _CACHE[key] = build_program(True, do_l1)
    nc, _ = _CACHE[key]
    in_maps = make_in_maps(inputs, True, do_l1)
    res = run_bass_kernel_spmd(nc, in_maps, core_ids=list(range(NCORE)))
    return assemble_out(res)


# version 07 (final)
# speedup vs baseline: 1.1385x; 1.0315x over previous
import numpy as np
import ml_dtypes
from contextlib import ExitStack

import concourse.bass as bass
import concourse.mybir as mybir
from concourse.bass_utils import run_bass_kernel_spmd

F32 = mybir.dt.float32
BF16 = mybir.dt.bfloat16
AF = mybir.ActivationFunctionType
ALU = mybir.AluOpType
AX = mybir.AxisListType

NCORE = 8
DM = 1024
TOK = 2048
HALO = 3
TOKH = TOK + HALO
NCH = DM // 128
DI = 2048
NCI = DI // 128
DT = 64
TB = 512
NTB = TOK // TB
TBS = [(i * TB, TB) for i in range(NTB)]
TBS_H = TBS + [(1540, 511)]
BIG = 1.0e30
EPS_LN = 1e-5

L0_NE, L0_DH = 8, 2048
DBG_NEX = None
DBG_NOSCALE = False
DBG_ALPHA_C = None
DBG_STOP = None
L1_NE, L1_DH = 16, 1024

DO_L1 = True


def build_program(do_l0=True, do_l1=True, dbg=()):
    nc = bass.Bass("TRN2", target_bir_lowering=False, debug=False, num_devices=NCORE)
    dbg = set(dbg)
    es = ExitStack()

    sem = {}
    cnt = {}
    for nm in ["ld", "wl", "wla", "wlb", "wl1", "wlu", "wl2",
               "pe", "ve", "ac", "po", "cc", "st"]:
        sem[nm] = es.enter_context(nc.semaphore(nm))
        cnt[nm] = 0

    def inc(name, inst, k=1):
        inst.then_inc(sem[name], k)
        cnt[name] += k
        return cnt[name]

    def wait(eng, name, val=None):
        eng.wait_ge(sem[name], cnt[name] if val is None else val)

    def vsafe(v, ins):
        v.wait_ge(sem["ve"], inc("ve", ins))

    inp = {}

    def dram_in(name, shape, dtype=F32):
        inp[name] = nc.dram_tensor(name, list(shape), dtype, kind="ExternalInput")
        return inp[name]

    if do_l0:
        dram_in("x_in", (DM, TOKH))
    else:
        dram_in("x_mid_in", (DM, TOK))
    out_x = nc.dram_tensor("out_x", [DM, TOK], F32, kind="ExternalOutput")

    dram_in("identity", (128, 128))
    dram_in("ones_col_bf", (128, 1), BF16)
    dram_in("ones_row_bf", (1, 128), BF16)
    dram_in("zeros_col", (128, 1))
    dram_in("one_col", (128, 1))
    dram_in("onehot8", (128, 8))
    dram_in("iota8eps", (128, 8))
    dram_in("iota16eps", (128, 16))
    dram_in("iota4eps", (128, 4))
    dram_in("neg1p", (128, 16))
    dram_in("colpack_in", (128, 160))

    if do_l0:
        dram_in("l0_ip_w", (DM, 2 * DI))
        dram_in("l0_xp_w", (DI, DT))
        dram_in("l0_dt_w", (DT, DI))
        dram_in("l0_op_w", (DI, DM))
        dram_in("l0_gate_w", (DM, L0_NE))
        dram_in("l0_bbias_bc", (128, L0_NE))
        dram_in("l0_w1", (L0_NE, DM, L0_DH))
        dram_in("l0_w2", (L0_NE, L0_DH, DM))
        cc0_in = nc.dram_tensor("cc0_in", [128, 32], F32, kind="Internal")
        cc0_out = nc.dram_tensor("cc0_out", [128 * NCORE, 32], F32, kind="Internal",
                                 addr_space="Local" if globals()["DBG_STOP"]
                                 else "Shared")
    if do_l1:
        dram_in("l1_Wd", (DM, DM))
        dram_in("l1_Wb", (DM, 16))
        dram_in("l1_Wc", (DM, 16))
        dram_in("l1_Wo", (DM, DM))
        dram_in("l1_A", (DM, 16))
        dram_in("l1_router_w", (DM, L1_NE))
        dram_in("l1_rbias_bc", (128, L1_NE))
        dram_in("l1_gate_w", (L1_NE, DM, L1_DH))
        dram_in("l1_up_w", (L1_NE, DM, L1_DH))
        dram_in("l1_down_w", (L1_NE, L1_DH, DM))
        dram_in("l1_shg_w", (DM, 2 * L1_DH))
        dram_in("l1_shu_w", (DM, 2 * L1_DH))
        dram_in("l1_shd_w", (2 * L1_DH, DM))
        cc1_in = nc.dram_tensor("cc1_in", [16, 2], F32, kind="Internal")
        cc1_out = nc.dram_tensor("cc1_out", [16 * NCORE, 2], F32, kind="Internal",
                                 addr_space="Shared")
    combT_d = nc.dram_tensor("combT_d", [16, TOK], F32, kind="Internal")

    dbg_outs = {}

    def dbg_add(name, ap_, shape, dtype=F32):
        if name in dbg:
            od = nc.dram_tensor("dbg_" + name, list(shape), dtype,
                                kind="ExternalOutput")
            dbg_outs[name] = (ap_, od)

    def dbg_snap(name, ap_, shape, dtype=F32):
        if name not in dbg:
            return
        od = nc.dram_tensor("dbg_" + name, list(shape), dtype,
                            kind="ExternalOutput")

        def sp(sync):
            wait(sync, "ve")
            wait(sync, "ac")
            wait(sync, "pe")
            inc("st", sync.dma_start(out=od.ap(), in_=ap_), 16)

        emit("sync", sp)
        stv = cnt["st"]
        emit("vector", lambda v: v.wait_ge(sem["st"], stv))
        emit("scalar", lambda s: s.wait_ge(sem["st"], stv))
        emit("tensor", lambda t: t.wait_ge(sem["st"], stv))
        emit("gpsimd", lambda g: g.wait_ge(sem["st"], stv))

    def sbt(name, shape, dtype):
        return es.enter_context(nc.sbuf_tensor(name, shape, dtype))

    x_res = sbt("x_res", [128, NCH * TOKH], F32)
    xn = sbt("xn", [128, NCH * TOKH], BF16)
    AR = sbt("AR", [128, 40960], BF16)
    SCR = sbt("SCR", [128, 4096], F32)
    ident = sbt("identity_sb", [128, 128], F32)
    ones_col = sbt("ones_col_sb", [128, 1], BF16)
    ones_row = sbt("ones_row_sb", [1, 128], BF16)
    zeros_col = sbt("zeros_col_sb", [128, 1], F32)
    one_col = sbt("one_col_sb", [128, 1], F32)
    onehot8 = sbt("onehot8_sb", [128, 8], F32)
    iota8 = sbt("iota8_sb", [128, 8], F32)
    iota16 = sbt("iota16_sb", [128, 16], F32)
    iota4 = sbt("iota4_sb", [128, 4], F32)
    neg1p = sbt("neg1p_sb", [128, 16], F32)
    colpack = sbt("colpack", [128, 160], F32)
    packF = sbt("packF", [128, 16], F32)
    packP = sbt("packP", [128, 16], F32)
    packR = sbt("packR", [128, 16], F32)
    S0 = sbt("S0", [128, 16], F32)
    agg = sbt("agg", [128, 32 * NCORE], F32)
    bias_bc = sbt("bias_bc", [128, 16], F32)
    comb = sbt("comb", [128, 16 * 16], F32)
    combT = sbt("combT", [16, 128], F32)
    cbc = sbt("cbc", [128, TOK], BF16)
    crow_bf = sbt("crow_bf", [1, TOK], BF16)
    aggL1 = sbt("aggL1", [16, 16], F32) if do_l1 else None
    cc1pack = sbt("cc1pack", [16, 2], F32) if do_l1 else None
    gw_sb = sbt("gw_sb", [128, NCH * 16], BF16)
    smallw = sbt("smallw", [128, 384], BF16) if do_l1 else None
    l1H = sbt("l1H", [16, 2], F32) if do_l1 else None

    psum = es.enter_context(nc.psum_tensor("psum", [128, 4096], F32))

    def bank(i, p0=0, np_=128, w=512):
        return psum[p0:p0 + np_, i * 512:i * 512 + w]

    U_OFF = 0
    DTL_OFF = 32768
    RING_OFF = 34816
    SZ_OFF = 38912
    CBT_OFF = 39936

    def Uc(j):
        return AR[:, U_OFF + j * TOK:U_OFF + (j + 1) * TOK]

    def ar(off, n):
        return AR[:, off:off + n]

    XP_OFF = 38912

    def xp_sb(j):
        return AR[:, XP_OFF + j * DT:XP_OFF + (j + 1) * DT]

    ROWS_OFF = 34816

    def row_b(i, n=TOKH, o=0):
        return AR[0:1, ROWS_OFF + i * TOKH + o:ROWS_OFF + i * TOKH + o + n]

    def s0(o=0, n=TOK):
        return SCR[:, o:o + n]

    def s1(o=0, n=TOK):
        return SCR[:, TOK + o:TOK + o + n]

    def sxz(o=0, n=TOKH):
        return SCR[:, o:o + n]

    def sconv(o=0, n=TOK):
        return SCR[:, o:o + n]



    def xs(c, o=0, n=TOKH):
        return x_res[:, c * TOKH + o:c * TOKH + o + n]

    def xsm(c, o=0, n=TOK):
        return x_res[:, c * TOKH + HALO + o:c * TOKH + HALO + o + n]

    def xnf(c, o=0, n=TOKH):
        return xn[:, c * TOKH + o:c * TOKH + o + n]

    def xnm(c, o=0, n=TOK):
        return xn[:, c * TOKH + HALO + o:c * TOKH + HALO + o + n]

    bh = {}

    _selfsem = {"vector": "ve", "scalar": "ac"}

    def emit(engine, fn):
        nm = _selfsem.get(engine)
        if nm is None:
            getattr(bh["block"], engine)(fn)
            return

        def wrapped(eng):
            eng.wait_ge(sem[nm], cnt[nm])
            fn(eng)

        getattr(bh["block"], engine)(wrapped)


    def ph_loads(sync):
        if do_l0:
            i0 = inp["x_in"].ap().rearrange("(c p) t -> p c t", p=128)
            d0 = x_res[:, :].rearrange("p (c t) -> p c t", c=NCH)
            inc("ld", sync.dma_start(out=d0, in_=i0), 16)
        else:
            for c in range(NCH):
                i0 = inp["x_mid_in"].ap()[c * 128:(c + 1) * 128, :]
                inc("ld", sync.dma_start(out=xsm(c), in_=i0), 16)
        for nm, dst in [("identity", ident), ("ones_col_bf", ones_col),
                        ("ones_row_bf", ones_row), ("zeros_col", zeros_col),
                        ("one_col", one_col), ("onehot8", onehot8),
                        ("iota8eps", iota8), ("iota16eps", iota16),
                        ("iota4eps", iota4), ("neg1p", neg1p),
                        ("colpack_in", colpack)]:
            inc("ld", sync.dma_start(out=dst[:, :], in_=inp[nm].ap()), 16)

    def ph_wloads_early(g):
        if do_l0:
            inc("wl", g.dma_start(
                out=gw_sb[:, 0:NCH * 8].rearrange("p (c e) -> p c e", c=NCH),
                in_=inp["l0_gate_w"].ap().rearrange("(c p) e -> p c e", p=128)), 16)

    def ph_colprep_ac(s):
        wait(s, "ld")
        inc("ac", s.activation(colpack[:, 0:16], colpack[:, 0:16], AF.Exp,
                               bias=0.0, scale=1.0))

    def ph_colprep_ve(v):
        wait(v, "ac")
        vsafe(v, v.tensor_scalar(colpack[:, 0:16], colpack[:, 0:16], -1.0, None,
                                 ALU.mult))
        inc("ve", v.tensor_scalar(colpack[:, 0:16], colpack[:, 0:16], -10.0, -1e-6,
                                  ALU.max, ALU.min))

    def emit_layernorm(full):
        tbs = TBS_H if full else TBS
        tokh = TOKH if full else TOK
        off = 0 if full else HALO
        SQ = 16408

        def xbf(c):
            return AR[:, c * tokh:(c + 1) * tokh]

        def sqbf(c):
            return AR[:, SQ + c * tokh:SQ + (c + 1) * tokh]

        def ve0(v):
            wait(v, "ld")
            wait(v, "ve")
            wait(v, "pe")
            for c in range(NCH):
                ins = v.tensor_copy(xbf(c), xs(c, off, tokh))
            inc("ve", ins)

        def ac0(s):
            wait(s, "ld")
            wait(s, "pe")
            wait(s, "ve")
            for c in range(NCH):
                ins = s.activation(sqbf(c), xs(c, off, tokh), AF.Square,
                                   bias=0.0, scale=1.0)
            inc("ac", ins)

        emit("vector", ve0)
        emit("scalar", ac0)
        ve_in, ac_in = cnt["ve"], cnt["ac"]

        pe_marks, ve_marks = [], []
        for bi, (t0, tw) in enumerate(tbs):
            bs, bq = bi % 2, 2 + bi % 2

            def pe(t, bi=bi, t0=t0, tw=tw, bs=bs, bq=bq):
                if bi == 0:
                    t.wait_ge(sem["ve"], ve_in)
                    t.wait_ge(sem["ac"], ac_in)
                if bi >= 2:
                    t.wait_ge(sem["ve"], ve_marks[bi - 2])
                for c in range(NCH):
                    t.matmul(bank(bs, 0, 1, tw), ones_col[:, :], xbf(c)[:, t0:t0 + tw],
                             start=(c == 0), stop=(c == NCH - 1))
                for c in range(NCH):
                    ins = t.matmul(bank(bq, 0, 1, tw), ones_col[:, :],
                                   sqbf(c)[:, t0:t0 + tw],
                                   start=(c == 0), stop=(c == NCH - 1))
                pe_marks.append(inc("pe", ins))

            def ve(v, bi=bi, t0=t0, tw=tw, bs=bs, bq=bq):
                v.wait_ge(sem["pe"], pe_marks[bi])
                m = row_b(0, tw, t0)
                q = row_b(1, tw, t0)
                t2 = SCR[0:1, 2048:2048 + tw]
                vsafe(v, v.tensor_scalar(m, bank(bs, 0, 1, tw), 1.0 / DM, None,
                                         ALU.mult))
                vsafe(v, v.tensor_scalar(q, bank(bq, 0, 1, tw), 1.0 / DM, None,
                                         ALU.mult))
                vsafe(v, v.tensor_mul(t2, m, m))
                vsafe(v, v.tensor_sub(q, q, t2))
                ins = v.tensor_scalar(q, q, EPS_LN, None, ALU.add)
                ve_marks.append(inc("ve", ins))

            emit("tensor", pe)
            emit("vector", ve)

        def ac1(s):
            wait(s, "ve")
            ins = s.activation(row_b(1, tokh), row_b(1, tokh), AF.Ln,
                               bias=0.0, scale=1.0)
            s.wait_ge(sem["ac"], inc("ac", ins))
            inc("ac", s.activation(row_b(1, tokh), row_b(1, tokh), AF.Exp,
                                   bias=0.0, scale=-0.5))

        emit("scalar", ac1)
        ac_rows = cnt["ac"]

        bc_marks, nrm_marks = [], []
        for bi, (t0, tw) in enumerate(tbs):
            bm, br = 4 + (bi % 2) * 2, 5 + (bi % 2) * 2

            def pe(t, bi=bi, t0=t0, tw=tw, bm=bm, br=br):
                if bi == 0:
                    t.wait_ge(sem["ac"], ac_rows)
                if bi >= 2:
                    t.wait_ge(sem["ve"], nrm_marks[bi - 2])
                t.matmul(bank(bm, 0, 128, tw), ones_row[:, :], row_b(0, tw, t0),
                         start=True, stop=True)
                ins = t.matmul(bank(br, 0, 128, tw), ones_row[:, :], row_b(1, tw, t0),
                               start=True, stop=True)
                bc_marks.append(inc("pe", ins))

            def ve(v, bi=bi, t0=t0, tw=tw, bm=bm, br=br):
                v.wait_ge(sem["pe"], bc_marks[bi])
                for c in range(NCH):
                    vsafe(v, v.tensor_sub(s0(0, tw), xs(c, off + t0, tw),
                                          bank(bm, 0, 128, tw)))
                    ins = v.tensor_mul(
                        xn[:, c * TOKH + off + t0:c * TOKH + off + t0 + tw],
                        s0(0, tw), bank(br, 0, 128, tw))
                    if c < NCH - 1:
                        vsafe(v, ins)
                nrm_marks.append(inc("ve", ins))

            emit("tensor", pe)
            emit("vector", ve)

    def emit_routing(layer):
        ne = L0_NE if layer == 0 else L1_NE
        bname = "l0_bbias_bc" if layer == 0 else "l1_rbias_bc"
        gwc = 8 if layer == 0 else 16
        ln_ve = cnt["ve"]

        def po(g):
            wait(g, "ve", ln_ve)
            if layer == 1:
                inc("wl", g.dma_start(
                    out=gw_sb[:, 0:NCH * 16].rearrange("p (c e) -> p c e", c=NCH),
                    in_=inp["l1_router_w"].ap().rearrange("(c p) e -> p c e", p=128)),
                    16)
            inc("wl", g.dma_start(out=bias_bc[:, 0:ne], in_=inp[bname].ap()), 16)

        emit("gpsimd", po)
        wl_done = cnt["wl"]

        def rb(tt, o, n):
            return SCR[:, tt * 256 + o:tt * 256 + o + n]

        pe_m, ve1_m, ac_m = {}, {}, {}
        for tt in range(16):
            bk = tt % 2

            def pe(t, tt=tt, bk=bk):
                if tt == 0:
                    t.wait_ge(sem["wl"], wl_done)
                    t.wait_ge(sem["ve"], ln_ve)
                if tt >= 2:
                    t.wait_ge(sem["ve"], ve1_m[tt - 2])
                for c in range(NCH):
                    ins = t.matmul(bank(bk, 0, 128, ne), xnm(c, tt * 128, 128),
                                   gw_sb[:, c * gwc:c * gwc + ne],
                                   start=(c == 0), stop=(c == NCH - 1))
                pe_m[tt] = inc("pe", ins)

            def ve1(v, tt=tt, bk=bk):
                v.wait_ge(sem["pe"], pe_m[tt])
                if layer == 0:
                    ins = v.tensor_add(rb(tt, 0, ne), bank(bk, 0, 128, ne),
                                       bias_bc[:, 0:ne])
                else:
                    ins = v.tensor_copy(rb(tt, 0, ne), bank(bk, 0, 128, ne))
                ve1_m[tt] = inc("ve", ins)

            def ac1(s, tt=tt):
                s.wait_ge(sem["ve"], ve1_m[tt])
                ins = s.activation(rb(tt, 0, ne), rb(tt, 0, ne), AF.Sigmoid,
                                   bias=0.0, scale=1.0)
                ac_m[tt] = inc("ac", ins)

            emit("tensor", pe)
            emit("vector", ve1)
            emit("scalar", ac1)

        def ve2(v):
            wait(v, "ac", ac_m[15])
            T16 = range(16)

            def sc(tt):
                return rb(tt, 0, ne)

            def each(f):
                ins = None
                for tt in T16:
                    ins = f(tt)
                return ins

            def step(f):
                ins = each(f)
                v.wait_ge(sem["ve"], inc("ve", ins))
                return ins

            if layer == 0:
                sp = lambda tt: rb(tt, 32, ne)
                u = lambda tt: rb(tt, 64, ne)
                m0 = lambda tt: rb(tt, 96, 1)
                m0b = lambda tt: rb(tt, 97, 1)
                den = lambda tt: rb(tt, 98, 1)
                rec = lambda tt: rb(tt, 99, 1)
                step(lambda tt: v.tensor_sub(sp(tt), sc(tt), iota8[:, 0:ne]))
                step(lambda tt: v.tensor_reduce(
                    m0(tt), sp(tt).rearrange("p (a e) -> p a e", a=1), AX.X, ALU.max))
                step(lambda tt: v.tensor_single_scalar(u(tt), sp(tt), m0(tt),
                                                       ALU.is_ge))
                step(lambda tt: v.scalar_tensor_tensor(sp(tt), u(tt), -BIG, sp(tt),
                                                       ALU.mult, ALU.add))
                step(lambda tt: v.tensor_reduce(
                    m0b(tt), sp(tt).rearrange("p (a e) -> p a e", a=1), AX.X, ALU.max))
                step(lambda tt: v.tensor_single_scalar(sp(tt), sp(tt), m0b(tt),
                                                       ALU.is_ge))
                step(lambda tt: v.tensor_add(u(tt), u(tt), sp(tt)))
                step(lambda tt: v.tensor_mul(sp(tt), sc(tt), u(tt)))
                step(lambda tt: v.tensor_reduce(
                    den(tt), sp(tt).rearrange("p (a e) -> p a e", a=1), AX.X, ALU.add))
                step(lambda tt: v.tensor_scalar(den(tt), den(tt), 1e-6, None,
                                                ALU.add))
                step(lambda tt: v.reciprocal(rec(tt), den(tt)))
                ins = each(lambda tt: v.tensor_single_scalar(
                    comb[:, tt * 16:tt * 16 + ne], sp(tt), rec(tt), ALU.mult))
            else:
                ss = lambda tt: rb(tt, 32, ne)
                u = lambda tt: rb(tt, 64, ne)
                ge = lambda tt: rb(tt, 96, ne)
                wk = lambda tt: rb(tt, 128, ne)
                g1 = lambda tt: rb(tt, 160, 4)
                g2 = lambda tt: rb(tt, 168, 4)
                g3 = lambda tt: rb(tt, 176, 4)
                m0 = lambda tt: rb(tt, 184, 1)
                m1 = lambda tt: rb(tt, 185, 1)
                step(lambda tt: v.tensor_add(ss(tt), sc(tt), bias_bc[:, 0:ne]))
                step(lambda tt: v.tensor_sub(ss(tt), ss(tt), iota16[:, 0:ne]))
                step(lambda tt: v.tensor_reduce(
                    g1(tt), ss(tt).rearrange("p (g r) -> p g r", g=4), AX.X, ALU.max))
                for jj in range(4):
                    step(lambda tt, jj=jj: v.tensor_copy(
                        ge(tt)[:, 4 * jj:4 * jj + 4],
                        g1(tt)[:, jj:jj + 1].broadcast_to([128, 4])))
                step(lambda tt: v.tensor_sub(wk(tt), ss(tt), ge(tt)))
                step(lambda tt: v.tensor_scalar(u(tt), wk(tt), -1e-12, None,
                                                ALU.is_ge))
                step(lambda tt: v.scalar_tensor_tensor(wk(tt), u(tt), -BIG, ss(tt),
                                                       ALU.mult, ALU.add))
                step(lambda tt: v.tensor_reduce(
                    g2(tt), wk(tt).rearrange("p (g r) -> p g r", g=4), AX.X, ALU.max))
                step(lambda tt: v.tensor_add(g1(tt), g1(tt), g2(tt)))
                step(lambda tt: v.tensor_sub(g1(tt), g1(tt), iota4[:, :]))
                step(lambda tt: v.tensor_reduce(
                    m0(tt), g1(tt).rearrange("p (a e) -> p a e", a=1), AX.X, ALU.max))
                step(lambda tt: v.tensor_single_scalar(g2(tt), g1(tt), m0(tt),
                                                       ALU.is_ge))
                step(lambda tt: v.scalar_tensor_tensor(g3(tt), g2(tt), -BIG, g1(tt),
                                                       ALU.mult, ALU.add))
                step(lambda tt: v.tensor_reduce(
                    m0(tt), g3(tt).rearrange("p (a e) -> p a e", a=1), AX.X, ALU.max))
                step(lambda tt: v.tensor_single_scalar(g3(tt), g1(tt), m0(tt),
                                                       ALU.is_ge))
                for jj in range(4):
                    step(lambda tt, jj=jj: v.tensor_copy(
                        ge(tt)[:, 4 * jj:4 * jj + 4],
                        g3(tt)[:, jj:jj + 1].broadcast_to([128, 4])))
                step(lambda tt: v.tensor_sub(wk(tt), ss(tt), neg1p[:, 0:ne]))
                step(lambda tt: v.tensor_mul(wk(tt), wk(tt), ge(tt)))
                step(lambda tt: v.tensor_add(wk(tt), wk(tt), neg1p[:, 0:ne]))
                step(lambda tt: v.tensor_single_scalar(u(tt), wk(tt), 0.0, ALU.add))
                for _ in range(3):
                    step(lambda tt: v.tensor_reduce(
                        m0(tt), u(tt).rearrange("p (a e) -> p a e", a=1), AX.X,
                        ALU.max))
                    step(lambda tt: v.tensor_single_scalar(ge(tt), u(tt), m0(tt),
                                                           ALU.is_ge))
                    step(lambda tt: v.scalar_tensor_tensor(u(tt), ge(tt), -BIG,
                                                           u(tt), ALU.mult, ALU.add))
                step(lambda tt: v.tensor_reduce(
                    m0(tt), u(tt).rearrange("p (a e) -> p a e", a=1), AX.X, ALU.max))
                step(lambda tt: v.tensor_single_scalar(u(tt), wk(tt), m0(tt),
                                                       ALU.is_ge))
                step(lambda tt: v.tensor_mul(wk(tt), sc(tt), u(tt)))
                step(lambda tt: v.tensor_reduce(
                    m0(tt), wk(tt).rearrange("p (a e) -> p a e", a=1), AX.X, ALU.add))
                step(lambda tt: v.tensor_scalar(m0(tt), m0(tt), 1e-20, None,
                                                ALU.add))
                step(lambda tt: v.reciprocal(m1(tt), m0(tt)))
                ins = each(lambda tt: v.tensor_scalar(
                    comb[:, tt * 16:tt * 16 + ne], wk(tt), m1(tt), 2.5,
                    ALU.mult, ALU.mult))
            inc("ve", ins)

        emit("vector", ve2)
        ve2_all = cnt["ve"]

        tr_pe, tr_ve, tr_po = {}, {}, {}
        for tt in range(16):
            bk = 4 + tt % 2

            def pe(t, tt=tt, bk=bk):
                if tt == 0:
                    t.wait_ge(sem["ve"], ve2_all)
                if tt >= 2:
                    t.wait_ge(sem["ve"], tr_ve[tt - 2])
                ins = t.transpose(bank(bk, 0, ne, 128), comb[:, tt * 16:tt * 16 + ne],
                                  ident[:, :])
                tr_pe[tt] = inc("pe", ins)

            def ve(v, tt=tt, bk=bk):
                v.wait_ge(sem["pe"], tr_pe[tt])
                if tt >= 1:
                    v.wait_ge(sem["po"], tr_po[tt - 1])
                ins = v.tensor_copy(combT[0:ne, :], bank(bk, 0, ne, 128))
                tr_ve[tt] = inc("ve", ins)

            def po2(g, tt=tt):
                g.wait_ge(sem["ve"], tr_ve[tt])
                ins = g.dma_start(out=combT_d.ap()[0:ne, tt * 128:(tt + 1) * 128],
                                  in_=combT[0:ne, :])
                tr_po[tt] = inc("po", ins, 16)

            emit("tensor", pe)
            emit("vector", ve)
            emit("gpsimd", po2)

    def emit_moe(layer):
        if layer == 0:
            nhc = L0_DH // 128
            W1_OFF, W2_OFF, H_OFF, SIL_OFF = 0, 16384, 32768, None
            WU_OFF = None
            experts = [("l0_w1", "l0_w2", None, e, True) for e in range(L0_NE)]
            if DBG_NEX:
                experts = experts[:DBG_NEX]
            dh = L0_DH
        else:
            nhc = L1_DH // 128
            W1_OFF, WU_OFF, W2_OFF, H_OFF, SIL_OFF = 0, 8192, 16384, 24576, 32768
            experts = [("l1_gate_w", "l1_down_w", "l1_up_w", e, True)
                       for e in range(L1_NE)]
            experts += [("l1_shg_w", "l1_shd_w", "l1_shu_w", sh, False)
                        for sh in range(2)]
            dh = L1_DH
        nex = len(experts)

        npar = 1 if layer == 0 else 2

        def Htile(par, hc, tw):
            o = H_OFF + (par % npar) * nhc * TB + hc * TB
            return AR[:, o:o + tw]

        ld_m = [None] * nex
        ld2_m = [None] * nex
        last_a = [0, 0]
        emit_moe.crow_ct = [0]
        emit_moe.crow_war = {0: 0, 1: 0}
        emit_moe.crow_po = {0: 0, 1: 0}
        emit_moe.crow_pref = {}
        alpha_done = [None] * nex
        beta_done = [None] * nex
        ev_m = {}
        hready = {}
        hfree = {}
        cb_last = [0]

        def emit_load(ei):
            g1n, g2n, gun, e, is_real = experts[ei]

            def po(g, ei=ei, g1n=g1n, g2n=g2n, gun=gun, e=e, is_real=is_real):
                if ei == 0:
                    wait(g, "pe")
                    wait(g, "ve")
                if ei >= 1 and alpha_done[ei - 1] is not None:
                    g.wait_ge(sem["pe"], alpha_done[ei - 1])
                if layer == 0 or is_real:
                    src = inp[g1n].ap()[e]
                else:
                    src = inp[g1n].ap()[:, e * L1_DH:(e + 1) * L1_DH]
                ins = g.dma_start(
                    out=ar(W1_OFF, NCH * dh).rearrange("p (c h) -> p c h", c=NCH),
                    in_=src.rearrange("(c p) h -> p c h", p=128))
                inc("wl1", ins, 16)
                if layer == 1:
                    srcu = inp[gun].ap()[e] if is_real else \
                        inp[gun].ap()[:, e * L1_DH:(e + 1) * L1_DH]
                    ins = g.dma_start(
                        out=ar(WU_OFF, NCH * dh).rearrange("p (c h) -> p c h", c=NCH),
                        in_=srcu.rearrange("(c p) h -> p c h", p=128))
                    inc("wlu", ins, 16)
                ld_m[ei] = (cnt["wl1"], cnt["wlu"])
                if ei >= 1 and beta_done[ei - 1] is not None:
                    g.wait_ge(sem["pe"], beta_done[ei - 1])
                if layer == 0 or is_real:
                    srcd = inp[g2n].ap()[e]
                else:
                    srcd = inp[g2n].ap()[e * L1_DH:(e + 1) * L1_DH, :]
                ins = g.dma_start(
                    out=ar(W2_OFF, nhc * DM).rearrange("p (h d) -> p h d", h=nhc),
                    in_=srcd.rearrange("(h p) d -> p h d", p=128))
                ld2_m[ei] = inc("wl2", ins, 16)

            emit("gpsimd", po)

        emit_load(0)
        rt_po = cnt["po"]

        for ei in range(nex):
            g1n, g2n, gun, e, is_real = experts[ei]

            if is_real:
                cb_pe = {}
                cb_vem = {}
                par_e = 0
                if ei == 0 or not emit_moe.crow_pref.get(ei):
                    def poc(g, ei=ei, e=e, par_e=par_e):
                        if ei == 0:
                            g.wait_ge(sem["po"], rt_po)
                        if emit_moe.crow_war[par_e]:
                            g.wait_ge(sem["pe"], emit_moe.crow_war[par_e])
                        emit_moe.crow_po[par_e] = inc("po", g.dma_start(
                            out=crow_bf[0:1, :], in_=combT_d.ap()[e:e + 1, :]), 16)

                    emit("gpsimd", poc)

                def pec(t, ei=ei, par_e=par_e):
                    ve_base = cnt["ve"]
                    t.wait_ge(sem["po"], emit_moe.crow_po[par_e])
                    for bi, (t0, tw) in enumerate(TBS):
                        if ei == 0 and bi < 2:
                            wait(t, "ve")
                        elif bi >= 2:
                            t.wait_ge(sem["ve"], ve_base + bi - 1)
                        elif cb_last[0]:
                            t.wait_ge(sem["ve"], cb_last[0])
                        ins = t.matmul(bank(6 + bi % 2, 0, 128, tw), ones_row[:, :],
                                       crow_bf[0:1, t0:t0 + tw],
                                       start=True, stop=True)
                        cb_pe[bi] = inc("pe", ins)
                    emit_moe.crow_war[par_e] = cnt["pe"]

                def vec(v):
                    for bi, (t0, tw) in enumerate(TBS):
                        v.wait_ge(sem["pe"], cb_pe[bi])
                        ins = v.tensor_copy(cbc[:, t0:t0 + tw],
                                            bank(6 + bi % 2, 0, 128, tw))
                        cb_vem[bi] = inc("ve", ins)
                    cb_last[0] = cnt["ve"]

                emit("tensor", pec)
                emit("vector", vec)
                emit_moe.crow_ct[0] += 1
            cb_ve = cnt["ve"]

            if layer == 0:
                sched = [("a", 0), ("b", 0), ("a", 1), ("b", 1), ("a", 2),
                         ("b", 2), ("a", 3), ("b", 3)]
            else:
                sched = [("a", 0), ("a", 1), ("b", 0), ("a", 2), ("b", 1),
                         ("a", 3), ("b", 2), ("b", 3)]
            al_marks = {}
            for kind, bi in sched:
                t0, tw = TBS[bi]
                par = bi % 2
                if kind == "a":
                    al_pe, al_ac, al_ve = {}, {}, {}
                    al_marks[bi] = (al_ac, al_ve)

                    def pea(t, ei=ei, bi=bi, t0=t0, tw=tw,
                            al_pe=al_pe, cb_ve=cb_ve, is_real=is_real):
                        ac_base = cnt["ac"]
                        ve_base = cnt["ve"]
                        if layer == 0:
                            nve = 1 if is_real else 0
                        else:
                            nve = 2 if is_real else 1
                        if bi == 0:
                            t.wait_ge(sem["wl1"], ld_m[ei][0])
                            if layer == 1:
                                t.wait_ge(sem["wlu"], ld_m[ei][1])
                            t.wait_ge(sem["ve"], cb_ve)
                        if bi >= npar:
                            t.wait_ge(sem["pe"], hfree[(ei, bi - npar)])
                        if last_a[0]:
                            t.wait_ge(sem["ac"], last_a[0])
                        if last_a[1]:
                            t.wait_ge(sem["ve"], last_a[1])
                        for hc in range(nhc):
                            if layer == 0:
                                bkg, bku = 4 + (hc % 2), None
                            else:
                                bkg, bku = 4 + (hc % 2) * 2, 5 + (hc % 2) * 2
                            if hc >= 2:
                                t.wait_ge(sem["ac"], ac_base + hc - 1)
                                if layer == 1 and nve:
                                    t.wait_ge(sem["ve"], ve_base + (hc - 1) * nve)
                            crange = (list(range(NCH)) if globals()["DBG_ALPHA_C"]
                                      is None else globals()["DBG_ALPHA_C"])
                            for ci, c in enumerate(crange):
                                ins = t.matmul(
                                    bank(bkg, 0, 128, tw),
                                    AR[:, W1_OFF + c * dh + hc * 128:
                                       W1_OFF + c * dh + hc * 128 + 128],
                                    xnm(c, t0, tw),
                                    start=(ci == 0), stop=(ci == len(crange) - 1))
                            al_pe[hc] = inc("pe", ins)
                            if layer == 1:
                                for c in range(NCH):
                                    ins = t.matmul(
                                        bank(bku, 0, 128, tw),
                                        AR[:, WU_OFF + c * dh + hc * 128:
                                           WU_OFF + c * dh + hc * 128 + 128],
                                        xnm(c, t0, tw),
                                        start=(c == 0), stop=(c == NCH - 1))
                                al_pe[hc] = inc("pe", ins)
                        if bi == 3:
                            alpha_done[ei] = cnt["pe"]

                    def aca(s, ei=ei, bi=bi, tw=tw, par=par,
                            al_pe=al_pe, al_ac=al_ac):
                        if layer == 1 and bi >= 2 and (ei, bi - 2) in hready:
                            s.wait_ge(sem["ve"], hready[(ei, bi - 2)][1])
                        if layer == 0 and bi >= 1:
                            s.wait_ge(sem["pe"], hfree[(ei, bi - 1)])
                        for hc in range(nhc):
                            bkg = 4 + (hc % 2) if layer == 0 else 4 + (hc % 2) * 2
                            s.wait_ge(sem["pe"], al_pe[hc])
                            if layer == 0:
                                dst = Htile(par, hc, tw)
                            else:
                                dst = AR[:, SIL_OFF + hc * TB:SIL_OFF + hc * TB + tw]
                            ins = s.activation(dst, bank(bkg, 0, 128, tw), AF.Silu,
                                               bias=0.0, scale=1.0)
                            al_ac[hc] = inc("ac", ins)

                    def vea(v, ei=ei, bi=bi, t0=t0, tw=tw, par=par, is_real=is_real,
                            al_ac=al_ac, al_ve=al_ve):
                        ins = None
                        for hc in range(nhc):
                            v.wait_ge(sem["ac"], al_ac[hc])
                            h = Htile(par, hc, tw)
                            if layer == 0:
                                if is_real and not globals()['DBG_NOSCALE']:
                                    ins = v.tensor_mul(h, h, cbc[:, t0:t0 + tw])
                            else:
                                bku = 5 + (hc % 2) * 2
                                sg = AR[:, SIL_OFF + hc * TB:SIL_OFF + hc * TB + tw]
                                ins = v.tensor_mul(h, bank(bku, 0, 128, tw), sg)
                                if is_real:
                                    vsafe(v, ins)
                                    ins = v.tensor_mul(h, h, cbc[:, t0:t0 + tw])
                            if ins is not None:
                                al_ve[hc] = inc("ve", ins)
                            else:
                                al_ve[hc] = cnt["ve"]

                    emit("tensor", pea)
                    emit("scalar", aca)
                    emit("vector", vea)
                    hready[(ei, bi)] = (cnt["ac"], cnt["ve"])
                    last_a[0], last_a[1] = cnt["ac"], cnt["ve"]
                    if ei == 0 and bi == 0 and layer == 0:
                        dbg_snap("H00", AR[:, H_OFF:H_OFF + nhc * TB],
                                 (128, nhc * TB), BF16)
                        dbg_snap("xnsnap", xn[:, :], (128, NCH * TOKH), BF16)
                        dbg_snap("w1snap", AR[:, 0:8192], (128, 8192), BF16)
                else:
                    for dq in range(2):
                        be_pe = {}

                        def peb(t, ei=ei, bi=bi, t0=t0, tw=tw, par=par, dq=dq,
                                be_pe=be_pe):
                            al_ac2, al_ve2 = al_marks[bi]
                            if dq == 0 and bi == 0:
                                t.wait_ge(sem["wl2"], ld2_m[ei])
                            for dc4 in range(4):
                                if dq == 1:
                                    km = ev_m.get((ei, bi, "q0"))
                                elif bi > 0:
                                    km = ev_m.get((ei, bi - 1, dc4))
                                elif ei > 0:
                                    km = ev_m.get((ei - 1, NTB - 1, dc4))
                                else:
                                    km = None
                                if km:
                                    t.wait_ge(sem["ve"], km)
                                    break
                            for hc in range(nhc):
                                if dq == 0:
                                    t.wait_ge(sem["ac"], al_ac2[hc])
                                    t.wait_ge(sem["ve"], al_ve2[hc])
                                for dc4 in range(4):
                                    dc = dq * 4 + dc4
                                    ins = t.matmul(
                                        bank(dc4, 0, 128, tw),
                                        AR[:, W2_OFF + hc * DM + dc * 128:
                                           W2_OFF + hc * DM + dc * 128 + 128],
                                        Htile(par, hc, tw),
                                        start=(hc == 0), stop=(hc == nhc - 1))
                            be_pe[0] = inc("pe", ins)
                            if dq == 1:
                                hfree[(ei, bi)] = cnt["pe"]
                                if bi == 3:
                                    beta_done[ei] = cnt["pe"]

                        def veb(v, ei=ei, bi=bi, t0=t0, tw=tw, dq=dq, be_pe=be_pe):
                            v.wait_ge(sem["pe"], be_pe[0])
                            for dc4 in range(4):
                                dc = dq * 4 + dc4
                                ins = v.tensor_add(xsm(dc, t0, tw), xsm(dc, t0, tw),
                                                   bank(dc4, 0, 128, tw))
                                inc("ve", ins)
                            if dq == 0:
                                ev_m[(ei, bi, "q0")] = cnt["ve"]
                            else:
                                for dc4 in range(4):
                                    ev_m[(ei, bi, dc4)] = cnt["ve"]

                        emit("tensor", peb)
                        emit("vector", veb)

            if ei + 1 < nex and experts[ei + 1][4]:
                def pocn(g, e2=experts[ei + 1][3]):
                    if emit_moe.crow_war[0]:
                        g.wait_ge(sem["pe"], emit_moe.crow_war[0])
                    emit_moe.crow_po[0] = inc("po", g.dma_start(
                        out=crow_bf[0:1, :], in_=combT_d.ap()[e2:e2 + 1, :]), 16)

                emit("gpsimd", pocn)
                emit_moe.crow_pref[ei + 1] = True
            if ei + 1 < nex:
                emit_load(ei + 1)

    def emit_l0_passA():
        ln_ve = cnt["ve"]
        ipw_m = {}
        pe_m, ac_m, ve_m = {}, {}, {}
        def xzh(j):
            return SCR[:, 2560 + 3 * (j % 2):2560 + 3 * (j % 2) + 3]

        def po0(g):
            wait(g, "ve", ln_ve)
            wait(g, "pe")
            inc("wl", g.dma_start(
                out=AR[:, XP_OFF:XP_OFF + NCI * DT]
                .rearrange("p (c r) -> p c r", c=NCI),
                in_=inp["l0_xp_w"].ap().rearrange("(c p) r -> p c r", p=128)), 16)
            for j in range(2):
                ins = g.dma_start(
                    out=ar(RING_OFF + (j % 2) * 1024, 1024)
                    .rearrange("p (c m) -> p c m", c=NCH),
                    in_=inp["l0_ip_w"].ap()[:, j * 128:(j + 1) * 128]
                    .rearrange("(c p) m -> p c m", p=128))
                ipw_m[j] = inc("wla" if j % 2 == 0 else "wlb", ins, 16)

        emit("gpsimd", po0)

        for j in range(NCI):
            if j >= 2:
                def po(g, j=j):
                    g.wait_ge(sem["pe"], pe_m[(j - 2, len(TBS_H) - 1)])
                    ins = g.dma_start(
                        out=ar(RING_OFF + (j % 2) * 1024, 1024)
                        .rearrange("p (c m) -> p c m", c=NCH),
                        in_=inp["l0_ip_w"].ap()[:, j * 128:(j + 1) * 128]
                        .rearrange("(c p) m -> p c m", p=128))
                    ipw_m[j] = inc("wla" if j % 2 == 0 else "wlb", ins, 16)

                emit("gpsimd", po)

            for bi, (t0, tw) in enumerate(TBS_H):
                g_idx = j * len(TBS_H) + bi
                bk = g_idx % 3

                def pe(t, j=j, bi=bi, t0=t0, tw=tw, bk=bk, g_idx=g_idx):
                    if bi == 0:
                        t.wait_ge(sem["wla" if j % 2 == 0 else "wlb"], ipw_m[j])
                        if j == 0:
                            t.wait_ge(sem["ve"], ln_ve)
                        if j >= 1:
                            km = ve_m.get((j - 2, "conv"))
                            if km:
                                t.wait_ge(sem["ve"], km)
                    if g_idx >= 3:
                        pj, pb = divmod(g_idx - 3, len(TBS_H))
                        t.wait_ge(sem["ac"], ac_m[(pj, pb)])
                    for c in range(NCH):
                        ins = t.matmul(bank(bk, 0, 128, tw),
                                       ar(RING_OFF + (j % 2) * 1024, 1024)
                                       [:, c * 128:(c + 1) * 128],
                                       xnf(c, t0, tw),
                                       start=(c == 0), stop=(c == NCH - 1))
                    pe_m[(j, bi)] = inc("pe", ins)

                def ac(s, j=j, bi=bi, t0=t0, tw=tw, bk=bk):
                    s.wait_ge(sem["pe"], pe_m[(j, bi)])
                    if bi == 0 and j >= 1:
                        km = ve_m.get((j - 2, "conv"))
                        if km:
                            s.wait_ge(sem["ve"], km)
                    if bi == len(TBS_H) - 1:
                        ins = s.copy(Uc(j)[:, 1537:2048], bank(bk, 0, 128, tw))
                    elif bi == 0:
                        s.copy(xzh(j), bank(bk, 0, 128, 3))
                        ins = s.copy(Uc(j)[:, 0:509], bank(bk, 0, 128, tw)[:, 3:512])
                    else:
                        ins = s.copy(Uc(j)[:, t0 - 3:t0 - 3 + tw],
                                     bank(bk, 0, 128, tw))
                    ac_m[(j, bi)] = inc("ac", ins)

                emit("tensor", pe)
                emit("scalar", ac)

            def ve(v, j=j):
                wait(v, "ac", ac_m[(j, len(TBS_H) - 1)])
                if j >= 1:
                    v.wait_ge(sem["ac"], ac_m[(j - 1, "silu")])
                first = True
                for k in range(4):
                    cwk = colpack[:, 64 + 16 * k + j:64 + 16 * k + j + 1]
                    hw = 3 - k
                    if hw > 0:
                        if first:
                            vsafe(v, v.tensor_single_scalar(
                                sconv(0, hw), xzh(j)[:, k:3], cwk, ALU.mult))
                        else:
                            vsafe(v, v.scalar_tensor_tensor(
                                sconv(0, hw), xzh(j)[:, k:3], cwk, sconv(0, hw),
                                ALU.mult, ALU.add))
                    mw = TOK - max(0, hw)
                    if first:
                        vsafe(v, v.tensor_single_scalar(
                            sconv(max(0, hw), mw), Uc(j)[:, 0:mw], cwk, ALU.mult))
                        first = False
                    else:
                        ins = v.scalar_tensor_tensor(
                            sconv(max(0, hw), mw), Uc(j)[:, 0:mw], cwk,
                            sconv(max(0, hw), mw), ALU.mult, ALU.add)
                        if k < 3:
                            vsafe(v, ins)
                ve_m[(j, "conv")] = inc("ve", ins)

            def ac2(s, j=j):
                wait(s, "ve", ve_m[(j, "conv")])
                cb = colpack[:, 32 + j:32 + j + 1]
                ins = s.activation(Uc(j), sconv(), AF.Silu, bias=cb, scale=1.0)
                ac_m[(j, "silu")] = inc("ac", ins)

            emit("vector", ve)
            emit("scalar", ac2)

            if j >= 1:
                def pexp(t, jj=j - 1):
                    if jj == 0:
                        wait(t, "wl")
                    t.wait_ge(sem["ac"], ac_m[(jj, "silu")])
                    for bi, (t0, tw) in enumerate(TBS):
                        ins = t.matmul(bank(4 + bi, 0, 64, tw),
                                       xp_sb(jj),
                                       Uc(jj)[:, t0:t0 + tw],
                                       start=(jj == 0), stop=(jj == NCI - 1))
                    inc("pe", ins)

                emit("tensor", pexp)

        def pexpl(t):
            jj = NCI - 1
            t.wait_ge(sem["ac"], ac_m[(jj, "silu")])
            for bi, (t0, tw) in enumerate(TBS):
                ins = t.matmul(bank(4 + bi, 0, 64, tw),
                               xp_sb(jj),
                               Uc(jj)[:, t0:t0 + tw],
                               start=(jj == 0), stop=(jj == NCI - 1))
            inc("pe", ins)

        emit("tensor", pexpl)
        pe_xp = cnt["pe"]

        def podtw(g):
            g.wait_ge(sem["pe"], pe_xp)
            inc("wl", g.dma_start(out=AR[0:64, 38912:38912 + DI],
                                  in_=inp["l0_dt_w"].ap()), 16)

        emit("gpsimd", podtw)

        def vedt(v):
            v.wait_ge(sem["pe"], pe_xp)
            for bi, (t0, tw) in enumerate(TBS):
                ins = v.tensor_copy(AR[0:64, DTL_OFF + t0:DTL_OFF + t0 + tw],
                                    bank(4 + bi, 0, 64, tw))
            inc("ve", ins)

        emit("vector", vedt)

    dc_war = {}

    def emit_delta_chain(j, accum_R, s0free=None):
        bks = (0, 4)[j % 2]
        marks = {}

        def pe(t, j=j, bks=bks):
            if bks in dc_war:
                t.wait_ge(sem["ac"], dc_war[bks])
            sw = dc_war.get("scan%d" % (j % 2))
            if sw:
                t.wait_ge(sem["ve"], sw)
            if j <= 1:
                wait(t, "ve")
                wait(t, "wl")
            for bi, (t0, tw) in enumerate(TBS):
                ins = t.matmul(bank(bks + bi, 0, 128, tw),
                               AR[0:64, 38912 + j * 128:38912 + (j + 1) * 128],
                               AR[0:64, DTL_OFF + t0:DTL_OFF + t0 + tw],
                               start=True, stop=True)
            marks["pe"] = inc("pe", ins)

        def ac(s, j=j, bks=bks):
            s.wait_ge(sem["pe"], marks["pe"])
            if s0free is not None:
                s.wait_ge(sem["ve"], s0free)
            dtb = colpack[:, 48 + j:48 + j + 1]
            for bi, (t0, tw) in enumerate(TBS):
                ins = s.activation(s0(t0, tw), bank(bks + bi, 0, 128, tw), AF.Exp,
                                   bias=dtb, scale=1.0)
            s.wait_ge(sem["ac"], inc("ac", ins))
            ins = s.activation(s0(), s0(), AF.Ln, bias=1.0, scale=1.0)
            marks["ac"] = inc("ac", ins)
            dc_war[bks] = marks["ac"]

        def ve(v, j=j):
            wait(v, "ac", marks["ac"])
            if accum_R:
                vsafe(v, v.tensor_scalar(s0(), s0(), 1e-6, None, ALU.max))
                vsafe(v, v.tensor_scalar(s0(), s0(), 10.0, None, ALU.min, ALU.add,
                                         accum_out=packR[:, j:j + 1]))
            else:
                vsafe(v, v.tensor_scalar(s0(), s0(), 1e-6, 10.0, ALU.max, ALU.min))
            Acol = colpack[:, j:j + 1]
            ins = v.tensor_single_scalar(s0(), s0(), Acol, ALU.mult)
            marks["ve"] = inc("ve", ins)

        def ac2(s, j=j):
            wait(s, "ve", marks["ve"])
            ins = s.activation(s0(), s0(), AF.Exp, bias=0.0, scale=1.0)
            marks["ac2"] = inc("ac", ins)

        emit("tensor", pe)
        emit("scalar", ac)
        emit("vector", ve)
        emit("scalar", ac2)
        return marks

    def emit_l0_prepass():
        prev_s0free = None
        for j in range(NCI):
            marks = emit_delta_chain(j, accum_R=True, s0free=prev_s0free)

            def ve(v, j=j, marks=marks):
                bks = (0, 4)[j % 2]
                wait(v, "ac", marks["ac2"])
                pscan = psum[:, bks * 512:bks * 512 + TOK]
                vsafe(v, v.tensor_tensor_scan(pscan, s0(), Uc(j), zeros_col[:, 0:1],
                                              ALU.mult, ALU.add))
                ins = v.tensor_copy(packF[:, j:j + 1], pscan[:, TOK - 1:TOK])
                marks["scan"] = inc("ve", ins)

            emit("vector", ve)
            prev_s0free = cnt["ve"]
            dc_war["scan%d" % (j % 2)] = cnt["ve"]

        def vp(v):
            wait(v, "ve")
            ins = v.tensor_mul(packP[:, :], packR[:, :], colpack[:, 0:16])
            inc("ve", ins)

        def acp(s):
            wait(s, "ve")
            inc("ac", s.activation(packP[:, :], packP[:, :], AF.Exp,
                                   bias=0.0, scale=1.0))

        def vpack(v):
            wait(v, "ac")
            wait(v, "ve")
            v.tensor_copy(agg[:, 0:16], packF[:, :])
            inc("ve", v.tensor_copy(agg[:, 16:32], packP[:, :]))

        emit("vector", vp)
        emit("scalar", acp)
        emit("vector", vpack)
        ve_pack = cnt["ve"]

        def po(g):
            g.wait_ge(sem["ve"], ve_pack)
            inc("po", g.dma_start(out=cc0_in.ap(), in_=agg[:, 0:32]), 16)
            wait(g, "po")
            inc("cc", g.collective_compute(
                "AllGather", ALU.bypass, replica_groups=[list(range(NCORE))],
                ins=[cc0_in.ap().opt()], outs=[cc0_out.ap().opt()]), 1)
            wait(g, "cc")
            for r in range(NCORE):
                inc("po", g.dma_start(out=agg[:, 32 * r:32 * (r + 1)],
                                      in_=cc0_out.ap()[128 * r:128 * (r + 1), :]), 16)

        emit("gpsimd", po)
        po_done = cnt["po"]

        def vcomb(v):
            v.wait_ge(sem["po"], po_done)
            vsafe(v, v.memset(S0[:, :], 0.0))
            vsafe(v, v.memset(packR[:, :], 0.0))
            for r in range(NCORE):
                vsafe(v, v.scalar_tensor_tensor(S0[:, :], packR[:, :],
                                                onehot8[:, r:r + 1], S0[:, :],
                                                ALU.mult, ALU.add))
                if r < NCORE - 1:
                    vsafe(v, v.tensor_mul(packP[:, :],
                                          agg[:, 32 * r + 16:32 * r + 32],
                                          packR[:, :]))
                    vsafe(v, v.tensor_add(packR[:, :], packP[:, :],
                                          agg[:, 32 * r:32 * r + 16]))

        emit("vector", vcomb)

    def emit_l0_passC1():
        emit_l0_passC1.s0_ready = cnt["ve"]
        prev_s0free = None
        for j in range(NCI):
            marks = emit_delta_chain(j, accum_R=False, s0free=prev_s0free)

            def ve(v, j=j, marks=marks):
                bks = (0, 4)[j % 2]
                wait(v, "ac", marks["ac2"])
                if j == 0:
                    wait(v, "ve", emit_l0_passC1.s0_ready)
                pscan = psum[:, bks * 512:bks * 512 + TOK]
                vsafe(v, v.tensor_tensor_scan(pscan, s0(), Uc(j), S0[:, j:j + 1],
                                              ALU.mult, ALU.add))
                Dcol = colpack[:, 16 + j:16 + j + 1]
                vsafe(v, v.scalar_tensor_tensor(s1(), Uc(j), Dcol, pscan,
                                                ALU.mult, ALU.add))
                vsafe(v, v.tensor_scalar(s1(), s1(), -1e4, 1e4, ALU.max, ALU.min))
                ins = v.tensor_copy(Uc(j), s1())
                inc("ve", ins)

            emit("vector", ve)
            prev_s0free = cnt["ve"]
            dc_war["scan%d" % (j % 2)] = cnt["ve"]

    def emit_l0_passC2():
        zm = {}
        for j in range(NCI):
            def po(g, j=j):
                if j < 2:
                    wait(g, "pe")
                else:
                    g.wait_ge(sem["pe"], zm[(j - 2, NTB - 1, "pe")])
                ins = g.dma_start(
                    out=ar(RING_OFF + (j % 2) * 1024, 1024)
                    .rearrange("p (c m) -> p c m", c=NCH),
                    in_=inp["l0_ip_w"].ap()[:, (NCI + j) * 128:(NCI + j + 1) * 128]
                    .rearrange("(c p) m -> p c m", p=128))
                zm[(j, "wl")] = inc("wla" if j % 2 == 0 else "wlb", ins, 16)

            emit("gpsimd", po)

            for bi, (t0, tw) in enumerate(TBS):
                bk = bi % 2
                sz = SZ_OFF + (bi % 2) * TB

                def pe(t, j=j, bi=bi, t0=t0, tw=tw, bk=bk):
                    if bi == 0:
                        t.wait_ge(sem["wla" if j % 2 == 0 else "wlb"],
                                  zm[(j, "wl")])
                        if j == 0:
                            wait(t, "ve")
                            wait(t, "ac")
                    prev = (j, bi - 2, "ac") if bi >= 2 else (j - 1, bi + 2, "ac")
                    if prev in zm:
                        t.wait_ge(sem["ac"], zm[prev])
                    for c in range(NCH):
                        ins = t.matmul(bank(bk, 0, 128, tw),
                                       ar(RING_OFF + (j % 2) * 1024, 1024)
                                       [:, c * 128:(c + 1) * 128],
                                       xnm(c, t0, tw),
                                       start=(c == 0), stop=(c == NCH - 1))
                    zm[(j, bi, "pe")] = inc("pe", ins)

                def ac(s, j=j, bi=bi, tw=tw, bk=bk, sz=sz):
                    s.wait_ge(sem["pe"], zm[(j, bi, "pe")])
                    prev = (j, bi - 2, "ve") if bi >= 2 else (j - 1, bi + 2, "ve")
                    if prev in zm:
                        s.wait_ge(sem["ve"], zm[prev])
                    ins = s.activation(ar(sz, tw), bank(bk, 0, 128, tw),
                                       AF.Silu, bias=0.0, scale=1.0)
                    zm[(j, bi, "ac")] = inc("ac", ins)

                def ve(v, j=j, bi=bi, t0=t0, tw=tw, sz=sz):
                    wait(v, "ac", zm[(j, bi, "ac")])
                    ins = v.tensor_mul(Uc(j)[:, t0:t0 + tw], Uc(j)[:, t0:t0 + tw],
                                       ar(sz, tw))
                    zm[(j, bi, "ve")] = inc("ve", ins)

                emit("tensor", pe)
                emit("scalar", ac)
                emit("vector", ve)

    def emit_l0_outproj():
        opm = {}
        cve = cnt["ve"]
        for bi, (t0, tw) in enumerate(TBS):
            for j in range(NCI):
                def po(g, bi=bi, j=j):
                    prev = (bi, j - 2) if j >= 2 else ((bi - 1, NCI - 2 + j)
                                                       if bi > 0 else None)
                    if prev and (prev[0], prev[1], "pe") in opm:
                        g.wait_ge(sem["pe"], opm[(prev[0], prev[1], "pe")])
                    elif prev is None:
                        wait(g, "pe")
                    ins = g.dma_start(
                        out=ar(RING_OFF + (j % 2) * 1024, 1024),
                        in_=inp["l0_op_w"].ap()[j * 128:(j + 1) * 128, :])
                    opm[(bi, j, "wl")] = inc("wla" if j % 2 == 0 else "wlb", ins, 16)

                def pe(t, bi=bi, j=j, t0=t0, tw=tw):
                    t.wait_ge(sem["wla" if j % 2 == 0 else "wlb"],
                              opm[(bi, j, "wl")])
                    if bi == 0 and j == 0:
                        wait(t, "ve", cve)
                    if j == 0 and bi > 0:
                        for dc in range(NCH):
                            t.wait_ge(sem["ve"], opm[(bi - 1, dc, "ev")])
                    for dc in range(NCH):
                        ins = t.matmul(bank(dc, 0, 128, tw),
                                       ar(RING_OFF + (j % 2) * 1024, 1024)
                                       [:, dc * 128:(dc + 1) * 128],
                                       Uc(j)[:, t0:t0 + tw],
                                       start=(j == 0), stop=(j == NCI - 1))
                    opm[(bi, j, "pe")] = inc("pe", ins)

                emit("gpsimd", po)
                emit("tensor", pe)

            def ve(v, bi=bi, t0=t0, tw=tw):
                v.wait_ge(sem["pe"], opm[(bi, NCI - 1, "pe")])
                for dc in range(NCH):
                    ins = v.tensor_add(xsm(dc, t0, tw), xsm(dc, t0, tw),
                                       bank(dc, 0, 128, tw))
                    opm[(bi, dc, "ev")] = inc("ve", ins)

            emit("vector", ve)

    def emit_l1_attn_pre():
        WD_OFF = 0
        DBAR_R = 8192
        A_R = 10240
        SZT = 12288
        A16_OFF = 12800
        B16_OFF = 14848
        PROW_OFF = 16896
        ZROW_OFF = 18944
        PC_OFF = 36864

        def Rhs_(o, n=TOK):
            return SCR[0:16, o:o + n]

        R_hs = SCR[0:16, 0:TOK]
        R_ct = SCR[0:16, TOK:2 * TOK]

        ve_at = cnt["ve"]

        def po(g):
            wait(g, "ve", ve_at)
            wait(g, "pe")
            inc("wl1", g.dma_start(
                out=ar(WD_OFF, NCH * 1024).rearrange("p (c o) -> p c o", c=NCH),
                in_=inp["l1_Wd"].ap().rearrange("(c p) o -> p c o", p=128)), 16)
            inc("wl", g.dma_start(
                out=smallw[:, 0:128].rearrange("p (c e) -> p c e", c=NCH),
                in_=inp["l1_Wb"].ap().rearrange("(c p) e -> p c e", p=128)), 16)
            inc("wl", g.dma_start(
                out=smallw[:, 128:256].rearrange("p (c e) -> p c e", c=NCH),
                in_=inp["l1_Wc"].ap().rearrange("(c p) e -> p c e", p=128)), 16)
            inc("wl", g.dma_start(
                out=smallw[:, 256:384].rearrange("p (c e) -> p c e", c=NCH),
                in_=inp["l1_A"].ap().rearrange("(c p) e -> p c e", p=128)), 16)

        emit("gpsimd", po)
        wl1_wd = cnt["wl1"]
        wl_small = cnt["wl"]

        def vz(v):
            inc("ve", v.memset(AR[0:1, ZROW_OFF:ZROW_OFF + TOK], 0.0))

        emit("vector", vz)

        sa_pe = {}

        def pesa(t):
            t.wait_ge(sem["wl"], wl_small)
            wait(t, "ve")
            wait(t, "ac")
            for c in range(NCH):
                ins = t.matmul(bank(3, 0, 16, 1), smallw[:, 256 + 16 * c:272 + 16 * c],
                               ones_col[:, :], start=(c == 0), stop=(c == NCH - 1))
            sa_pe[0] = inc("pe", ins)

        def acsa(s):
            s.wait_ge(sem["pe"], sa_pe[0])
            ins = s.activation(l1H[:, 1:2], bank(3, 0, 16, 1), AF.Sigmoid,
                               bias=0.0, scale=1.0 / DM)
            inc("ac", ins)

        emit("tensor", pesa)
        emit("scalar", acsa)
        sa_ac = cnt["ac"]

        dm = {}
        for bi, (t0, tw) in enumerate(TBS):
            for dc in range(NCH):
                bk = dc % 2

                def pe(t, bi=bi, dc=dc, t0=t0, tw=tw, bk=bk):
                    if bi == 0 and dc == 0:
                        t.wait_ge(sem["wl1"], wl1_wd)
                    prev = dm.get((bi, dc - 2, "ac")) or \
                        (dm.get((bi - 1, NCH - 2 + dc, "ac")) if bi > 0 else None)
                    if prev:
                        t.wait_ge(sem["ac"], prev)
                    for c in range(NCH):
                        ins = t.matmul(bank(bk, 0, 128, tw),
                                       AR[:, WD_OFF + c * 1024 + dc * 128:
                                          WD_OFF + c * 1024 + dc * 128 + 128],
                                       xnm(c, t0, tw),
                                       start=(c == 0), stop=(c == NCH - 1))
                    dm[(bi, dc, "pe")] = inc("pe", ins)

                def ac(s, bi=bi, dc=dc, tw=tw, bk=bk):
                    s.wait_ge(sem["pe"], dm[(bi, dc, "pe")])
                    bd = colpack[:, 128 + dc:129 + dc]
                    ins = s.activation(SCR[:, 2048:2048 + tw], bank(bk, 0, 128, tw),
                                       AF.Exp, bias=bd, scale=1.0)
                    s.wait_ge(sem["ac"], inc("ac", ins))
                    ins = s.activation(AR[:, SZT:SZT + tw], SCR[:, 2048:2048 + tw],
                                       AF.Ln, bias=1.0, scale=1.0)
                    dm[(bi, dc, "ac")] = inc("ac", ins)

                def pe2(t, bi=bi, dc=dc, t0=t0, tw=tw):
                    t.wait_ge(sem["ac"], dm[(bi, dc, "ac")])
                    if dc == 0 and bi > 0:
                        t.wait_ge(sem["ac"], dm[(bi - 1, "dbar")])
                    ins = t.matmul(bank(2, 0, 1, tw), ones_col[:, :],
                                   AR[:, SZT:SZT + tw],
                                   start=(dc == 0), stop=(dc == NCH - 1))
                    dm[(bi, dc, "pe2")] = inc("pe", ins)

                emit("tensor", pe)
                emit("scalar", ac)
                emit("tensor", pe2)

            def acd(s, bi=bi, t0=t0, tw=tw):
                s.wait_ge(sem["pe"], dm[(bi, NCH - 1, "pe2")])
                ins = s.activation(AR[0:1, DBAR_R + t0:DBAR_R + t0 + tw],
                                   bank(2, 0, 1, tw), AF.Copy, bias=0.0,
                                   scale=1.0 / DM)
                dm[(bi, "dbar")] = inc("ac", ins)

            emit("scalar", acd)

        def va(v):
            wait(v, "ac")
            ins = v.tensor_scalar(AR[0:1, A_R:A_R + TOK],
                                  AR[0:1, DBAR_R:DBAR_R + TOK], -1.0, 1.0,
                                  ALU.mult, ALU.add)
            inc("ve", ins)

        emit("vector", va)

        bc_m = {}
        for which, woff, bcol, dst in [("b", 0, 144, R_hs), ("c", 128, 145, R_ct)]:
            for bi, (t0, tw) in enumerate(TBS):
                bk = 4 + bi % 2

                def pe(t, which=which, woff=woff, bi=bi, t0=t0, tw=tw, bk=bk):
                    prev = bc_m.get((which, bi - 2, "ac")) or \
                        (bc_m.get(("b", bi + 2, "ac")) if which == "c" else None)
                    if prev:
                        t.wait_ge(sem["ve"], prev)
                    for c in range(NCH):
                        ins = t.matmul(bank(bk, 0, 16, tw),
                                       smallw[:, woff + 16 * c:woff + 16 * c + 16],
                                       xnm(c, t0, tw),
                                       start=(c == 0), stop=(c == NCH - 1))
                    bc_m[(which, bi, "pe")] = inc("pe", ins)

                def vv(v, which=which, bi=bi, t0=t0, tw=tw, bk=bk, bcol=bcol,
                       dst=dst):
                    v.wait_ge(sem["pe"], bc_m[(which, bi, "pe")])
                    ins = v.tensor_scalar(dst[:, t0:t0 + tw], bank(bk, 0, 16, tw),
                                          colpack[0:16, bcol:bcol + 1], None,
                                          ALU.add)
                    bc_m[(which, bi, "ac")] = inc("ve", ins)

                emit("tensor", pe)
                emit("vector", vv)

        d16_m = {}
        for bi, (t0, tw) in enumerate(TBS):
            bk = 6 + bi % 2

            def pe(t, bi=bi, t0=t0, tw=tw, bk=bk):
                wait(t, "ac") if bi == 0 else None
                prev = d16_m.get((bi - 2, "ve"))
                if prev:
                    t.wait_ge(sem["ve"], prev)
                ins = t.matmul(bank(bk, 0, 16, tw), ones_row[0:1, 0:16],
                               AR[0:1, DBAR_R + t0:DBAR_R + t0 + tw],
                               start=True, stop=True)
                d16_m[(bi, "pe")] = inc("pe", ins)

            def ve(v, bi=bi, t0=t0, tw=tw, bk=bk):
                v.wait_ge(sem["pe"], d16_m[(bi, "pe")])
                if bi == 0:
                    wait(v, "ac")
                vsafe(v, v.tensor_scalar(
                    AR[0:16, A16_OFF + t0:A16_OFF + t0 + tw],
                    bank(bk, 0, 16, tw), -1.0, 1.0, ALU.mult, ALU.add))
                b16 = AR[0:16, B16_OFF + t0:B16_OFF + t0 + tw]
                vsafe(v, v.scalar_tensor_tensor(
                    b16, R_hs[:, t0:t0 + tw], l1H[:, 1:2], bank(bk, 0, 16, tw),
                    ALU.mult, ALU.mult))
                ins = v.engine_nop()
                d16_m[(bi, "ve")] = cnt["ve"]

            emit("tensor", pe)
            emit("vector", ve)

        def vsc(v):
            wait(v, "ac")
            wait(v, "ve")
            vsafe(v, v.tensor_tensor_scan(
                R_hs[:, :], AR[0:16, A16_OFF:A16_OFF + TOK],
                AR[0:16, B16_OFF:B16_OFF + TOK], zeros_col[0:16, 0:1],
                ALU.mult, ALU.add))
            vsafe(v, v.tensor_tensor_scan(
                AR[0:1, PROW_OFF:PROW_OFF + TOK], AR[0:1, A_R:A_R + TOK],
                AR[0:1, ZROW_OFF:ZROW_OFF + TOK], one_col[0:1, 0:1],
                ALU.mult, ALU.add))
            ins = v.tensor_copy(cc1pack[:, 0:1], R_hs[:, TOK - 1:TOK])
            inc("ve", ins)

        emit("vector", vsc)
        sc_ve = cnt["ve"]

        pc_m = {}
        for bi, (t0, tw) in enumerate(TBS):
            bk = 6 + bi % 2

            def pe(t, bi=bi, t0=t0, tw=tw, bk=bk):
                if bi == 0:
                    t.wait_ge(sem["ve"], sc_ve)
                prev = pc_m.get((bi - 2, "ve"))
                if prev:
                    t.wait_ge(sem["ve"], prev)
                ins = t.matmul(bank(bk, 0, 16, tw), ones_row[0:1, 0:16],
                               AR[0:1, PROW_OFF + t0:PROW_OFF + t0 + tw],
                               start=True, stop=True)
                pc_m[(bi, "pe")] = inc("pe", ins)

            def ve(v, bi=bi, t0=t0, tw=tw, bk=bk):
                v.wait_ge(sem["pe"], pc_m[(bi, "pe")])
                ins = v.tensor_copy(AR[0:16, PC_OFF + t0:PC_OFF + t0 + tw],
                                    bank(bk, 0, 16, tw))
                pc_m[(bi, "ve")] = inc("ve", ins)

            emit("tensor", pe)
            emit("vector", ve)

        def pct(t):
            wait(t, "ve")
            ins = t.matmul(bank(3, 0, 16, 1), ones_row[0:1, 0:16],
                           AR[0:1, PROW_OFF + TOK - 1:PROW_OFF + TOK],
                           start=True, stop=True)
            pc_m["ct"] = inc("pe", ins)

        def vct(v):
            v.wait_ge(sem["pe"], pc_m["ct"])
            ins = v.tensor_copy(cc1pack[:, 1:2], bank(3, 0, 16, 1))
            inc("ve", ins)

        emit("tensor", pct)
        emit("vector", vct)
        pack_ve = cnt["ve"]

        def po2(g):
            g.wait_ge(sem["ve"], pack_ve)
            inc("po", g.dma_start(out=cc1_in.ap(), in_=cc1pack[:, :]), 16)
            wait(g, "po")
            inc("cc", g.collective_compute(
                "AllGather", ALU.bypass, replica_groups=[list(range(NCORE))],
                ins=[cc1_in.ap().opt()], outs=[cc1_out.ap().opt()]), 1)

        emit("gpsimd", po2)

        def spagg(sync):
            wait(sync, "cc")
            ins = sync.dma_start(
                out=aggL1[:, :].rearrange("p (r j) -> p r j", r=NCORE),
                in_=cc1_out.ap().rearrange("(r p) j -> p r j", p=16))
            inc("ld", ins, 16)

        emit("sync", spagg)

    def emit_l1_attn_fin():
        PC_OFF = 36864
        YSCBF_OFF = 36864
        PRD_OFF = 38912
        R_hs = SCR[0:16, 0:TOK]
        R_ct = SCR[0:16, TOK:2 * TOK]

        def vcomb(v):
            wait(v, "ld")
            vsafe(v, v.memset(l1H[:, 0:1], 0.0))
            vsafe(v, v.memset(cc1pack[:, 0:1], 0.0))
            for r in range(NCORE):
                vsafe(v, v.scalar_tensor_tensor(l1H[:, 0:1], cc1pack[:, 0:1],
                                                onehot8[0:16, r:r + 1], l1H[:, 0:1],
                                                ALU.mult, ALU.add))
                if r < NCORE - 1:
                    vsafe(v, v.tensor_mul(cc1pack[:, 1:2],
                                          aggL1[:, 2 * r + 1:2 * r + 2],
                                          cc1pack[:, 0:1]))
                    vsafe(v, v.tensor_add(cc1pack[:, 0:1], cc1pack[:, 1:2],
                                          aggL1[:, 2 * r:2 * r + 1]))
            for bi, (t0, tw) in enumerate(TBS):
                vsafe(v, v.scalar_tensor_tensor(
                    R_hs[:, t0:t0 + tw], AR[0:16, PC_OFF + t0:PC_OFF + t0 + tw],
                    l1H[:, 0:1], R_hs[:, t0:t0 + tw], ALU.mult, ALU.add))
            ins = v.tensor_mul(AR[0:16, PRD_OFF:PRD_OFF + TOK], R_hs[:, :],
                               R_ct[:, :])
            inc("ve", ins)

        emit("vector", vcomb)
        prod_ve = cnt["ve"]

        ys_m = {}
        for bi, (t0, tw) in enumerate(TBS):
            def pe(t, bi=bi, t0=t0, tw=tw):
                if bi == 0:
                    t.wait_ge(sem["ve"], prod_ve)
                ins = t.matmul(bank(bi, 0, 1, tw), ones_col[0:16, 0:1],
                               AR[0:16, PRD_OFF + t0:PRD_OFF + t0 + tw],
                               start=True, stop=True)
                ys_m[(bi, "pe")] = inc("pe", ins)

            def ve(v, bi=bi, t0=t0, tw=tw):
                v.wait_ge(sem["pe"], ys_m[(bi, "pe")])
                ins = v.tensor_copy(AR[0:1, YSCBF_OFF + t0:YSCBF_OFF + t0 + tw],
                                    bank(bi, 0, 1, tw))
                ys_m[(bi, "ve")] = inc("ve", ins)

            emit("tensor", pe)
            emit("vector", ve)
        ysc_ve = cnt["ve"]

        yb_m = {}
        for bi, (t0, tw) in enumerate(TBS):
            bk = 4 + bi % 2

            def pe(t, bi=bi, t0=t0, tw=tw, bk=bk):
                if bi == 0:
                    t.wait_ge(sem["ve"], ysc_ve)
                prev = yb_m.get((bi - 2, "ve"))
                if prev:
                    t.wait_ge(sem["ve"], prev)
                ins = t.matmul(bank(bk, 0, 128, tw), ones_row[:, :],
                               AR[0:1, YSCBF_OFF + t0:YSCBF_OFF + t0 + tw],
                               start=True, stop=True)
                yb_m[(bi, "pe")] = inc("pe", ins)

            def ve(v, bi=bi, t0=t0, tw=tw, bk=bk):
                v.wait_ge(sem["pe"], yb_m[(bi, "pe")])
                ins = v.tensor_copy(SCR[:, t0:t0 + tw], bank(bk, 0, 128, tw))
                yb_m[(bi, "ve")] = inc("ve", ins)

            emit("tensor", pe)
            emit("vector", ve)

        def po(g):
            wait(g, "pe")
            wait(g, "ve")
            inc("wl1", g.dma_start(
                out=ar(0, NCH * 1024).rearrange("p (c o) -> p c o", c=NCH),
                in_=inp["l1_Wo"].ap().rearrange("(c p) o -> p c o", p=128)), 16)

        emit("gpsimd", po)
        wo_wl = cnt["wl1"]

        wo_m = {}
        for bi, (t0, tw) in enumerate(TBS):
            def pe(t, bi=bi, t0=t0, tw=tw):
                if bi == 0:
                    t.wait_ge(sem["wl1"], wo_wl)
                    wait(t, "ve")
                if bi > 0:
                    for dc in range(NCH):
                        t.wait_ge(sem["ve"], wo_m[(bi - 1, "ev")])
                        break
                for dj in range(NCH):
                    for dc in range(NCH):
                        ins = t.matmul(bank(dc, 0, 128, tw),
                                       AR[:, dj * 1024 + dc * 128:
                                          dj * 1024 + dc * 128 + 128],
                                       xnm(dj, t0, tw),
                                       start=(dj == 0), stop=(dj == NCH - 1))
                wo_m[(bi, "pe")] = inc("pe", ins)

            def ve(v, bi=bi, t0=t0, tw=tw):
                v.wait_ge(sem["pe"], wo_m[(bi, "pe")])
                for dc in range(NCH):
                    vsafe(v, v.tensor_mul(SCR[:, 2048:2048 + tw],
                                          bank(dc, 0, 128, tw), SCR[:, t0:t0 + tw]))
                    bo = colpack[:, 136 + dc:137 + dc]
                    ins = v.scalar_tensor_tensor(xsm(dc, t0, tw),
                                                 SCR[:, 2048:2048 + tw], bo,
                                                 xsm(dc, t0, tw), ALU.add, ALU.add)
                    if dc < NCH - 1:
                        vsafe(v, ins)
                wo_m[(bi, "ev")] = inc("ve", ins)

            emit("tensor", pe)
            emit("vector", ve)

    def emit_store():
        def sp(sync):
            wait(sync, "ve")
            wait(sync, "ac")
            wait(sync, "pe")
            for c in range(NCH):
                inc("st", sync.dma_start(out=out_x.ap()[c * 128:(c + 1) * 128, :],
                                         in_=xsm(c)), 16)
            for name, (ap_, od) in dbg_outs.items():
                inc("st", sync.dma_start(out=od.ap(), in_=ap_), 16)
            sync.wait_ge(sem["st"], cnt["st"])

        emit("sync", sp)

    with nc.Block() as block:
        bh["block"] = block
        emit("sync", ph_loads)
        emit("gpsimd", ph_wloads_early)
        emit("scalar", ph_colprep_ac)
        emit("vector", ph_colprep_ve)

        if do_l0:
            with nc.named_scope("ln0"):
                emit_layernorm(full=True)
            dbg_add("xn0", xn[:, :], (128, NCH * TOKH), BF16)
            if globals()["DBG_STOP"] != "ln":
                with nc.named_scope("rt0"):
                    emit_routing(0)
                dbg_add("comb0", comb[:, :], (128, 256))
            if globals()["DBG_STOP"] not in ("ln", "rt"):
                with nc.named_scope("moe0"):
                    emit_moe(0)
                dbg_snap("x_moe0", x_res[:, :], (128, NCH * TOKH))
        if do_l0 and globals()["DBG_STOP"] is None:
            with nc.named_scope("passA"):
                emit_l0_passA()
            dbg_add("dtlow", AR[0:64, DTL_OFF:DTL_OFF + TOK], (64, TOK), BF16)
            dbg_snap("xi", AR[:, U_OFF:U_OFF + NCI * TOK], (128, NCI * TOK), BF16)
            with nc.named_scope("prepass"):
                emit_l0_prepass()
            dbg_add("packF", packF[:, :], (128, 16))
            dbg_add("packP", packP[:, :], (128, 16))
            dbg_add("S0", S0[:, :], (128, 16))
            with nc.named_scope("passC1"):
                emit_l0_passC1()
            dbg_add("y0", AR[:, U_OFF:U_OFF + NCI * TOK], (128, NCI * TOK), BF16)
            with nc.named_scope("passC2"):
                emit_l0_passC2()
            with nc.named_scope("outproj"):
                emit_l0_outproj()
            dbg_snap("x_l0s", x_res[:, :], (128, NCH * TOKH))

        if do_l1 and globals()["DBG_STOP"] is None:
            with nc.named_scope("ln1"):
                emit_layernorm(full=False)
            dbg_snap("xn1", xn[:, :], (128, NCH * TOKH), BF16)
            with nc.named_scope("rt1"):
                emit_routing(1)
            dbg_add("comb1", comb[:, :], (128, 256))
            with nc.named_scope("attn1pre"):
                emit_l1_attn_pre()
            with nc.named_scope("moe1"):
                emit_moe(1)
            dbg_snap("x_moe1", x_res[:, :], (128, NCH * TOKH))
            with nc.named_scope("attn1fin"):
                emit_l1_attn_fin()

        emit_store()

    es.close()
    return nc, dbg_outs


_CACHE = {}


def _prep_const():
    c = {}
    c["identity"] = np.eye(128, dtype=np.float32)
    c["ones_col_bf"] = np.ones((128, 1), dtype=ml_dtypes.bfloat16)
    c["ones_row_bf"] = np.ones((1, 128), dtype=ml_dtypes.bfloat16)
    c["zeros_col"] = np.zeros((128, 1), dtype=np.float32)
    c["one_col"] = np.ones((128, 1), dtype=np.float32)
    c["iota8eps"] = np.broadcast_to((np.arange(8) * 1e-6).astype(np.float32),
                                    (128, 8)).copy()
    c["iota16eps"] = np.broadcast_to((np.arange(16) * 1e-6).astype(np.float32),
                                     (128, 16)).copy()
    c["iota4eps"] = np.broadcast_to((np.arange(4) * 1e-6).astype(np.float32),
                                    (128, 4)).copy()
    c["neg1p"] = np.broadcast_to((-1.0 - np.arange(16) * 1e-6).astype(np.float32),
                                 (128, 16)).copy()
    return c


def _colpack(inputs, do_l0, do_l1):
    cp = np.zeros((128, 160), dtype=np.float32)
    if do_l0:
        cp[:, 0:16] = inputs["L0_attn_A_log"].reshape(16, 128).T
        cp[:, 16:32] = inputs["L0_attn_D"].reshape(16, 128).T
        cp[:, 32:48] = inputs["L0_attn_conv_b"].reshape(16, 128).T
        cp[:, 48:64] = inputs["L0_attn_dt_proj_b"].reshape(16, 128).T
        cw = inputs["L0_attn_conv_w"].reshape(2048, 4)
        for k in range(4):
            cp[:, 64 + 16 * k:64 + 16 * (k + 1)] = cw[:, k].reshape(16, 128).T
    if do_l1:
        cp[:, 128:136] = inputs["L1_attn_bd"].reshape(8, 128).T
        cp[:, 136:144] = inputs["L1_attn_bo"].reshape(8, 128).T
        cp[0:16, 144] = inputs["L1_attn_bb"]
        cp[0:16, 145] = inputs["L1_attn_bc"]
    return cp


def make_in_maps(inputs, do_l0=True, do_l1=True, x_mid=None):
    const = _prep_const()
    shared = dict(const)
    shared["colpack_in"] = _colpack(inputs, do_l0, do_l1)
    if do_l0:
        shared["l0_ip_w"] = np.ascontiguousarray(inputs["L0_attn_in_proj_w"].T)
        shared["l0_xp_w"] = np.ascontiguousarray(inputs["L0_attn_x_proj_w"].T)
        shared["l0_dt_w"] = np.ascontiguousarray(inputs["L0_attn_dt_proj_w"].T)
        shared["l0_op_w"] = np.ascontiguousarray(inputs["L0_attn_out_proj_w"].T)
        shared["l0_gate_w"] = np.ascontiguousarray(inputs["L0_ffn_gate_w"].T)
        shared["l0_bbias_bc"] = np.broadcast_to(
            inputs["L0_ffn_balance_bias"], (128, 8)).copy()
        shared["l0_w1"] = np.ascontiguousarray(inputs["L0_ffn_w1"].transpose(0, 2, 1))
        shared["l0_w2"] = np.ascontiguousarray(inputs["L0_ffn_w2"].transpose(0, 2, 1))
    if do_l1:
        shared["l1_Wd"] = np.ascontiguousarray(inputs["L1_attn_Wd"])
        shared["l1_Wb"] = np.ascontiguousarray(inputs["L1_attn_Wb"])
        shared["l1_Wc"] = np.ascontiguousarray(inputs["L1_attn_Wc"])
        shared["l1_Wo"] = np.ascontiguousarray(inputs["L1_attn_Wo"])
        shared["l1_A"] = np.ascontiguousarray(inputs["L1_attn_A"])
        shared["l1_router_w"] = np.ascontiguousarray(inputs["L1_ffn_router_w"].T)
        shared["l1_rbias_bc"] = np.broadcast_to(inputs["L1_ffn_bias"],
                                                (128, 16)).copy()
        shared["l1_gate_w"] = np.ascontiguousarray(inputs["L1_ffn_gate_w"])
        shared["l1_up_w"] = np.ascontiguousarray(inputs["L1_ffn_up_w"])
        shared["l1_down_w"] = np.ascontiguousarray(inputs["L1_ffn_down_w"])
        shared["l1_shg_w"] = np.ascontiguousarray(inputs["L1_ffn_sh_gate_w"])
        shared["l1_shu_w"] = np.ascontiguousarray(inputs["L1_ffn_sh_up_w"])
        shared["l1_shd_w"] = np.ascontiguousarray(inputs["L1_ffn_sh_down_w"])

    if do_l0:
        x = np.ascontiguousarray(inputs["L0_attn_x"][0].T)
        xpad = np.concatenate([np.zeros((DM, HALO), np.float32), x], axis=1)

    in_maps = []
    for c in range(NCORE):
        m = dict(shared)
        oh = np.zeros((128, 8), np.float32)
        oh[:, c] = 1.0
        m["onehot8"] = oh
        if do_l0:
            m["x_in"] = np.ascontiguousarray(xpad[:, c * TOK:c * TOK + TOKH])
        else:
            m["x_mid_in"] = np.ascontiguousarray(x_mid[c])
        in_maps.append(m)
    return in_maps


def assemble_out(res):
    full = np.concatenate([res.results[c]["out_x"] for c in range(NCORE)], axis=1)
    return np.ascontiguousarray(full.T)[None]


def kernel(**inputs):
    inputs = {k: np.asarray(v) for k, v in inputs.items()}
    do_l1 = DO_L1
    key = (True, do_l1)
    if key not in _CACHE:
        _CACHE[key] = build_program(True, do_l1)
    nc, _ = _CACHE[key]
    in_maps = make_in_maps(inputs, True, do_l1)
    res = run_bass_kernel_spmd(nc, in_maps, core_ids=list(range(NCORE)))
    return assemble_out(res)
